# version 06
# baseline (speedup 1.0000x reference)
import math
from contextlib import ExitStack

import numpy as np
import ml_dtypes
import concourse.bass as bass
import concourse.mybir as mybir
import bass_rust
from concourse.bass_utils import run_bass_kernel_spmd

F32 = mybir.dt.float32
BF16 = mybir.dt.bfloat16
U32 = mybir.dt.uint32
AF = mybir.ActivationFunctionType
ALU = mybir.AluOpType
AX = mybir.AxisListType
IOffs = bass_rust.IndirectOffsetOnAxis

BATCH, S, D, HID, NH = 4, 4096, 2048, 8192, 16
DN, DR, DV, DQK = 128, 64, 128, 192
QL, KVL = 768, 512
TOWN = 2048
LNEPS = 1e-5
RMSEPS = 1e-6
W2SCALE = 0.5 / math.sqrt(HID)
QSC_BIAS = math.log(1.0 / math.sqrt(DQK))
C_COEF = 8.0 / math.sqrt(DQK)
NG = 8
NTT_G = sum(4 * (g8 + 1) for g8 in range(8))


def _hadamard(n):
    h = np.array([[1.0]])
    while h.shape[0] < n:
        h = np.block([[h, h], [h, -h]])
    return h


class Sem:
    def __init__(self, ctx, nc, name):
        self.h = ctx.enter_context(nc.semaphore(name))
        self.n = 0

    def inc(self, instr, k=1):
        instr.then_inc(self.h, k)
        self.n += k
        return self.n


_WRAP_OPS = frozenset((
    "activation", "copy", "add", "mul", "tensor_tensor", "tensor_scalar",
    "tensor_copy", "tensor_reduce", "reduce_sum", "reduce_max", "reciprocal",
    "memset", "tensor_tensor_reduce", "select", "transpose", "iota",
))


class _Shim:
    def __init__(self, e):
        self._e = e

    def then_inc(self, h, k):
        self._e.sem_inc(h, k)
        return self


class SafeEng:

    def __init__(self, e, prog):
        self._e = e
        self._p = prog

    @property
    def raw(self):
        return self._e

    def __getattr__(self, nm):
        a = getattr(self._e, nm)
        if nm in _WRAP_OPS:
            def wrap(*args, **kw):
                ins = a(*args, **kw)
                self._p.n += 1
                ins.then_inc(self._p.h, 1)
                self._e.wait_ge(self._p.h, self._p.n)
                return _Shim(self._e)
            return wrap
        return a


def build_nc(stages=(0, 1, 2, 3), debug=False):
    nc = bass.Bass("TRN2", target_bir_lowering=False)

    x_in = nc.dram_tensor("x", [S, D], F32, kind="ExternalInput").ap()
    w1tg = nc.dram_tensor("w1tg", [64, 16, 128, 128], F32, kind="ExternalInput").ap()
    w3tg = nc.dram_tensor("w3tg", [64, 16, 128, 128], F32, kind="ExternalInput").ap()
    w2t = nc.dram_tensor("w2t", [HID, D], F32, kind="ExternalInput").ap()
    wqa = nc.dram_tensor("wqa", [D, QL], F32, kind="ExternalInput").ap()
    wqb = nc.dram_tensor("wqb", [QL, NH * DQK], F32, kind="ExternalInput").ap()
    wkva = nc.dram_tensor("wkva", [D, KVL + DR], F32, kind="ExternalInput").ap()
    wkvb = nc.dram_tensor("wkvb", [KVL, NH * 256], F32, kind="ExternalInput").ap()
    wo = nc.dram_tensor("wo", [NH * DV, D], F32, kind="ExternalInput").ap()
    qnw = nc.dram_tensor("qnw", [128, 6], F32, kind="ExternalInput").ap()
    kvnw = nc.dram_tensor("kvnw", [128, 4], F32, kind="ExternalInput").ap()
    cos_f = nc.dram_tensor("cos_f", [S, DR], F32, kind="ExternalInput").ap()
    sin_f = nc.dram_tensor("sin_f", [S, DR], F32, kind="ExternalInput").ap()
    cos_o = nc.dram_tensor("cos_o", [TOWN, DR], F32, kind="ExternalInput").ap()
    sin_o = nc.dram_tensor("sin_o", [TOWN, DR], F32, kind="ExternalInput").ap()
    hconst = nc.dram_tensor("hconst", [128, 256], BF16, kind="ExternalInput").ap()
    ones_f = nc.dram_tensor("ones_f", [128, 128], F32, kind="ExternalInput").ap()
    ones_b = nc.dram_tensor("ones_b", [128, 1], BF16, kind="ExternalInput").ap()
    maskc = nc.dram_tensor("maskc", [128, 4 * 256], BF16, kind="ExternalInput").ap()
    epsc = nc.dram_tensor("epsc", [128, 4], F32, kind="ExternalInput").ap()
    ridx = nc.dram_tensor("ridx", [128, 16], U32, kind="ExternalInput").ap()

    out_t = nc.dram_tensor("out", [TOWN, D], F32, kind="ExternalOutput").ap()

    dbg = "ExternalOutput" if debug else "Internal"
    zsc = nc.dram_tensor("zsc", [HID, D], BF16, kind="Internal").ap()
    wf = nc.dram_tensor("wfs", [64, 128, D], BF16, kind=dbg).ap()
    x1s = nc.dram_tensor("x1s", [S, D], F32, kind=dbg).ap()
    x2s = nc.dram_tensor("x2s", [TOWN, D], F32, kind=dbg).ap()
    lnt_b = nc.dram_tensor("lnt_b", [1024, D], BF16, kind="Internal").ap()
    ln2_b = nc.dram_tensor("ln2_b", [S, D], BF16, kind="Internal").ap()
    lno_b = nc.dram_tensor("lno_b", [TOWN, D], BF16, kind="Internal").ap()
    cn_b = nc.dram_tensor("cn_b", [S, KVL], BF16, kind="Internal").ap()
    kr_b = nc.dram_tensor("kr_b", [S, 128], BF16, kind="Internal").ap()
    qn_b = nc.dram_tensor("qn_b", [2, TOWN, 128], BF16, kind="Internal").ap()
    qr_b = nc.dram_tensor("qr_b", [2, TOWN, 128], BF16, kind="Internal").ap()
    o_b = nc.dram_tensor("o_b", [NH, TOWN, DV], BF16, kind="Internal").ap()

    octx = ExitStack()
    with octx:
        osb = lambda *a, **kw: octx.enter_context(nc.sbuf_tensor(*a, **kw))
        s_hc = osb("s_hc", [128, 256], BF16)
        s_1f = osb("s_1f", [128, 128], F32)
        s_1b = osb("s_1b", [128, 1], BF16)
        s_mask = osb("s_mask", [128, 4 * 256], BF16)
        s_ridx = osb("s_ridx", [128, 16], U32)
        s_eps = osb("s_eps", [128, 4], F32)

        ph = Sem(octx, nc, "ph")
        block = octx.enter_context(nc.Block())
        prog_v = Sem(octx, nc, "prog_v")
        prog_s = Sem(octx, nc, "prog_s")
        def bvector(fn):
            block.vector(lambda e: fn(SafeEng(e, prog_v)))

        def bscalar(fn):
            block.scalar(lambda e: fn(SafeEng(e, prog_s)))

        sem_id = [0]

        def push_scope(ctx_):
            pass

        def mksem(_ctx, label=None):
            sem_id[0] += 1
            return Sem(octx, nc, label or ("s%d" % sem_id[0]))

        mkdsem = lambda label: mksem(None, label)

        def pop_scope():
            barrier()

        def barrier():
            tgt = ph.n + 5
            ph.n = tgt
            for name in ("sync", "gpsimd", "tensor", "vector", "scalar"):
                def _f(e, tgt=tgt):
                    e.sem_inc(ph.h, 1)
                    e.wait_ge(ph.h, tgt)
                getattr(block, name)(_f)

        push_scope(octx)
        c_ld = Sem(octx, nc, "c_ld")

        def _consts(e):
            for dst, src in ((s_hc, hconst), (s_1f, ones_f), (s_1b, ones_b),
                             (s_mask, maskc), (s_ridx, ridx), (s_eps, epsc)):
                c_ld.inc(e.dma_start(out=dst[:], in_=src[:, :]), 16)
        block.sync(_consts)

        if 0 in stages:
            with ExitStack() as c0:
                push_scope(c0)
                sb0 = lambda *a, **kw: c0.enter_context(nc.sbuf_tensor(*a, **kw))
                s_w2 = sb0("s_w2", [128, 2 * D], BF16)
                s_z = sb0("s_z", [128, 2 * D], BF16)
                s_zs = sb0("s_zs", [128, 2 * D], BF16)
                s_wfo = sb0("s_wfo", [128, 2 * D], BF16)
                pA = c0.enter_context(nc.psum_tensor("p0a", [128, D], F32))
                pB = c0.enter_context(nc.psum_tensor("p0b", [128, D], F32))
                w_ld = [mkdsem("w_ld0"), mkdsem("w_ld1")]
                w_mm = mksem(octx, "w_mm")
                w_ev = mksem(octx, "w_ev")
                w_st = [mksem(octx, "w_st0"), mksem(octx, "w_st1")]
                w_fst = [mksem(octx, "w_fst0"), mksem(octx, "w_fst1")]
                w_zl = [mksem(octx, "w_zl0"), mksem(octx, "w_zl1")]

                def _g(e):
                    for hc in range(64):
                        if hc >= 2:
                            e.wait_ge(w_mm.h, hc - 1)
                        w_ld[hc % 2].inc(e.dma_start(
                            out=s_w2[:, (hc % 2) * D:(hc % 2 + 1) * D],
                            in_=w2t[hc * 128:(hc + 1) * 128, :]), 16)
                block.gpsimd(_g)

                def _t(e):
                    e.wait_ge(c_ld.h, 16 * 6)
                    for hc in range(64):
                        e.wait_ge(w_ld[hc % 2].h, 16 * (hc // 2 + 1))
                        if hc >= 2:
                            e.wait_ge(w_ev.h, hc - 1)
                        pp = pA if hc % 2 == 0 else pB
                        for ncx in range(4):
                            ins = e.matmul(
                                pp[:, ncx * 512:(ncx + 1) * 512],
                                s_hc[:, 0:128],
                                s_w2[:, (hc % 2) * D + ncx * 512:(hc % 2) * D + (ncx + 1) * 512],
                                start=True, stop=True)
                        w_mm.inc(ins, 1)
                    for si in range(64):
                        e.wait_ge(w_zl[si % 2].h, 16 * (si // 2 + 1))
                        if si >= 2:
                            e.wait_ge(w_ev.h, 64 + si - 1)
                        pp = pA if si % 2 == 0 else pB
                        for ncx in range(4):
                            ins = e.matmul(
                                pp[:, ncx * 512:(ncx + 1) * 512],
                                s_hc[:, 128:256],
                                s_zs[:, (si % 2) * D + ncx * 512:(si % 2) * D + (ncx + 1) * 512],
                                start=True, stop=True)
                        w_mm.inc(ins, 1)
                block.tensor(_t)

                def _s(e):
                    for hc in range(64):
                        e.wait_ge(w_mm.h, hc + 1)
                        if hc >= 2:
                            e.wait_ge(w_st[hc % 2].h, 16 * (hc // 2))
                        pp = pA if hc % 2 == 0 else pB
                        w_ev.inc(e.activation(
                            out=s_z[:, (hc % 2) * D:(hc % 2 + 1) * D],
                            in_=pp[:, :], func=AF.Copy), 1)
                        w_st[hc % 2].inc(e.dma_start(
                            out=zsc[hc * 128:(hc + 1) * 128, :],
                            in_=s_z[:, (hc % 2) * D:(hc % 2 + 1) * D]), 16)
                    for si in range(64):
                        e.wait_ge(w_mm.h, 64 + si + 1)
                        if si >= 2:
                            e.wait_ge(w_fst[si % 2].h, 16 * (si // 2))
                        pp = pA if si % 2 == 0 else pB
                        w_ev.inc(e.activation(
                            out=s_wfo[:, (si % 2) * D:(si % 2 + 1) * D],
                            in_=pp[:, :], func=AF.Identity, scale=W2SCALE), 1)
                        w_fst[si % 2].inc(e.dma_start(
                            out=wf[si, :, :],
                            in_=s_wfo[:, (si % 2) * D:(si % 2 + 1) * D]), 16)
                bscalar(_s)

                def _sy(e):
                    zr = zsc.rearrange("(j s) d -> s j d", s=64)
                    for si in range(64):
                        if si == 0:
                            e.wait_ge(w_st[0].h, 16 * 32)
                            e.wait_ge(w_st[1].h, 16 * 32)
                        if si >= 2:
                            e.wait_ge(w_mm.h, 64 + si - 1)
                        w_zl[si % 2].inc(e.dma_start(
                            out=s_zs[:, (si % 2) * D:(si % 2 + 1) * D],
                            in_=zr[si, :, :]), 16)
                    e.wait_ge(w_fst[0].h, 16 * 32)
                    e.wait_ge(w_fst[1].h, 16 * 32)
                block.sync(_sy)
                pop_scope()

        FFN_ID = [0]
        FFN_SEMS = {}

        def emit_ffn(x_src, x_dst, T):
            npass = T // 1024
            FFN_ID[0] += 1
            tag = "f%d_" % FFN_ID[0]
            with ExitStack() as cf:
                push_scope(cf)
                sbf = lambda nm, *a, **kw: cf.enter_context(nc.sbuf_tensor(tag + nm, *a, **kw))
                acc = sbf("acc", [128, 8 * D], F32)
                l1t = sbf("l1t", [128, 16 * 1024], BF16, is_transpose_dest=True)
                hT = sbf("hT", [128, 8 * 1024], BF16)
                wfg = sbf("wfg", [128, 8 * D], BF16)
                sW1 = sbf("w1", [128, 2 * 2048], BF16)
                sW3 = sbf("w3", [128, 2 * 2048], BF16)
                sg = sbf("sg", [128, 2 * 1024], BF16)
                junk = sbf("junk", [128, D], BF16)
                lnsb = sbf("lnsb", [128, 2 * D], BF16)
                xl = sbf("xl", [128, 2 * D], F32)
                stat = sbf("stat", [128, 8 * 6], F32)
                psg = cf.enter_context(nc.psum_tensor(tag + "psg", [128, 1024], F32))
                psl = cf.enter_context(nc.psum_tensor(tag + "psl", [128, 1024], F32))
                psd = cf.enter_context(nc.psum_tensor(tag + "psd", [128, D], F32))
                if not FFN_SEMS:
                    FFN_SEMS["pb"] = 0
                    for nm_ in ("f_x", "f_tr", "f_wf", "f_mmu", "f_mmd", "f_se",
                                "f_vm", "f_da", "f_acw"):
                        FFN_SEMS[nm_] = mksem(octx, nm_)
                    FFN_SEMS["f_lnw"] = [mksem(octx, "f_lnw0"), mksem(octx, "f_lnw1")]
                    FFN_SEMS["f_xl"] = [mksem(octx, "f_xl0"), mksem(octx, "f_xl1")]
                    FFN_SEMS["f_wl"] = [mkdsem("f_wl0"), mkdsem("f_wl1")]
                pb = FFN_SEMS["pb"]
                FFN_SEMS["pb"] = pb + npass
                f_x = FFN_SEMS["f_x"]
                f_lnw = FFN_SEMS["f_lnw"]
                f_tr = FFN_SEMS["f_tr"]
                f_wf = FFN_SEMS["f_wf"]
                f_wl = FFN_SEMS["f_wl"]
                f_mmu = FFN_SEMS["f_mmu"]
                f_mmd = FFN_SEMS["f_mmd"]
                f_se = FFN_SEMS["f_se"]
                f_vm = FFN_SEMS["f_vm"]
                f_da = FFN_SEMS["f_da"]
                f_xl = FFN_SEMS["f_xl"]
                f_acw = FFN_SEMS["f_acw"]

                accv = acc[:, :].rearrange("p (t d) -> p t d", d=D)

                def stt(ti, j):
                    return stat[:, ti * 6 + j:ti * 6 + j + 1]

                def _sy(e):
                    for p0 in range(npass):
                        p = pb + p0
                        if p > 0:
                            e.wait_ge(f_da.h, 64 * p)
                            e.wait_ge(f_acw.h, 16 * p)
                        f_x.inc(e.dma_start(
                            out=accv[:, :, :],
                            in_=x_src[p0 * 1024:(p0 + 1) * 1024, :]
                                .rearrange("(t q) d -> q t d", q=128)), 16)
                        for u in range(64):
                            g = u // 8
                            e.wait_ge(f_mmd.h, 64 * p + 8 * g)
                            f_wf.inc(e.dma_start(
                                out=wfg[:, (u % 8) * D:(u % 8 + 1) * D],
                                in_=wf[u, :, :]), 16)
                        e.wait_ge(f_da.h, 64 * (p + 1))
                        f_acw.inc(e.dma_start(
                            out=x_dst[p0 * 1024:(p0 + 1) * 1024, :]
                                .rearrange("(t q) d -> q t d", q=128),
                            in_=accv[:, :, :]), 16)
                    e.wait_ge(f_acw.h, 16 * (pb + npass))
                block.sync(_sy)

                def _g(e):
                    for p0 in range(npass):
                        p = pb + p0
                        for u in range(64):
                            uu = 64 * p + u
                            if uu >= 2:
                                e.wait_ge(f_mmu.h, 2 * uu - 2)
                            sl = u % 2
                            f_wl[sl].inc(e.dma_start(
                                out=sW1[:, sl * 2048:(sl + 1) * 2048]
                                    .rearrange("p (dc j) -> p dc j", j=128),
                                in_=w1tg[u, :, :, :].rearrange("dc p j -> p dc j")), 16)
                            f_wl[sl].inc(e.dma_start(
                                out=sW3[:, sl * 2048:(sl + 1) * 2048]
                                    .rearrange("p (dc j) -> p dc j", j=128),
                                in_=w3tg[u, :, :, :].rearrange("dc p j -> p dc j")), 16)
                block.gpsimd(_g)

                def _t(e):
                    for p0 in range(npass):
                        p = pb + p0
                        e.wait_ge(f_tr.h, 16 * 16 * (p + 1))
                        for g in range(8):
                            for ul in range(8):
                                u = 8 * g + ul
                                uu = 64 * p + u
                                e.wait_ge(f_wl[uu % 2].h, 32 * (uu // 2 + 1))
                                e.wait_ge(f_se.h, 64 * p + u)
                                sl = u % 2
                                for dc in range(16):
                                    for ncx in range(2):
                                        ins = e.matmul(
                                            psg[:, ncx * 512:(ncx + 1) * 512],
                                            sW1[:, sl * 2048 + dc * 128:sl * 2048 + (dc + 1) * 128],
                                            l1t[:, dc * 1024 + ncx * 512:dc * 1024 + (ncx + 1) * 512],
                                            start=(dc == 0), stop=(dc == 15))
                                f_mmu.inc(ins, 1)
                                e.wait_ge(f_vm.h, 64 * p + u)
                                for dc in range(16):
                                    for ncx in range(2):
                                        ins = e.matmul(
                                            psl[:, ncx * 512:(ncx + 1) * 512],
                                            sW3[:, sl * 2048 + dc * 128:sl * 2048 + (dc + 1) * 128],
                                            l1t[:, dc * 1024 + ncx * 512:dc * 1024 + (ncx + 1) * 512],
                                            start=(dc == 0), stop=(dc == 15))
                                f_mmu.inc(ins, 1)
                            e.wait_ge(f_vm.h, 64 * p + 8 * (g + 1))
                            e.wait_ge(f_wf.h, 16 * (64 * p + 8 * (g + 1)))
                            for ti in range(8):
                                e.wait_ge(f_da.h, 64 * p + 8 * g + ti)
                                for ul2 in range(8):
                                    for ncx in range(4):
                                        ins = e.matmul(
                                            psd[:, ncx * 512:(ncx + 1) * 512],
                                            hT[:, ul2 * 1024 + ti * 128:ul2 * 1024 + (ti + 1) * 128],
                                            wfg[:, ul2 * D + ncx * 512:ul2 * D + (ncx + 1) * 512],
                                            start=(ul2 == 0), stop=(ul2 == 7))
                                f_mmd.inc(ins, 1)
                block.tensor(_t)

                def ln_tile(e, q, ti):
                    def xload(ti):
                        kk = 8 * q + ti
                        f_xl[kk % 2].inc(e.dma_start(
                            out=xl[:, (kk % 2) * D:(kk % 2 + 1) * D],
                            in_=x_src[(q - pb) * 1024 + ti * 128:
                                      (q - pb) * 1024 + (ti + 1) * 128, :]), 16)
                    if ti == 0:
                        xload(0)
                        xload(1)
                    if True:
                        kk = 8 * q + ti
                        sl = (kk % 2) * D
                        e.wait_ge(f_xl[kk % 2].h, 16 * (kk // 2 + 1))
                        e.activation(out=junk[:, :], in_=xl[:, sl:sl + D],
                                     func=AF.Identity, accum_out=stt(ti, 0))
                        e.activation(out=stt(ti, 1), in_=stt(ti, 0),
                                     func=AF.Identity, scale=-1.0 / D)
                        e.activation(out=junk[:, :], in_=xl[:, sl:sl + D],
                                     func=AF.Square, bias=stt(ti, 1),
                                     accum_out=stt(ti, 4))
                        e.activation(out=stt(ti, 5), in_=stt(ti, 4),
                                     func=AF.Ln, scale=1.0 / D, bias=s_eps[:, 0:1])
                        e.activation(out=stt(ti, 3), in_=stt(ti, 5),
                                     func=AF.Exp, scale=-0.5)
                        e.activation(out=stt(ti, 2), in_=stt(ti, 1),
                                     func=AF.Identity, scale=stt(ti, 3))
                        if kk >= 2:
                            e.wait_ge(f_lnw[kk % 2].h, 16 * (kk // 2))
                        e.activation(
                            out=lnsb[:, sl:sl + D],
                            in_=xl[:, sl:sl + D], func=AF.Identity,
                            scale=stt(ti, 3), bias=stt(ti, 2))
                        f_lnw[kk % 2].inc(e.dma_start(
                            out=lnt_b[ti * 128:(ti + 1) * 128, :],
                            in_=lnsb[:, sl:sl + D]), 16)
                        if ti + 2 < 8:
                            xload(ti + 2)

                def ln_trans(e, q):
                    e.wait_ge(f_lnw[0].h, 64 * (q + 1))
                    e.wait_ge(f_lnw[1].h, 64 * (q + 1))
                    if q > 0:
                        e.wait_ge(f_mmu.h, 128 * q)
                    for dc in range(16):
                        f_tr.inc(e.dma_start(
                            out=l1t[:, dc * 1024:(dc + 1) * 1024],
                            in_=lnt_b[:, dc * 128:(dc + 1) * 128],
                            transpose=True), 16)

                def _s(e):
                    for ti in range(8):
                        ln_tile(e, pb, ti)
                    ln_trans(e, pb)
                    for p0 in range(npass):
                        p = pb + p0
                        for u in range(64):
                            uu = 64 * p + u
                            e.wait_ge(f_mmu.h, 2 * uu + 1)
                            if uu >= 2:
                                e.wait_ge(f_vm.h, uu - 1)
                            f_se.inc(e.activation(
                                out=sg[:, (u % 2) * 1024:(u % 2 + 1) * 1024],
                                in_=psg[:, :], func=AF.Silu), 1)
                            if p0 + 1 < npass and u >= 24 and (u - 24) % 5 == 0 and (u - 24) // 5 < 8:
                                ln_tile(e, p + 1, (u - 24) // 5)
                        if p0 + 1 < npass:
                            ln_trans(e, p + 1)
                bscalar(_s)

                def _v(e):
                    for p0 in range(npass):
                        p = pb + p0
                        for u in range(64):
                            uu = 64 * p + u
                            e.wait_ge(f_se.h, uu + 1)
                            e.wait_ge(f_mmu.h, 2 * uu + 2)
                            f_vm.inc(e.tensor_tensor(
                                out=hT[:, (u % 8) * 1024:(u % 8 + 1) * 1024],
                                in0=sg[:, (u % 2) * 1024:(u % 2 + 1) * 1024],
                                in1=psl[:, :], op=ALU.mult), 1)
                            if u % 8 == 7:
                                g = u // 8
                                for ti in range(8):
                                    e.wait_ge(f_mmd.h, 64 * p + 8 * g + ti + 1)
                                    if g == 0 and ti == 0:
                                        e.wait_ge(f_x.h, 16 * (p + 1))
                                    f_da.inc(e.tensor_tensor(
                                        out=accv[:, ti, :], in0=accv[:, ti, :],
                                        in1=psd[:, :], op=ALU.add), 1)
                bvector(_v)
                pop_scope()

        if 1 in stages:
            emit_ffn(x_in, x1s, S)

        if 2 in stages:
            with ExitStack() as c2:
                push_scope(c2)
                sb2 = lambda *a, **kw: c2.enter_context(nc.sbuf_tensor(*a, **kw))
                s_cosf = sb2("s_cosf", [128, 32 * DR], F32)
                s_sinf = sb2("s_sinf", [128, 32 * DR], F32)
                s_coso = sb2("s_coso", [128, 16 * DR], F32)
                s_sino = sb2("s_sino", [128, 16 * DR], F32)
                cnT = sb2("cnT", [128, 4 * S], BF16, is_transpose_dest=True)
                krT = sb2("krT", [128, S], BF16, is_transpose_dest=True)
                scol = sb2("scol", [128, 16], F32)
                tmp1 = sb2("tmp1", [128, 512], F32)
                tmp2 = sb2("tmp2", [128, 512], F32)
                a_c = mksem(octx, "a_c")
                b_fold = mksem(octx, "b_fold")

                def _cl(e):
                    for dst, src in ((s_cosf, cos_f), (s_sinf, sin_f),
                                     (s_coso, cos_o), (s_sino, sin_o)):
                        a_c.inc(e.dma_start(
                            out=dst[:].rearrange("p (t f) -> p t f", f=DR),
                            in_=src.rearrange("(t q) f -> q t f", q=128)), 16)
                block.sync(_cl)

                def rope6(e, srcE, srcO, cE, cO, sE, sO, dE, dO, w):
                    e.tensor_tensor(out=tmp1[:, 0:w], in0=srcE, in1=cE, op=ALU.mult)
                    e.tensor_tensor(out=tmp2[:, 0:w], in0=srcO, in1=sE, op=ALU.mult)
                    e.tensor_tensor(out=dE, in0=tmp1[:, 0:w], in1=tmp2[:, 0:w], op=ALU.subtract)
                    e.tensor_tensor(out=tmp1[:, 0:w], in0=srcO, in1=cO, op=ALU.mult)
                    e.tensor_tensor(out=tmp2[:, 0:w], in0=srcE, in1=sO, op=ALU.mult)
                    return e.tensor_tensor(out=dO, in0=tmp1[:, 0:w], in1=tmp2[:, 0:w], op=ALU.add)

                with ExitStack() as ca:
                    push_scope(ca)
                    sba = lambda *a, **kw: ca.enter_context(nc.sbuf_tensor(*a, **kw))
                    xc = sba("a_xc", [128, 8 * D], F32)
                    l2t = sba("a_l2t", [128, 2 * 16 * 512], BF16, is_transpose_dest=True)
                    lnsb2 = sba("a_lnsb", [128, 2 * D], BF16)
                    junk2 = sba("a_junk", [128, D], BF16)
                    wkvab = sba("a_wkvab", [128, 16 * (KVL + DR)], BF16)
                    cnsb = sba("a_cnsb", [128, 2 * KVL], BF16)
                    krf = sba("a_krf", [128, 2 * DR], F32)
                    krsb = sba("a_krsb", [128, 2 * 128], BF16)
                    ast = sba("a_stat", [128, 32 * 6], F32)
                    psk = ca.enter_context(nc.psum_tensor("a_psk", [128, 2 * 1024], F32))
                    a_x = [mksem(octx, "a_x0"), mksem(octx, "a_x1")]
                    a_vst = mksem(octx, "a_vst")
                    a_sr = mksem(octx, "a_sr")
                    a_sev = mksem(octx, "a_sev")
                    a_lnw = [mksem(octx, "a_lnw0"), mksem(octx, "a_lnw1")]
                    a_tr = mksem(octx, "a_tr")
                    a_kmm = mksem(octx, "a_kmm")
                    a_kev = mksem(octx, "a_kev")
                    a_kro = mksem(octx, "a_kro")
                    a_cnw = [mkdsem("a_cnw0"), mkdsem("a_cnw1")]
                    a_wk = mkdsem("a_wk")

                    def ss(k, j):
                        return ast[:, k * 6 + j:k * 6 + j + 1]

                    def _g(e):
                        a_wk.inc(e.dma_start(
                            out=wkvab[:].rearrange("p (dc f) -> p dc f", f=KVL + DR),
                            in_=wkva.rearrange("(dc p) f -> p dc f", p=128)), 16)
                        for k in range(32):
                            e.wait_ge(a_kev.h, k + 1)
                            e.wait_ge(a_kro.h, k + 1)
                            a_cnw[k % 2].inc(e.dma_start(
                                out=cn_b[k * 128:(k + 1) * 128, :],
                                in_=cnsb[:, (k % 2) * KVL:(k % 2 + 1) * KVL]), 16)
                            a_cnw[k % 2].inc(e.dma_start(
                                out=kr_b[k * 128:(k + 1) * 128, :],
                                in_=krsb[:, (k % 2) * 128:(k % 2 + 1) * 128]), 16)
                    block.gpsimd(_g)

                    def _sy(e):
                        for sc in range(8):
                            if sc >= 2:
                                e.wait_ge(a_sev.h, 4 * (sc - 1))
                            hx = (sc % 2) * 4
                            a_x[sc % 2].inc(e.dma_start(
                                out=xc[:, hx * D:(hx + 4) * D].rearrange("p (t d) -> p t d", d=D),
                                in_=x1s[sc * 512:(sc + 1) * 512, :].rearrange("(t q) d -> q t d", q=128)), 16)
                            for ti in range(4):
                                k = 4 * sc + ti
                                e.wait_ge(a_sev.h, k + 1)
                                a_lnw[k % 2].inc(e.dma_start(
                                    out=ln2_b[k * 128:(k + 1) * 128, :],
                                    in_=lnsb2[:, (k % 2) * D:(k % 2 + 1) * D]), 16)
                        e.wait_ge(a_cnw[0].h, 32 * 16)
                        e.wait_ge(a_cnw[1].h, 32 * 16)
                    block.sync(_sy)

                    def _v(e):
                        e.wait_ge(a_c.h, 64)
                        e.memset(krsb[:, DR:128], 0.0)
                        e.memset(krsb[:, 128 + DR:256], 0.0)
                        for sc in range(8):
                            e.wait_ge(a_x[sc % 2].h, 16 * (sc // 2 + 1))
                            hx = (sc % 2) * 4
                            for ti in range(4):
                                k = 4 * sc + ti
                                e.tensor_reduce(ss(k, 0), xc[:, (hx + ti) * D:(hx + ti + 1) * D],
                                                axis=AX.X, op=ALU.add)
                                a_vst.inc(e.tensor_scalar(
                                    out=ss(k, 1), in0=ss(k, 0),
                                    scalar1=-1.0 / D, scalar2=None, op0=ALU.mult), 1)
                                e.wait_ge(a_sr.h, k + 1)
                                a_vst.inc(e.tensor_tensor(
                                    out=ss(k, 2), in0=ss(k, 1), in1=ss(k, 3), op=ALU.mult), 1)
                        for k in range(32):
                            e.wait_ge(a_kev.h, k + 1)
                            if k >= 2:
                                e.wait_ge(a_cnw[k % 2].h, 32 * (k // 2))
                            kbf = (k % 2) * DR
                            kb = (k % 2) * 128
                            cs_ = s_cosf[:, k * DR:(k + 1) * DR]
                            sn_ = s_sinf[:, k * DR:(k + 1) * DR]
                            a_kro.inc(rope6(
                                e,
                                krf[:, kbf + 0:kbf + DR:2], krf[:, kbf + 1:kbf + DR:2],
                                cs_[:, 0:DR:2], cs_[:, 1:DR:2],
                                sn_[:, 0:DR:2], sn_[:, 1:DR:2],
                                krsb[:, kb + 0:kb + DR:2], krsb[:, kb + 1:kb + DR:2],
                                DR // 2), 1)
                    bvector(_v)

                    def _s(e):
                        def atrans(e, sc):
                            e.wait_ge(a_lnw[0].h, 16 * 2 * (sc + 1))
                            e.wait_ge(a_lnw[1].h, 16 * 2 * (sc + 1))
                            if sc >= 1:
                                e.wait_ge(a_tr.h, 256 * sc)
                            if sc >= 2:
                                e.wait_ge(a_kmm.h, 4 * (sc - 1))
                            for dc in range(16):
                                a_tr.inc(e.dma_start(
                                    out=l2t[:, (sc % 2) * 8192 + dc * 512:(sc % 2) * 8192 + (dc + 1) * 512],
                                    in_=ln2_b[sc * 512:(sc + 1) * 512, dc * 128:(dc + 1) * 128],
                                    transpose=True), 16)
                        for sc in range(8):
                            hx = (sc % 2) * 4
                            for ti in range(4):
                                k = 4 * sc + ti
                                e.wait_ge(a_x[sc % 2].h, 16 * (sc // 2 + 1))
                                e.wait_ge(a_vst.h, 2 * k + 1)
                                e.activation(out=junk2[:, :], in_=xc[:, (hx + ti) * D:(hx + ti + 1) * D],
                                             func=AF.Square, bias=ss(k, 1), accum_out=ss(k, 4))
                                e.activation(out=ss(k, 5), in_=ss(k, 4), func=AF.Ln,
                                             scale=1.0 / D, bias=s_eps[:, 0:1])
                                a_sr.inc(e.activation(out=ss(k, 3), in_=ss(k, 5),
                                                      func=AF.Exp, scale=-0.5), 1)
                                e.wait_ge(a_vst.h, 2 * k + 2)
                                if k >= 2:
                                    e.wait_ge(a_lnw[k % 2].h, 16 * (k // 2))
                                a_sev.inc(e.activation(
                                    out=lnsb2[:, (k % 2) * D:(k % 2 + 1) * D],
                                    in_=xc[:, (hx + ti) * D:(hx + ti + 1) * D],
                                    func=AF.Identity, scale=ss(k, 3), bias=ss(k, 2)), 1)
                        atrans(e, 0)
                        atrans(e, 1)
                        for k in range(32):
                            e.wait_ge(a_kmm.h, k + 1)
                            if k >= 2:
                                e.wait_ge(a_cnw[k % 2].h, 32 * (k // 2))
                            kp = (k % 2) * 1024
                            e.activation(out=junk2[:, 0:KVL], in_=psk[:, kp:kp + KVL],
                                         func=AF.Square, accum_out=ss(k, 0))
                            e.activation(out=ss(k, 1), in_=ss(k, 0), func=AF.Ln,
                                         scale=1.0 / KVL, bias=s_eps[:, 1:2])
                            e.activation(out=ss(k, 2), in_=ss(k, 1), func=AF.Exp, scale=-0.5)
                            e.activation(out=cnsb[:, (k % 2) * KVL:(k % 2 + 1) * KVL],
                                         in_=psk[:, kp:kp + KVL],
                                         func=AF.Identity, scale=ss(k, 2))
                            if k >= 2:
                                e.wait_ge(a_kro.h, k - 1)
                            a_kev.inc(e.activation(
                                out=krf[:, (k % 2) * DR:(k % 2 + 1) * DR],
                                in_=psk[:, kp + KVL:kp + KVL + DR], func=AF.Copy), 1)
                            if k % 4 == 3 and k // 4 + 2 < 8:
                                atrans(e, k // 4 + 2)
                    bscalar(_s)

                    def _t(e):
                        e.wait_ge(a_wk.h, 16)
                        for sc in range(8):
                            e.wait_ge(a_tr.h, 16 * 16 * (sc + 1))
                            for ti in range(4):
                                k = 4 * sc + ti
                                if k >= 2:
                                    e.wait_ge(a_kev.h, k - 1)
                                kp = (k % 2) * 1024
                                base = (sc % 2) * 8192
                                for dc in range(16):
                                    lh = l2t[:, base + dc * 512 + ti * 128:base + dc * 512 + (ti + 1) * 128]
                                    e.matmul(psk[:, kp:kp + 512], lh,
                                             wkvab[:, dc * (KVL + DR):dc * (KVL + DR) + 512],
                                             start=(dc == 0), stop=(dc == 15))
                                    ins = e.matmul(psk[:, kp + 512:kp + 576], lh,
                                                   wkvab[:, dc * (KVL + DR) + 512:dc * (KVL + DR) + 576],
                                                   start=(dc == 0), stop=(dc == 15))
                                a_kmm.inc(ins, 1)
                    block.tensor(_t)
                    pop_scope()

                cbc = ExitStack()
                qaT = cbc.enter_context(nc.sbuf_tensor("qaT", [128, 6 * TOWN], BF16))

                with ExitStack() as cb:
                    push_scope(cb)
                    sbb = lambda *a, **kw: cb.enter_context(nc.sbuf_tensor(*a, **kw))
                    lgo = sbb("b_lgo", [128, 2 * D], BF16)
                    l2ot = sbb("b_l2ot", [128, 16 * 1024], BF16, is_transpose_dest=True)
                    wqab = sbb("b_wqab", [128, 16 * QL], BF16)
                    sqaT = sbb("b_sqaT", [128, 6 * TOWN], BF16)
                    qnwb = sbb("b_qnwb", [128, 6], F32)
                    sc1 = sbb("b_sc1", [128, 16], F32)
                    psq = cb.enter_context(nc.psum_tensor("b_psq", [128, 2 * 512], F32))
                    pcs = cb.enter_context(nc.psum_tensor("b_pcs", [128, 16], F32))
                    b_tr = mksem(octx, "b_tr")
                    b_gth = [mkdsem("b_gth0"), mkdsem("b_gth1")]
                    b_gw = [mksem(octx, "b_gw0"), mksem(octx, "b_gw1")]
                    b_wq = mkdsem("b_wq")
                    b_mm = mksem(octx, "b_mm")
                    b_ev = mksem(octx, "b_ev")
                    b_cs = mksem(octx, "b_cs")
                    b_sc = mksem(octx, "b_sc")

                    def _sy(e):
                        for j in range(4):
                            b_tr.inc(e.dma_start(out=cnT[:, j * S:(j + 1) * S],
                                                 in_=cn_b[:, j * 128:(j + 1) * 128],
                                                 transpose=True), 16)
                        b_tr.inc(e.dma_start(out=krT[:, :], in_=kr_b[:, :],
                                             transpose=True), 16)
                        for j in range(16):
                            e.wait_ge(b_gth[j % 2].h, 16 * (j // 2 + 1))
                            b_gw[j % 2].inc(e.dma_start(
                                out=lno_b[j * 128:(j + 1) * 128, :],
                                in_=lgo[:, (j % 2) * D:(j % 2 + 1) * D]), 16)
                        for half in range(2):
                            e.wait_ge(b_gw[0].h, 16 * 8)
                            e.wait_ge(b_gw[1].h, 16 * 8)
                            if half == 1:
                                e.wait_ge(b_mm.h, 12)
                            for dc in range(16):
                                b_tr.inc(e.dma_start(
                                    out=l2ot[:, dc * 1024:(dc + 1) * 1024],
                                    in_=lno_b[half * 1024:(half + 1) * 1024,
                                              dc * 128:(dc + 1) * 128],
                                    transpose=True), 16)
                        e.wait_ge(b_tr.h, 16 * 37)
                    block.sync(_sy)

                    def _g(e):
                        b_wq.inc(e.dma_start(
                            out=wqab[:].rearrange("p (dc f) -> p dc f", f=QL),
                            in_=wqa.rearrange("(dc p) f -> p dc f", p=128)), 16)
                        b_wq.inc(e.dma_start(out=qnwb[:, :], in_=qnw[:, :]), 16)
                        for j in range(16):
                            if j >= 2:
                                e.wait_ge(b_gw[j % 2].h, 16 * (j // 2))
                            b_gth[j % 2].inc(e.indirect_dma_start(
                                out=lgo[:, (j % 2) * D:(j % 2 + 1) * D], out_offset=None,
                                in_=ln2_b[:, :],
                                in_offset=IOffs(ap=s_ridx[:, j:j + 1], axis=0)), 16)
                    block.gpsimd(_g)

                    def _t(e):
                        e.wait_ge(b_wq.h, 32)
                        idx = 0
                        for half in range(2):
                            e.wait_ge(b_tr.h, 16 * (5 + 16 * (half + 1)))
                            for m6 in range(6):
                                for ncx in range(2):
                                    if idx >= 2:
                                        e.wait_ge(b_ev.h, 2 * (idx - 1))
                                    for dc in range(16):
                                        ins = e.matmul(
                                            psq[:, (idx % 2) * 512:(idx % 2 + 1) * 512],
                                            wqab[:, dc * QL + m6 * 128:dc * QL + (m6 + 1) * 128],
                                            l2ot[:, dc * 1024 + ncx * 512:dc * 1024 + (ncx + 1) * 512],
                                            start=(dc == 0), stop=(dc == 15))
                                    b_mm.inc(ins, 1)
                                    idx += 1
                        e.wait_ge(b_ev.h, 48)
                        for sub in range(16):
                            for k6 in range(6):
                                ins = e.matmul(pcs[:, sub:sub + 1],
                                               sqaT[:, k6 * TOWN + sub * 128:k6 * TOWN + (sub + 1) * 128],
                                               s_1b[:, 0:1],
                                               start=(k6 == 0), stop=(k6 == 5))
                        b_cs.inc(ins, 1)
                    block.tensor(_t)

                    def _s(e):
                        idx = 0
                        for half in range(2):
                            for m6 in range(6):
                                for ncx in range(2):
                                    cidx = 2 * half + ncx
                                    e.wait_ge(b_mm.h, idx + 1)
                                    e.activation(
                                        out=qaT[:, m6 * TOWN + cidx * 512:m6 * TOWN + (cidx + 1) * 512],
                                        in_=psq[:, (idx % 2) * 512:(idx % 2 + 1) * 512],
                                        func=AF.Copy)
                                    b_ev.inc(e.activation(
                                        out=sqaT[:, m6 * TOWN + cidx * 512:m6 * TOWN + (cidx + 1) * 512],
                                        in_=psq[:, (idx % 2) * 512:(idx % 2 + 1) * 512],
                                        func=AF.Square), 2)
                                    idx += 1
                        e.wait_ge(b_cs.h, 1)
                        e.activation(out=sc1[:, :], in_=pcs[:, :], func=AF.Ln,
                                     scale=1.0 / QL, bias=s_eps[:, 1:2])
                        b_sc.inc(e.activation(out=scol[:, :], in_=sc1[:, :],
                                              func=AF.Exp, scale=-0.5, bias=s_eps[:, 2:3]), 1)
                    bscalar(_s)

                    def _v(e):
                        e.wait_ge(b_cs.h, 1)
                        e.wait_ge(b_wq.h, 32)
                        for k6 in range(6):
                            ins = e.tensor_scalar(out=qaT[:, k6 * TOWN:(k6 + 1) * TOWN],
                                                  in0=qaT[:, k6 * TOWN:(k6 + 1) * TOWN],
                                                  scalar1=qnwb[:, k6:k6 + 1], scalar2=None,
                                                  op0=ALU.mult)
                        b_fold.inc(ins, 1)
                        e.memset(krT[64:65, :], 1.0)
                    bvector(_v)
                    pop_scope()

                with ExitStack() as cg:
                    push_scope(cg)
                    sbg = lambda *a, **kw: cg.enter_context(nc.sbuf_tensor(*a, **kw))
                    wkvbb = sbg("g_wkvbb", [128, 4 * NH * 256], BF16)
                    kvnwb = sbg("g_kvnwb", [128, 4], F32)
                    wqbg = sbg("g_wqbg", [128, 2 * 6 * 384], BF16)
                    qg = sbg("g_qg", [128, 16 * 384], BF16)
                    qrc = sbg("g_qrc", [128, 16 * 256], BF16)
                    qs2 = sbg("g_qs2", [128, 32], F32)
                    sqq = sbg("g_sqq", [128, 32], F32)
                    cn_ = sbg("g_cn", [128, 32], F32)
                    junk3 = sbg("g_junk", [128, S], BF16)
                    vjunk = sbg("g_vjunk", [128, 192], BF16)
                    qnT = sbg("g_qnT", [128, 2 * TOWN], BF16, is_transpose_dest=True)
                    qrT = sbg("g_qrT", [128, 2 * TOWN], BF16, is_transpose_dest=True)
                    knT = sbg("g_knT", [128, S], BF16)
                    Vh = sbg("g_Vh", [128, 32 * 132], BF16)
                    PT = sbg("g_PT", [128, 2 * 512], BF16)
                    og = sbg("g_og", [128, 2 * 16 * 128], BF16)
                    kq = sbg("g_kq", [128, 1], F32)
                    kr64 = sbg("g_kr64", [128, 1], F32)
                    krs = sbg("g_krs", [1, 1], F32)
                    tta = sbg("g_tta", [1, 1], F32)
                    rks = sbg("g_rks", [1, 8], F32)
                    rkc = sbg("g_rkc", [128, 1], F32)
                    rcp = sbg("g_rcp", [128, 2], F32)
                    PS = cg.enter_context(nc.psum_tensor("g_ps", [128, 4096], F32))
                    def vS(ii):
                        return PS[:, (ii % 2) * 512:(ii % 2) * 512 + 256]

                    def vA(s2):
                        return PS[:, 1024 + s2 * 512:1024 + s2 * 512 + 129]

                    def vQ(sub):
                        return PS[:, 2048 + (sub % 2) * 512:2048 + (sub % 2) * 512 + 384]

                    def vV(vs):
                        return PS[:, 2560 + (vs % 2) * 512:2560 + (vs % 2) * 512 + 128]

                    psn = PS[:, 2048:2560]
                    ptiny = PS[:, 3584:3600]
                    g_wk = mkdsem("g_wk")
                    g_wkf = mksem(octx, "g_wkf")
                    g_wq = [mkdsem("g_wq0"), mkdsem("g_wq1")]
                    g_qmm = mksem(octx, "g_qmm")
                    g_qev = mksem(octx, "g_qev")
                    g_qs = mksem(octx, "g_qs")
                    g_sq = mksem(octx, "g_sq")
                    g_rope = mksem(octx, "g_rope")
                    g_cw = mksem(octx, "g_cw")
                    g_qw = mksem(octx, "g_qw")
                    g_qtr = mksem(octx, "g_qtr")
                    g_kmm = mksem(octx, "g_kmm")
                    g_kev = mksem(octx, "g_kev")
                    g_vev = mksem(octx, "g_vev")
                    g_bc = mksem(octx, "g_bc")
                    g_rk = mksem(octx, "g_rk")
                    g_rkb = mksem(octx, "g_rkb")
                    g_smm = mksem(octx, "g_smm")
                    g_sex = mksem(octx, "g_sex")
                    g_msk = mksem(octx, "g_msk")
                    g_amm = mksem(octx, "g_amm")
                    g_rc = mksem(octx, "g_rc")
                    g_oev = mksem(octx, "g_oev")
                    g_ow = mksem(octx, "g_ow")
                    g_kr0 = mksem(octx, "g_kr0")
                    g_kr1 = mksem(octx, "g_kr1")

                    P_pair, P_amm, P_mskp, P_plast = {}, {}, {}, {}
                    P_kmmK, P_kmmV = {}, {}
                    n_pair = n_msk = n_kmm = n_amm = 0
                    for grp in range(NG):
                        for h2 in range(2):
                            for ncx in range(8):
                                n_kmm += 1
                                P_kmmK[(grp, h2, ncx)] = n_kmm
                            for vs in range(32):
                                n_kmm += 1
                                P_kmmV[(grp, h2, vs)] = n_kmm
                            for g8 in range(8):
                                for jj in range(2 * (g8 + 1)):
                                    n_pair += 1
                                    P_pair[(grp, h2, g8, jj)] = n_pair
                                    if 2 * jj >= 4 * g8:
                                        n_msk += 1
                                        P_mskp[(grp, h2, g8, jj)] = n_msk
                                    for k_ in range(2):
                                        n_amm += 1
                                        P_amm[(grp, h2, g8, 2 * jj + k_)] = n_amm
                                    P_plast[n_pair] = n_amm

                    def P_kev(grp, h2, j):
                        return 9 * (2 * grp + h2) + j + 1

                    def P_vev(grp, h2, vs):
                        return 32 * (2 * grp + h2) + vs + 1

                    def P_oev(grp, h2, g8, s2):
                        return 16 * (2 * grp + h2) + 2 * g8 + s2 + 1

                    def P_rc(grp, h2, g8):
                        return 8 * (2 * grp + h2) + g8 + 1

                    def _g(e):
                        g_wk.inc(e.dma_start(out=kvnwb[:, :], in_=kvnw[:, :]), 16)
                        g_wk.inc(e.dma_start(
                            out=wkvbb[:].rearrange("p (kc f) -> p kc f", f=NH * 256),
                            in_=wkvb.rearrange("(kc p) f -> p kc f", p=128)), 16)
                        wqbr = wqb.rearrange("(k p) f -> p k f", p=128)
                        for grp in range(NG):
                            if grp >= 2:
                                e.wait_ge(g_qmm.h, 16 * (grp - 1))
                            g_wq[grp % 2].inc(e.dma_start(
                                out=wqbg[:, (grp % 2) * 2304:(grp % 2 + 1) * 2304]
                                    .rearrange("p (k f) -> p k f", f=384),
                                in_=wqbr[:, :, grp * 384:(grp + 1) * 384]), 16)
                    block.gpsimd(_g)

                    def _vw(e):
                        e.wait_ge(g_wk.h, 32)
                        for kc in range(4):
                            ins = e.tensor_scalar(
                                out=wkvbb[:, kc * NH * 256:(kc + 1) * NH * 256],
                                in0=wkvbb[:, kc * NH * 256:(kc + 1) * NH * 256],
                                scalar1=kvnwb[:, kc:kc + 1], scalar2=None, op0=ALU.mult)
                        g_wkf.inc(ins, 1)
                    bvector(_vw)

                    def _t(e):
                        e.wait_ge(g_wkf.h, 1)
                        e.wait_ge(g_kr0.h, 1)
                        ins = e.matmul(ptiny[0:1, 3:4], kr64[0:64, 0:1], s_1f[0:64, 0:1],
                                       start=True, stop=True)
                        g_kr1.inc(ins, 1)
                        for grp in range(NG):
                            e.wait_ge(g_wq[grp % 2].h, 16 * (grp // 2 + 1))
                            e.wait_ge(b_fold.h, 1)
                            for sub in range(16):
                                gi = 16 * grp + sub
                                if gi >= 2:
                                    e.wait_ge(g_qev.h, gi - 1)
                                for k6 in range(6):
                                    ins = e.matmul(
                                        vQ(sub),
                                        qaT[:, k6 * TOWN + sub * 128:k6 * TOWN + (sub + 1) * 128],
                                        wqbg[:, (grp % 2) * 2304 + k6 * 384:(grp % 2) * 2304 + (k6 + 1) * 384],
                                        start=(k6 == 0), stop=(k6 == 5))
                                g_qmm.inc(ins, 1)
                            for h2 in range(2):
                                h = 2 * grp + h2
                                if h2 == 0:
                                    e.wait_ge(g_qev.h, 16 * (grp + 1))
                                for ncx in range(8):
                                    e.wait_ge(g_kev.h, 9 * h + ncx)
                                    for kc in range(4):
                                        ins = e.matmul(
                                            psn[:, 0:512],
                                            wkvbb[:, kc * NH * 256 + h * 256:kc * NH * 256 + h * 256 + 128],
                                            cnT[:, kc * S + ncx * 512:kc * S + (ncx + 1) * 512],
                                            start=(kc == 0), stop=(kc == 3))
                                    g_kmm.inc(ins, 1)
                                for vs in range(32):
                                    if vs >= 2:
                                        e.wait_ge(g_vev.h, P_vev(grp, h2, vs - 2))
                                    for kc in range(4):
                                        ins = e.matmul(
                                            vV(vs),
                                            cnT[:, kc * S + vs * 128:kc * S + (vs + 1) * 128],
                                            wkvbb[:, kc * NH * 256 + h * 256 + 128:kc * NH * 256 + (h + 1) * 256],
                                            start=(kc == 0), stop=(kc == 3))
                                    g_kmm.inc(ins, 1)
                                if h2 == 0:
                                    e.wait_ge(g_kev.h, P_kev(grp, 0, 8))
                                    if grp >= 1:
                                        e.wait_ge(g_rkb.h, 2 * grp - 1)
                                    ins = e.matmul(ptiny[0:1, 0:1], kq[:, 0:1], s_1f[:, 0:1],
                                                   start=True, stop=True)
                                    g_bc.inc(ins, 1)
                                    e.wait_ge(g_rk.h, grp + 1)
                                    ins = e.matmul(ptiny[:, 2:3], s_1f[0:1, :], rks[0:1, grp:grp + 1],
                                                   start=True, stop=True)
                                    g_bc.inc(ins, 1)
                                    e.wait_ge(g_qtr.h, 16 * 4 * (grp + 1))
                                e.wait_ge(g_kev.h, 9 * (2 * grp + h2) + 8)
                                e.wait_ge(g_vev.h, 32 * (2 * grp + h2) + 32)
                                for g8 in range(8):
                                    ntt = 4 * (g8 + 1)
                                    qsl = qnT[:, h2 * TOWN + g8 * 256:h2 * TOWN + (g8 + 1) * 256]
                                    qrl = qrT[0:65, h2 * TOWN + g8 * 256:h2 * TOWN + (g8 + 1) * 256]
                                    for jj in range(2 * (g8 + 1)):
                                        pi = P_pair[(grp, h2, g8, jj)]
                                        if pi >= 3:
                                            e.wait_ge(g_sex.h, pi - 2)
                                        for k_ in range(2):
                                            tt = 2 * jj + k_
                                            sl = (pi % 2) * 512 + k_ * 256
                                            e.matmul(PS[:, sl:sl + 256],
                                                     knT[:, tt * 128:(tt + 1) * 128],
                                                     qsl, start=True, stop=False)
                                            ins = e.matmul(PS[:, sl:sl + 256],
                                                           krT[0:65, tt * 128:(tt + 1) * 128],
                                                           qrl, start=False, stop=True)
                                        g_smm.inc(ins, 1)
                                        if 2 * jj >= 4 * g8:
                                            e.wait_ge(g_msk.h, P_mskp[(grp, h2, g8, jj)])
                                        else:
                                            e.wait_ge(g_sex.h, pi)
                                        for k_ in range(2):
                                            tt = 2 * jj + k_
                                            if tt == 0 and (grp, h2, g8) != (0, 0, 0):
                                                pg8, ph2, pgrp = g8 - 1, h2, grp
                                                if pg8 < 0:
                                                    pg8 = 7
                                                    ph2 -= 1
                                                    if ph2 < 0:
                                                        ph2, pgrp = 1, grp - 1
                                                e.wait_ge(g_oev.h, P_oev(pgrp, ph2, pg8, 1))
                                            for s2 in range(2):
                                                ins = e.matmul(
                                                    vA(s2),
                                                    PT[:, (pi % 2) * 512 + k_ * 256 + s2 * 128:(pi % 2) * 512 + k_ * 256 + (s2 + 1) * 128],
                                                    Vh[:, tt * 132:tt * 132 + 129],
                                                    start=(tt == 0), stop=(tt == ntt - 1))
                                            g_amm.inc(ins, 1)
                    block.tensor(_t)

                    def _s(e):
                        e.wait_ge(b_fold.h, 1)
                        g_kr0.inc(e.activation(out=junk3[0:64, 0:S], in_=krT[0:64, :],
                                               func=AF.Square, accum_out=kr64[0:64, 0:1]), 1)

                        def kv_evacs(e, grp, h2):
                            for ncx in range(8):
                                e.wait_ge(g_kmm.h, P_kmmK[(grp, h2, ncx)])
                                g_kev.inc(e.activation(
                                    out=knT[:, ncx * 512:(ncx + 1) * 512],
                                    in_=psn[:, 0:512], func=AF.Copy), 1)
                            g_kev.inc(e.activation(out=junk3[:, 0:S], in_=knT[:, :],
                                                   func=AF.Square, accum_out=kq[:, 0:1]), 1)

                        def exps_oev(e, grp, h2):
                            if grp >= 1:
                                e.wait_ge(g_ow.h, 32 * grp)
                            for g8 in range(8):
                                ntt = 4 * (g8 + 1)
                                for jj in range(2 * (g8 + 1)):
                                    pi = P_pair[(grp, h2, g8, jj)]
                                    e.wait_ge(g_smm.h, pi)
                                    if pi >= 3:
                                        e.wait_ge(g_amm.h, P_plast[pi - 2])
                                    ins = e.raw.activation(
                                        out=PT[:, (pi % 2) * 512:(pi % 2 + 1) * 512],
                                        in_=PS[:, (pi % 2) * 512:(pi % 2 + 1) * 512],
                                        func=AF.Exp)
                                    ins.then_inc(g_sex.h, 1)
                                    g_sex.n += 1
                                e.wait_ge(g_amm.h, P_amm[(grp, h2, g8, ntt - 1)])
                                e.wait_ge(g_rc.h, P_rc(grp, h2, g8))
                                for s2 in range(2):
                                    g_oev.inc(e.activation(
                                        out=og[:, (h2 * 16 + g8 * 2 + s2) * 128:(h2 * 16 + g8 * 2 + s2 + 1) * 128],
                                        in_=vA(s2)[:, 0:128],
                                        func=AF.Identity, scale=rcp[:, s2:s2 + 1]), 1)

                        for grp in range(NG):
                            for sub in range(16):
                                gi = 16 * grp + sub
                                e.wait_ge(g_qmm.h, gi + 1)
                                if grp >= 1 and sub == 0:
                                    e.wait_ge(g_qw.h, 64 * grp)
                                g_qev.inc(e.activation(
                                    out=qg[:, sub * 384:(sub + 1) * 384],
                                    in_=vQ(sub),
                                    func=AF.Identity, scale=scol[:, sub:sub + 1]), 1)
                            kv_evacs(e, grp, 0)
                            e.wait_ge(g_qs.h, grp + 1)
                            g_sq.inc(e.activation(out=sqq[:, :], in_=qs2[:, :], func=AF.Sqrt), 1)
                            e.wait_ge(g_rkb.h, 2 * grp + 1)
                            g_rk.inc(e.activation(out=rks[0:1, grp:grp + 1], in_=tta[0:1, 0:1],
                                                  func=AF.Sqrt, scale=1.0 / S), 1)
                            exps_oev(e, grp, 0)
                            kv_evacs(e, grp, 1)
                            exps_oev(e, grp, 1)
                    bscalar(_s)

                    def _v(e):
                        e.wait_ge(a_c.h, 64)
                        e.wait_ge(c_ld.h, 16 * 6)
                        e.memset(Vh[:, :].rearrange("p (v c) -> p v c", c=132)[:, :, 128:132], 1.0)
                        e.memset(qrc[:, :].rearrange("p (s f) -> p s f", f=128)[:, :, 65:128], 0.0)
                        e.wait_ge(g_kr1.h, 1)
                        e.tensor_copy(out=krs[0:1, 0:1], in_=ptiny[0:1, 3:4])

                        def v_evacs(e, grp, h2):
                            for vs in range(32):
                                e.wait_ge(g_kmm.h, P_kmmV[(grp, h2, vs)])
                                g_vev.inc(e.tensor_copy(
                                    out=Vh[:, vs * 132:vs * 132 + 128],
                                    in_=vV(vs)), 1)

                        def masks_recips(e, grp, h2):
                            for g8 in range(8):
                                ntt = 4 * (g8 + 1)
                                for jj in (2 * g8, 2 * g8 + 1):
                                    pi = P_pair[(grp, h2, g8, jj)]
                                    e.wait_ge(g_sex.h, pi)
                                    o0 = 2 * jj - 4 * g8
                                    ins = e.raw.tensor_tensor(
                                        out=PT[:, (pi % 2) * 512:(pi % 2 + 1) * 512],
                                        in0=PT[:, (pi % 2) * 512:(pi % 2 + 1) * 512],
                                        in1=s_mask[:, o0 * 256:(o0 + 2) * 256],
                                        op=ALU.mult)
                                    ins.then_inc(g_msk.h, 1)
                                    g_msk.n += 1
                                e.wait_ge(g_amm.h, P_amm[(grp, h2, g8, ntt - 1)])
                                e.reciprocal(out=rcp[:, 0:1], in_=vA(0)[:, 128:129])
                                g_rc.inc(e.reciprocal(out=rcp[:, 1:2], in_=vA(1)[:, 128:129]), 1)

                        qv = qg[:, :].rearrange("p (s f) -> p s f", f=384)
                        qrv = qrc[:, :].rearrange("p (s f) -> p s f", f=256)
                        co = s_coso[:, :].rearrange("p (s f) -> p s f", f=DR)
                        si_ = s_sino[:, :].rearrange("p (s f) -> p s f", f=DR)
                        for grp in range(NG):
                            for sub in range(16):
                                e.wait_ge(g_qev.h, 16 * grp + sub + 1)
                                for h2 in range(2):
                                    e.tensor_tensor(
                                        out=vjunk[:, 0:192],
                                        in0=qg[:, sub * 384 + h2 * 192:sub * 384 + (h2 + 1) * 192],
                                        in1=qg[:, sub * 384 + h2 * 192:sub * 384 + (h2 + 1) * 192],
                                        op=ALU.mult)
                                    ins = e.tensor_reduce(
                                        qs2[:, h2 * 16 + sub:h2 * 16 + sub + 1],
                                        vjunk[:, 0:192], axis=AX.X, op=ALU.add)
                            g_qs.inc(ins, 1)
                            if grp >= 1:
                                e.wait_ge(g_qw.h, 64 * grp)
                            for h2 in range(2):
                                g_rope.inc(rope6(
                                    e,
                                    qv[:, :, h2 * 192 + 128 + 0:h2 * 192 + 192:2],
                                    qv[:, :, h2 * 192 + 128 + 1:h2 * 192 + 192:2],
                                    co[:, :, 0:DR:2], co[:, :, 1:DR:2],
                                    si_[:, :, 0:DR:2], si_[:, :, 1:DR:2],
                                    qrv[:, :, h2 * 128 + 0:h2 * 128 + 64:2],
                                    qrv[:, :, h2 * 128 + 1:h2 * 128 + 64:2],
                                    512), 1)
                            v_evacs(e, grp, 0)
                            e.wait_ge(g_bc.h, 2 * grp + 1)
                            e.tensor_copy(out=tta[0:1, 0:1], in_=ptiny[0:1, 0:1])
                            g_rkb.inc(e.tensor_tensor(out=tta[0:1, 0:1], in0=tta[0:1, 0:1],
                                                      in1=krs[0:1, 0:1], op=ALU.add), 1)
                            e.wait_ge(g_bc.h, 2 * grp + 2)
                            g_rkb.inc(e.tensor_scalar(out=rkc[:, 0:1], in0=ptiny[:, 2:3],
                                                      scalar1=-C_COEF, scalar2=None,
                                                      op0=ALU.mult), 1)
                            e.wait_ge(g_sq.h, grp + 1)
                            e.tensor_scalar(out=cn_[:, :], in0=sqq[:, :],
                                            scalar1=rkc[:, 0:1], scalar2=-80.0,
                                            op0=ALU.mult, op1=ALU.max)
                            for h2 in range(2):
                                g_cw.inc(e.tensor_copy(
                                    out=qrv[:, :, h2 * 128 + 64:h2 * 128 + 65],
                                    in_=cn_[:, :].rearrange("p (h s) -> p s h", h=2)[:, :, h2:h2 + 1]), 1)
                            masks_recips(e, grp, 0)
                            v_evacs(e, grp, 1)
                            masks_recips(e, grp, 1)
                    bvector(_v)

                    def _sy(e):
                        for grp in range(NG):
                            e.wait_ge(g_qev.h, 16 * (grp + 1))
                            e.wait_ge(g_rope.h, 2 * (grp + 1))
                            e.wait_ge(g_cw.h, 2 * (grp + 1))
                            if grp >= 1:
                                e.wait_ge(g_amm.h, P_amm[(grp - 1, 1, 7, 31)])
                            qv = qg[:, :].rearrange("p (s f) -> p s f", f=384)
                            qrv = qrc[:, :].rearrange("p (s f) -> p s f", f=256)
                            for h2 in range(2):
                                g_qw.inc(e.dma_start(
                                    out=qn_b[h2, :, :].rearrange("(s p) f -> p s f", p=128),
                                    in_=qv[:, :, h2 * 192:h2 * 192 + 128]), 16)
                                g_qw.inc(e.dma_start(
                                    out=qr_b[h2, :, :].rearrange("(s p) f -> p s f", p=128),
                                    in_=qrv[:, :, h2 * 128:(h2 + 1) * 128]), 16)
                            e.wait_ge(g_qw.h, 64 * (grp + 1))
                            for h2 in range(2):
                                g_qtr.inc(e.dma_start(
                                    out=qnT[:, h2 * TOWN:(h2 + 1) * TOWN],
                                    in_=qn_b[h2, :, :], transpose=True), 16)
                                g_qtr.inc(e.dma_start(
                                    out=qrT[:, h2 * TOWN:(h2 + 1) * TOWN],
                                    in_=qr_b[h2, :, :], transpose=True), 16)
                            for h2 in range(2):
                                e.wait_ge(g_oev.h, P_oev(grp, h2, 7, 1))
                                g_ow.inc(e.dma_start(
                                    out=o_b[2 * grp + h2, :, :].rearrange("(s p) f -> p s f", p=128),
                                    in_=og[:, :].rearrange("p (hs f) -> p hs f", f=128)[:, h2 * 16:(h2 + 1) * 16, :]), 16)
                        e.wait_ge(g_ow.h, 16 * 2 * NG)
                    block.sync(_sy)
                    pop_scope()

                cbc.close()

                with ExitStack() as cd:
                    push_scope(cd)
                    sbd = lambda *a, **kw: cd.enter_context(nc.sbuf_tensor(*a, **kw))
                    oT = sbd("d_oT", [128, 16 * TOWN], BF16, is_transpose_dest=True)
                    wob = sbd("d_wob", [128, 16 * 1024], BF16)
                    xg = sbd("d_xg", [128, 2 * D], F32)
                    x2t = sbd("d_x2t", [128, 2 * 1024], F32)
                    pso = cd.enter_context(nc.psum_tensor("d_pso", [128, 2 * 1024], F32))
                    d_tr = mksem(octx, "d_tr")
                    d_wo = mkdsem("d_wo")
                    d_gx = [mkdsem("d_gx0"), mkdsem("d_gx1")]
                    d_mm = mksem(octx, "d_mm")
                    d_ad = mksem(octx, "d_ad")
                    d_xw = [mksem(octx, "d_xw0"), mksem(octx, "d_xw1")]

                    def _sy(e):
                        for hd in range(16):
                            d_tr.inc(e.dma_start(
                                out=oT[:, hd * TOWN:(hd + 1) * TOWN],
                                in_=o_b[hd, :, :], transpose=True), 16)
                        for dh in range(2):
                            for sub in range(16):
                                k = 16 * dh + sub
                                e.wait_ge(d_ad.h, k + 1)
                                d_xw[k % 2].inc(e.dma_start(
                                    out=x2s[sub * 128:(sub + 1) * 128, dh * 1024:(dh + 1) * 1024],
                                    in_=x2t[:, (k % 2) * 1024:(k % 2 + 1) * 1024]), 16)
                        e.wait_ge(d_xw[0].h, 16 * 16)
                        e.wait_ge(d_xw[1].h, 16 * 16)
                    block.sync(_sy)

                    def _g(e):
                        wor = wo.rearrange("(hd p) d -> p hd d", p=128)
                        for dh in range(2):
                            if dh == 1:
                                e.wait_ge(d_mm.h, 16)
                            d_wo.inc(e.dma_start(
                                out=wob[:].rearrange("p (hd f) -> p hd f", f=1024),
                                in_=wor[:, :, dh * 1024:(dh + 1) * 1024]), 16)
                            for sub in range(16):
                                k = 16 * dh + sub
                                if k >= 2:
                                    e.wait_ge(d_ad.h, k - 1)
                                d_gx[k % 2].inc(e.indirect_dma_start(
                                    out=xg[:, (k % 2) * D:(k % 2 + 1) * D], out_offset=None,
                                    in_=x1s[:, :],
                                    in_offset=IOffs(ap=s_ridx[:, sub:sub + 1], axis=0)), 16)
                    block.gpsimd(_g)

                    def _t(e):
                        e.wait_ge(d_tr.h, 16 * 16)
                        for dh in range(2):
                            e.wait_ge(d_wo.h, 16 * (dh + 1))
                            for sub in range(16):
                                k = 16 * dh + sub
                                if k >= 2:
                                    e.wait_ge(d_ad.h, k - 1)
                                for hd in range(16):
                                    for ncx in range(2):
                                        ins = e.matmul(
                                            pso[:, (k % 2) * 1024 + ncx * 512:(k % 2) * 1024 + (ncx + 1) * 512],
                                            oT[:, hd * TOWN + sub * 128:hd * TOWN + (sub + 1) * 128],
                                            wob[:, hd * 1024 + ncx * 512:hd * 1024 + (ncx + 1) * 512],
                                            start=(hd == 0), stop=(hd == 15))
                                d_mm.inc(ins, 1)
                    block.tensor(_t)

                    def _v(e):
                        for dh in range(2):
                            for sub in range(16):
                                k = 16 * dh + sub
                                e.wait_ge(d_mm.h, k + 1)
                                e.wait_ge(d_gx[k % 2].h, 16 * (k // 2 + 1))
                                if k >= 2:
                                    e.wait_ge(d_xw[k % 2].h, 16 * (k // 2))
                                d_ad.inc(e.tensor_tensor(
                                    out=x2t[:, (k % 2) * 1024:(k % 2 + 1) * 1024],
                                    in0=pso[:, (k % 2) * 1024:(k % 2 + 1) * 1024],
                                    in1=xg[:, (k % 2) * D + dh * 1024:(k % 2) * D + (dh + 1) * 1024],
                                    op=ALU.add), 1)
                    bvector(_v)
                    pop_scope()

        if 3 in stages:
            emit_ffn(x2s, out_t, TOWN)

    return nc


def _host_consts():
    inv = 1.0 / (10000.0 ** (np.arange(0, DR, 2, dtype=np.float32) / DR))
    pos = np.arange(S, dtype=np.float32)
    freqs = np.outer(pos, inv).astype(np.float32)
    emb = np.concatenate([freqs, freqs], axis=-1)
    cosf = np.cos(emb).astype(np.float32)
    sinf = np.sin(emb).astype(np.float32)
    H64 = _hadamard(64)
    H128 = _hadamard(128)
    HA = np.kron(np.eye(2), H64)
    hc = np.zeros((128, 256), np.float32)
    hc[:, 0:128] = HA
    hc[:, 128:256] = H128
    hconst = hc.astype(ml_dtypes.bfloat16)
    onesf = np.ones((128, 128), np.float32)
    onesb = np.ones((128, 1), np.float32).astype(ml_dtypes.bfloat16)
    return cosf, sinf, hconst, onesf, onesb


def make_in_maps(inputs):
    x = np.ascontiguousarray(np.asarray(inputs["L0_attn_x"], dtype=np.float32))
    W1 = np.asarray(inputs["L0_ffn_W1"], dtype=np.float32)
    W2 = np.asarray(inputs["L0_ffn_W2"], dtype=np.float32)
    W3 = np.asarray(inputs["L0_ffn_W3"], dtype=np.float32)

    w1tg = np.ascontiguousarray(W1.reshape(128, 64, 16, 128).transpose(1, 2, 3, 0))
    w3tg = np.ascontiguousarray(W3.reshape(128, 64, 16, 128).transpose(1, 2, 3, 0))
    w2t = np.ascontiguousarray(W2.T)

    cosf, sinf, hconst, onesf, onesb = _host_consts()

    base = {
        "w1tg": w1tg, "w3tg": w3tg, "w2t": w2t,
        "wqa": np.ascontiguousarray(inputs["L0_attn_w_qa"], dtype=np.float32),
        "wqb": np.ascontiguousarray(inputs["L0_attn_w_qb"], dtype=np.float32),
        "wkva": np.ascontiguousarray(inputs["L0_attn_w_kva"], dtype=np.float32),
        "wkvb": np.ascontiguousarray(inputs["L0_attn_w_kvb"], dtype=np.float32),
        "wo": np.ascontiguousarray(inputs["L0_attn_w_o"], dtype=np.float32),
        "qnw": np.ascontiguousarray(np.asarray(inputs["L0_attn_q_a_norm_w"], np.float32).reshape(6, 128).T),
        "kvnw": np.ascontiguousarray(np.asarray(inputs["L0_attn_kv_a_norm_w"], np.float32).reshape(4, 128).T),
        "cos_f": cosf, "sin_f": sinf,
        "hconst": hconst, "ones_f": onesf, "ones_b": onesb,
        "epsc": np.broadcast_to(
            np.array([LNEPS, RMSEPS, QSC_BIAS, 0.0], np.float32), (128, 4)).copy(),
    }

    in_maps = []
    own_rows_all = []
    for c in range(8):
        b, half = c // 2, c % 2
        own = np.concatenate([np.arange(g * 512 + half * 256, g * 512 + half * 256 + 256)
                              for g in range(8)]).astype(np.int64)
        own_rows_all.append(own)
        rix = own.reshape(16, 128).T.astype(np.uint32)
        mask = np.zeros((128, 4, 256), np.float32)
        ti = np.arange(128)[:, None]
        qi = np.arange(256)[None, :]
        for o in range(4):
            mask[:, o, :] = (o * 128 + ti - half * 256 <= qi).astype(np.float32)
        im = dict(base)
        im["x"] = np.ascontiguousarray(x[b])
        im["cos_o"] = np.ascontiguousarray(cosf[own])
        im["sin_o"] = np.ascontiguousarray(sinf[own])
        im["maskc"] = np.ascontiguousarray(mask.reshape(128, 4 * 256).astype(ml_dtypes.bfloat16))
        im["ridx"] = np.ascontiguousarray(rix)
        in_maps.append(im)
    return in_maps, own_rows_all


def kernel(**inputs):
    in_maps, own_rows_all = make_in_maps(inputs)
    nc = build_nc()
    res = run_bass_kernel_spmd(nc, in_maps, core_ids=list(range(8)))
    out = np.empty((BATCH, S, D), np.float32)
    for c in range(8):
        b = c // 2
        out[b, own_rows_all[c]] = np.asarray(res.results[c]["out"], np.float32)
    return out


# version 07
# speedup vs baseline: 1.2016x; 1.2016x over previous
import math
from contextlib import ExitStack

import numpy as np
import ml_dtypes
import concourse.bass as bass
import concourse.mybir as mybir
import bass_rust
from concourse.bass_utils import run_bass_kernel_spmd

F32 = mybir.dt.float32
BF16 = mybir.dt.bfloat16
U32 = mybir.dt.uint32
AF = mybir.ActivationFunctionType
ALU = mybir.AluOpType
AX = mybir.AxisListType
IOffs = bass_rust.IndirectOffsetOnAxis

BATCH, S, D, HID, NH = 4, 4096, 2048, 8192, 16
DN, DR, DV, DQK = 128, 64, 128, 192
QL, KVL = 768, 512
TOWN = 2048
LNEPS = 1e-5
RMSEPS = 1e-6
W2SCALE = 0.5 / math.sqrt(HID)
QSC_BIAS = math.log(1.0 / math.sqrt(DQK))
C_COEF = 8.0 / math.sqrt(DQK)
NG = 8
NTT_G = sum(4 * (g8 + 1) for g8 in range(8))


def _hadamard(n):
    h = np.array([[1.0]])
    while h.shape[0] < n:
        h = np.block([[h, h], [h, -h]])
    return h


class Sem:
    def __init__(self, ctx, nc, name):
        self.h = ctx.enter_context(nc.semaphore(name))
        self.n = 0

    def inc(self, instr, k=1):
        instr.then_inc(self.h, k)
        self.n += k
        return self.n


_WRAP_OPS = frozenset((
    "activation", "copy", "add", "mul", "tensor_tensor", "tensor_scalar",
    "tensor_copy", "tensor_reduce", "reduce_sum", "reduce_max", "reciprocal",
    "memset", "tensor_tensor_reduce", "select", "transpose", "iota",
))


class _Shim:
    def __init__(self, e):
        self._e = e

    def then_inc(self, h, k):
        self._e.sem_inc(h, k)
        return self


class SafeEng:

    def __init__(self, e, prog):
        self._e = e
        self._p = prog

    @property
    def raw(self):
        return self._e

    def __getattr__(self, nm):
        a = getattr(self._e, nm)
        if nm in _WRAP_OPS:
            def wrap(*args, **kw):
                ins = a(*args, **kw)
                self._p.n += 1
                ins.then_inc(self._p.h, 1)
                self._e.wait_ge(self._p.h, self._p.n)
                return _Shim(self._e)
            return wrap
        return a


def build_nc(stages=(0, 1, 2, 3), debug=False):
    nc = bass.Bass("TRN2", target_bir_lowering=False)

    x_in = nc.dram_tensor("x", [S, D], F32, kind="ExternalInput").ap()
    w1tg = nc.dram_tensor("w1tg", [64, 16, 128, 128], F32, kind="ExternalInput").ap()
    w3tg = nc.dram_tensor("w3tg", [64, 16, 128, 128], F32, kind="ExternalInput").ap()
    w2t = nc.dram_tensor("w2t", [HID, D], F32, kind="ExternalInput").ap()
    wqa = nc.dram_tensor("wqa", [D, QL], F32, kind="ExternalInput").ap()
    wqb = nc.dram_tensor("wqb", [QL, NH * DQK], F32, kind="ExternalInput").ap()
    wkva = nc.dram_tensor("wkva", [D, KVL + DR], F32, kind="ExternalInput").ap()
    wkvb = nc.dram_tensor("wkvb", [KVL, NH * 256], F32, kind="ExternalInput").ap()
    wo = nc.dram_tensor("wo", [NH * DV, D], F32, kind="ExternalInput").ap()
    qnw = nc.dram_tensor("qnw", [128, 6], F32, kind="ExternalInput").ap()
    kvnw = nc.dram_tensor("kvnw", [128, 4], F32, kind="ExternalInput").ap()
    cos_f = nc.dram_tensor("cos_f", [S, DR], F32, kind="ExternalInput").ap()
    sin_f = nc.dram_tensor("sin_f", [S, DR], F32, kind="ExternalInput").ap()
    cos_o = nc.dram_tensor("cos_o", [TOWN, DR], F32, kind="ExternalInput").ap()
    sin_o = nc.dram_tensor("sin_o", [TOWN, DR], F32, kind="ExternalInput").ap()
    hconst = nc.dram_tensor("hconst", [128, 256], BF16, kind="ExternalInput").ap()
    ones_f = nc.dram_tensor("ones_f", [128, 128], F32, kind="ExternalInput").ap()
    ones_b = nc.dram_tensor("ones_b", [128, 1], BF16, kind="ExternalInput").ap()
    maskc = nc.dram_tensor("maskc", [128, 4 * 256], BF16, kind="ExternalInput").ap()
    epsc = nc.dram_tensor("epsc", [128, 4], F32, kind="ExternalInput").ap()
    ridx = nc.dram_tensor("ridx", [128, 16], U32, kind="ExternalInput").ap()

    out_t = nc.dram_tensor("out", [TOWN, D], F32, kind="ExternalOutput").ap()

    dbg = "ExternalOutput" if debug else "Internal"
    zsc = nc.dram_tensor("zsc", [HID, D], BF16, kind="Internal").ap()
    wf = nc.dram_tensor("wfs", [64, 128, D], BF16, kind=dbg).ap()
    x1s = nc.dram_tensor("x1s", [S, D], F32, kind=dbg).ap()
    x2s = nc.dram_tensor("x2s", [TOWN, D], F32, kind=dbg).ap()
    lnt_b = nc.dram_tensor("lnt_b", [1024, D], BF16, kind="Internal").ap()
    ln2_b = nc.dram_tensor("ln2_b", [S, D], BF16, kind="Internal").ap()
    lno_b = nc.dram_tensor("lno_b", [TOWN, D], BF16, kind="Internal").ap()
    cn_b = nc.dram_tensor("cn_b", [S, KVL], BF16, kind="Internal").ap()
    kr_b = nc.dram_tensor("kr_b", [S, 128], BF16, kind="Internal").ap()
    qn_b = nc.dram_tensor("qn_b", [2, TOWN, 128], BF16, kind="Internal").ap()
    qr_b = nc.dram_tensor("qr_b", [2, TOWN, 128], BF16, kind="Internal").ap()
    o_b = nc.dram_tensor("o_b", [NH, TOWN, DV], BF16, kind="Internal").ap()

    octx = ExitStack()
    with octx:
        osb = lambda *a, **kw: octx.enter_context(nc.sbuf_tensor(*a, **kw))
        s_hc = osb("s_hc", [128, 256], BF16)
        s_1f = osb("s_1f", [128, 128], F32)
        s_1b = osb("s_1b", [128, 1], BF16)
        s_mask = osb("s_mask", [128, 4 * 256], BF16)
        s_ridx = osb("s_ridx", [128, 16], U32)
        s_eps = osb("s_eps", [128, 4], F32)

        ph = Sem(octx, nc, "ph")
        block = octx.enter_context(nc.Block())
        prog_v = Sem(octx, nc, "prog_v")
        prog_s = Sem(octx, nc, "prog_s")
        def bvector(fn):
            block.vector(lambda e: fn(SafeEng(e, prog_v)))

        def bscalar(fn):
            block.scalar(lambda e: fn(SafeEng(e, prog_s)))

        sem_id = [0]

        def push_scope(ctx_):
            pass

        def mksem(_ctx, label=None):
            sem_id[0] += 1
            return Sem(octx, nc, label or ("s%d" % sem_id[0]))

        mkdsem = lambda label: mksem(None, label)

        def pop_scope():
            barrier()

        def barrier():
            tgt = ph.n + 5
            ph.n = tgt
            for name in ("sync", "gpsimd", "tensor", "vector", "scalar"):
                def _f(e, tgt=tgt):
                    e.sem_inc(ph.h, 1)
                    e.wait_ge(ph.h, tgt)
                getattr(block, name)(_f)

        push_scope(octx)
        c_ld = Sem(octx, nc, "c_ld")

        def _consts(e):
            for dst, src in ((s_hc, hconst), (s_1f, ones_f), (s_1b, ones_b),
                             (s_mask, maskc), (s_ridx, ridx), (s_eps, epsc)):
                c_ld.inc(e.dma_start(out=dst[:], in_=src[:, :]), 16)
        block.sync(_consts)

        if 0 in stages:
            with ExitStack() as c0:
                push_scope(c0)
                sb0 = lambda *a, **kw: c0.enter_context(nc.sbuf_tensor(*a, **kw))
                s_w2 = sb0("s_w2", [128, 2 * D], BF16)
                s_z = sb0("s_z", [128, 2 * D], BF16)
                s_zs = sb0("s_zs", [128, 2 * D], BF16)
                s_wfo = sb0("s_wfo", [128, 2 * D], BF16)
                pA = c0.enter_context(nc.psum_tensor("p0a", [128, D], F32))
                pB = c0.enter_context(nc.psum_tensor("p0b", [128, D], F32))
                w_ld = [mkdsem("w_ld0"), mkdsem("w_ld1")]
                w_mm = mksem(octx, "w_mm")
                w_ev = mksem(octx, "w_ev")
                w_st = [mksem(octx, "w_st0"), mksem(octx, "w_st1")]
                w_fst = [mksem(octx, "w_fst0"), mksem(octx, "w_fst1")]
                w_zl = [mksem(octx, "w_zl0"), mksem(octx, "w_zl1")]

                def _g(e):
                    for hc in range(64):
                        if hc >= 2:
                            e.wait_ge(w_mm.h, hc - 1)
                        w_ld[hc % 2].inc(e.dma_start(
                            out=s_w2[:, (hc % 2) * D:(hc % 2 + 1) * D],
                            in_=w2t[hc * 128:(hc + 1) * 128, :]), 16)
                block.gpsimd(_g)

                def _t(e):
                    e.wait_ge(c_ld.h, 16 * 6)
                    for hc in range(64):
                        e.wait_ge(w_ld[hc % 2].h, 16 * (hc // 2 + 1))
                        if hc >= 2:
                            e.wait_ge(w_ev.h, hc - 1)
                        pp = pA if hc % 2 == 0 else pB
                        for ncx in range(4):
                            ins = e.matmul(
                                pp[:, ncx * 512:(ncx + 1) * 512],
                                s_hc[:, 0:128],
                                s_w2[:, (hc % 2) * D + ncx * 512:(hc % 2) * D + (ncx + 1) * 512],
                                start=True, stop=True)
                        w_mm.inc(ins, 1)
                    for si in range(64):
                        e.wait_ge(w_zl[si % 2].h, 16 * (si // 2 + 1))
                        if si >= 2:
                            e.wait_ge(w_ev.h, 64 + si - 1)
                        pp = pA if si % 2 == 0 else pB
                        for ncx in range(4):
                            ins = e.matmul(
                                pp[:, ncx * 512:(ncx + 1) * 512],
                                s_hc[:, 128:256],
                                s_zs[:, (si % 2) * D + ncx * 512:(si % 2) * D + (ncx + 1) * 512],
                                start=True, stop=True)
                        w_mm.inc(ins, 1)
                block.tensor(_t)

                def _s(e):
                    for hc in range(64):
                        e.wait_ge(w_mm.h, hc + 1)
                        if hc >= 2:
                            e.wait_ge(w_st[hc % 2].h, 16 * (hc // 2))
                        pp = pA if hc % 2 == 0 else pB
                        w_ev.inc(e.activation(
                            out=s_z[:, (hc % 2) * D:(hc % 2 + 1) * D],
                            in_=pp[:, :], func=AF.Copy), 1)
                        w_st[hc % 2].inc(e.dma_start(
                            out=zsc[hc * 128:(hc + 1) * 128, :],
                            in_=s_z[:, (hc % 2) * D:(hc % 2 + 1) * D]), 16)
                    for si in range(64):
                        e.wait_ge(w_mm.h, 64 + si + 1)
                        if si >= 2:
                            e.wait_ge(w_fst[si % 2].h, 16 * (si // 2))
                        pp = pA if si % 2 == 0 else pB
                        w_ev.inc(e.activation(
                            out=s_wfo[:, (si % 2) * D:(si % 2 + 1) * D],
                            in_=pp[:, :], func=AF.Identity, scale=W2SCALE), 1)
                        w_fst[si % 2].inc(e.dma_start(
                            out=wf[si, :, :],
                            in_=s_wfo[:, (si % 2) * D:(si % 2 + 1) * D]), 16)
                bscalar(_s)

                def _sy(e):
                    zr = zsc.rearrange("(j s) d -> s j d", s=64)
                    for si in range(64):
                        if si == 0:
                            e.wait_ge(w_st[0].h, 16 * 32)
                            e.wait_ge(w_st[1].h, 16 * 32)
                        if si >= 2:
                            e.wait_ge(w_mm.h, 64 + si - 1)
                        w_zl[si % 2].inc(e.dma_start(
                            out=s_zs[:, (si % 2) * D:(si % 2 + 1) * D],
                            in_=zr[si, :, :]), 16)
                    e.wait_ge(w_fst[0].h, 16 * 32)
                    e.wait_ge(w_fst[1].h, 16 * 32)
                block.sync(_sy)
                pop_scope()

        FFN_ID = [0]
        FFN_SEMS = {}

        def emit_ffn(x_src, x_dst, T):
            npass = T // 1024
            FFN_ID[0] += 1
            tag = "f%d_" % FFN_ID[0]
            with ExitStack() as cf:
                push_scope(cf)
                sbf = lambda nm, *a, **kw: cf.enter_context(nc.sbuf_tensor(tag + nm, *a, **kw))
                acc = sbf("acc", [128, 8 * D], F32)
                l1t = sbf("l1t", [128, 16 * 1024], BF16, is_transpose_dest=True)
                hT = sbf("hT", [128, 8 * 1024], BF16)
                wfg = sbf("wfg", [128, 8 * D], BF16)
                sW1 = sbf("w1", [128, 2 * 2048], BF16)
                sW3 = sbf("w3", [128, 2 * 2048], BF16)
                sg = sbf("sg", [128, 2 * 1024], BF16)
                junk = sbf("junk", [128, D], BF16)
                lnsb = sbf("lnsb", [128, 2 * D], BF16)
                xl = sbf("xl", [128, 2 * D], F32)
                stat = sbf("stat", [128, 8 * 6], F32)
                psg = cf.enter_context(nc.psum_tensor(tag + "psg", [128, 1024], F32))
                psl = cf.enter_context(nc.psum_tensor(tag + "psl", [128, 1024], F32))
                psda = cf.enter_context(nc.psum_tensor(tag + "psda", [128, 1024], F32))
                psdb = cf.enter_context(nc.psum_tensor(tag + "psdb", [128, 1024], F32))
                psdh = [psda, psdb]
                if not FFN_SEMS:
                    FFN_SEMS["pb"] = 0
                    for nm_ in ("f_x", "f_tr", "f_wf", "f_mmu", "f_mmd", "f_se",
                                "f_vm", "f_da", "f_acw"):
                        FFN_SEMS[nm_] = mksem(octx, nm_)
                    FFN_SEMS["f_lnw"] = [mksem(octx, "f_lnw0"), mksem(octx, "f_lnw1")]
                    FFN_SEMS["f_xl"] = [mksem(octx, "f_xl0"), mksem(octx, "f_xl1")]
                    FFN_SEMS["f_wl"] = [mkdsem("f_wl0"), mkdsem("f_wl1")]
                pb = FFN_SEMS["pb"]
                FFN_SEMS["pb"] = pb + npass
                f_x = FFN_SEMS["f_x"]
                f_lnw = FFN_SEMS["f_lnw"]
                f_tr = FFN_SEMS["f_tr"]
                f_wf = FFN_SEMS["f_wf"]
                f_wl = FFN_SEMS["f_wl"]
                f_mmu = FFN_SEMS["f_mmu"]
                f_mmd = FFN_SEMS["f_mmd"]
                f_se = FFN_SEMS["f_se"]
                f_vm = FFN_SEMS["f_vm"]
                f_da = FFN_SEMS["f_da"]
                f_xl = FFN_SEMS["f_xl"]
                f_acw = FFN_SEMS["f_acw"]

                accv = acc[:, :].rearrange("p (t d) -> p t d", d=D)

                def stt(ti, j):
                    return stat[:, ti * 6 + j:ti * 6 + j + 1]

                def _sy(e):
                    for p0 in range(npass):
                        p = pb + p0
                        if p > 0:
                            e.wait_ge(f_da.h, 128 * p)
                            e.wait_ge(f_acw.h, 16 * p)
                        f_x.inc(e.dma_start(
                            out=accv[:, :, :],
                            in_=x_src[p0 * 1024:(p0 + 1) * 1024, :]
                                .rearrange("(t q) d -> q t d", q=128)), 16)
                        for u in range(64):
                            g = u // 8
                            e.wait_ge(f_mmd.h, 128 * p + 16 * g)
                            f_wf.inc(e.dma_start(
                                out=wfg[:, (u % 8) * D:(u % 8 + 1) * D],
                                in_=wf[u, :, :]), 16)
                        e.wait_ge(f_da.h, 128 * (p + 1))
                        f_acw.inc(e.dma_start(
                            out=x_dst[p0 * 1024:(p0 + 1) * 1024, :]
                                .rearrange("(t q) d -> q t d", q=128),
                            in_=accv[:, :, :]), 16)
                    e.wait_ge(f_acw.h, 16 * (pb + npass))
                block.sync(_sy)

                def _g(e):
                    for p0 in range(npass):
                        p = pb + p0
                        for u in range(64):
                            uu = 64 * p + u
                            if uu >= 2:
                                e.wait_ge(f_mmu.h, 2 * uu - 2)
                            sl = u % 2
                            f_wl[sl].inc(e.dma_start(
                                out=sW1[:, sl * 2048:(sl + 1) * 2048]
                                    .rearrange("p (dc j) -> p dc j", j=128),
                                in_=w1tg[u, :, :, :].rearrange("dc p j -> p dc j")), 16)
                            f_wl[sl].inc(e.dma_start(
                                out=sW3[:, sl * 2048:(sl + 1) * 2048]
                                    .rearrange("p (dc j) -> p dc j", j=128),
                                in_=w3tg[u, :, :, :].rearrange("dc p j -> p dc j")), 16)
                block.gpsimd(_g)

                def _t(e):
                    for p0 in range(npass):
                        p = pb + p0
                        e.wait_ge(f_tr.h, 16 * 16 * (p + 1))
                        for g in range(8):
                            for ul in range(8):
                                u = 8 * g + ul
                                uu = 64 * p + u
                                e.wait_ge(f_wl[uu % 2].h, 32 * (uu // 2 + 1))
                                e.wait_ge(f_se.h, 64 * p + u)
                                sl = u % 2
                                for dc in range(16):
                                    for ncx in range(2):
                                        ins = e.matmul(
                                            psg[:, ncx * 512:(ncx + 1) * 512],
                                            sW1[:, sl * 2048 + dc * 128:sl * 2048 + (dc + 1) * 128],
                                            l1t[:, dc * 1024 + ncx * 512:dc * 1024 + (ncx + 1) * 512],
                                            start=(dc == 0), stop=(dc == 15))
                                f_mmu.inc(ins, 1)
                                e.wait_ge(f_vm.h, 64 * p + u)
                                for dc in range(16):
                                    for ncx in range(2):
                                        ins = e.matmul(
                                            psl[:, ncx * 512:(ncx + 1) * 512],
                                            sW3[:, sl * 2048 + dc * 128:sl * 2048 + (dc + 1) * 128],
                                            l1t[:, dc * 1024 + ncx * 512:dc * 1024 + (ncx + 1) * 512],
                                            start=(dc == 0), stop=(dc == 15))
                                f_mmu.inc(ins, 1)
                            e.wait_ge(f_vm.h, 64 * p + 8 * (g + 1))
                            e.wait_ge(f_wf.h, 16 * (64 * p + 8 * (g + 1)))
                            for ti in range(8):
                                for hf in range(2):
                                    wv = 128 * p + 16 * g + 2 * ti - 1 + hf
                                    if wv > 0:
                                        e.wait_ge(f_da.h, wv)
                                    for ul2 in range(8):
                                        for nc2 in range(2):
                                            ins = e.matmul(
                                                psdh[hf][:, nc2 * 512:(nc2 + 1) * 512],
                                                hT[:, ul2 * 1024 + ti * 128:ul2 * 1024 + (ti + 1) * 128],
                                                wfg[:, ul2 * D + (2 * hf + nc2) * 512:ul2 * D + (2 * hf + nc2 + 1) * 512],
                                                start=(ul2 == 0), stop=(ul2 == 7))
                                    f_mmd.inc(ins, 1)
                block.tensor(_t)

                def ln_tile(e, q, ti):
                    def xload(ti):
                        kk = 8 * q + ti
                        f_xl[kk % 2].inc(e.dma_start(
                            out=xl[:, (kk % 2) * D:(kk % 2 + 1) * D],
                            in_=x_src[(q - pb) * 1024 + ti * 128:
                                      (q - pb) * 1024 + (ti + 1) * 128, :]), 16)
                    if ti == 0:
                        xload(0)
                        xload(1)
                    if True:
                        kk = 8 * q + ti
                        sl = (kk % 2) * D
                        e.wait_ge(f_xl[kk % 2].h, 16 * (kk // 2 + 1))
                        e.activation(out=junk[:, :], in_=xl[:, sl:sl + D],
                                     func=AF.Identity, accum_out=stt(ti, 0))
                        e.activation(out=stt(ti, 1), in_=stt(ti, 0),
                                     func=AF.Identity, scale=-1.0 / D)
                        e.activation(out=junk[:, :], in_=xl[:, sl:sl + D],
                                     func=AF.Square, bias=stt(ti, 1),
                                     accum_out=stt(ti, 4))
                        e.activation(out=stt(ti, 5), in_=stt(ti, 4),
                                     func=AF.Ln, scale=1.0 / D, bias=s_eps[:, 0:1])
                        e.activation(out=stt(ti, 3), in_=stt(ti, 5),
                                     func=AF.Exp, scale=-0.5)
                        e.activation(out=stt(ti, 2), in_=stt(ti, 1),
                                     func=AF.Identity, scale=stt(ti, 3))
                        if kk >= 2:
                            e.wait_ge(f_lnw[kk % 2].h, 16 * (kk // 2))
                        e.activation(
                            out=lnsb[:, sl:sl + D],
                            in_=xl[:, sl:sl + D], func=AF.Identity,
                            scale=stt(ti, 3), bias=stt(ti, 2))
                        f_lnw[kk % 2].inc(e.dma_start(
                            out=lnt_b[ti * 128:(ti + 1) * 128, :],
                            in_=lnsb[:, sl:sl + D]), 16)
                        if ti + 2 < 8:
                            xload(ti + 2)

                def ln_trans(e, q):
                    e.wait_ge(f_lnw[0].h, 64 * (q + 1))
                    e.wait_ge(f_lnw[1].h, 64 * (q + 1))
                    if q > 0:
                        e.wait_ge(f_mmu.h, 128 * q)
                    for dc in range(16):
                        f_tr.inc(e.dma_start(
                            out=l1t[:, dc * 1024:(dc + 1) * 1024],
                            in_=lnt_b[:, dc * 128:(dc + 1) * 128],
                            transpose=True), 16)

                def _s(e):
                    for ti in range(8):
                        ln_tile(e, pb, ti)
                    ln_trans(e, pb)
                    for p0 in range(npass):
                        p = pb + p0
                        for u in range(64):
                            uu = 64 * p + u
                            e.wait_ge(f_mmu.h, 2 * uu + 1)
                            if uu >= 2:
                                e.wait_ge(f_vm.h, uu - 1)
                            f_se.inc(e.activation(
                                out=sg[:, (u % 2) * 1024:(u % 2 + 1) * 1024],
                                in_=psg[:, :], func=AF.Silu), 1)
                            if p0 + 1 < npass and u % 8 == 7:
                                ln_tile(e, p + 1, u // 8)
                        if p0 + 1 < npass:
                            ln_trans(e, p + 1)
                bscalar(_s)

                def _v(e):
                    for p0 in range(npass):
                        p = pb + p0
                        for u in range(64):
                            uu = 64 * p + u
                            e.wait_ge(f_se.h, uu + 1)
                            e.wait_ge(f_mmu.h, 2 * uu + 2)
                            f_vm.inc(e.tensor_tensor(
                                out=hT[:, (u % 8) * 1024:(u % 8 + 1) * 1024],
                                in0=sg[:, (u % 2) * 1024:(u % 2 + 1) * 1024],
                                in1=psl[:, :], op=ALU.mult), 1)
                            if u % 8 == 7:
                                g = u // 8
                                for ti in range(8):
                                    for hf in range(2):
                                        e.wait_ge(f_mmd.h, 128 * p + 16 * g + 2 * ti + hf + 1)
                                        if g == 0 and ti == 0 and hf == 0:
                                            e.wait_ge(f_x.h, 16 * (p + 1))
                                        f_da.inc(e.tensor_tensor(
                                            out=accv[:, ti, hf * 1024:(hf + 1) * 1024],
                                            in0=accv[:, ti, hf * 1024:(hf + 1) * 1024],
                                            in1=psdh[hf][:, :], op=ALU.add), 1)
                bvector(_v)
                pop_scope()

        if 1 in stages:
            emit_ffn(x_in, x1s, S)

        if 2 in stages:
            with ExitStack() as c2:
                push_scope(c2)
                sb2 = lambda *a, **kw: c2.enter_context(nc.sbuf_tensor(*a, **kw))
                s_cosf = sb2("s_cosf", [128, 32 * DR], F32)
                s_sinf = sb2("s_sinf", [128, 32 * DR], F32)
                s_coso = sb2("s_coso", [128, 16 * DR], F32)
                s_sino = sb2("s_sino", [128, 16 * DR], F32)
                cnT = sb2("cnT", [128, 4 * S], BF16, is_transpose_dest=True)
                krT = sb2("krT", [128, S], BF16, is_transpose_dest=True)
                scol = sb2("scol", [128, 16], F32)
                tmp1 = sb2("tmp1", [128, 512], F32)
                tmp2 = sb2("tmp2", [128, 512], F32)
                a_c = mksem(octx, "a_c")
                b_fold = mksem(octx, "b_fold")

                def _cl(e):
                    for dst, src in ((s_cosf, cos_f), (s_sinf, sin_f),
                                     (s_coso, cos_o), (s_sino, sin_o)):
                        a_c.inc(e.dma_start(
                            out=dst[:].rearrange("p (t f) -> p t f", f=DR),
                            in_=src.rearrange("(t q) f -> q t f", q=128)), 16)
                block.sync(_cl)

                def rope6(e, srcE, srcO, cE, cO, sE, sO, dE, dO, w):
                    e.tensor_tensor(out=tmp1[:, 0:w], in0=srcE, in1=cE, op=ALU.mult)
                    e.tensor_tensor(out=tmp2[:, 0:w], in0=srcO, in1=sE, op=ALU.mult)
                    e.tensor_tensor(out=dE, in0=tmp1[:, 0:w], in1=tmp2[:, 0:w], op=ALU.subtract)
                    e.tensor_tensor(out=tmp1[:, 0:w], in0=srcO, in1=cO, op=ALU.mult)
                    e.tensor_tensor(out=tmp2[:, 0:w], in0=srcE, in1=sO, op=ALU.mult)
                    return e.tensor_tensor(out=dO, in0=tmp1[:, 0:w], in1=tmp2[:, 0:w], op=ALU.add)

                with ExitStack() as ca:
                    push_scope(ca)
                    sba = lambda *a, **kw: ca.enter_context(nc.sbuf_tensor(*a, **kw))
                    xc = sba("a_xc", [128, 8 * D], F32)
                    l2t = sba("a_l2t", [128, 2 * 16 * 512], BF16, is_transpose_dest=True)
                    lnsb2 = sba("a_lnsb", [128, 2 * D], BF16)
                    junk2 = sba("a_junk", [128, D], BF16)
                    wkvab = sba("a_wkvab", [128, 16 * (KVL + DR)], BF16)
                    cnsb = sba("a_cnsb", [128, 2 * KVL], BF16)
                    krf = sba("a_krf", [128, 2 * DR], F32)
                    krsb = sba("a_krsb", [128, 2 * 128], BF16)
                    ast = sba("a_stat", [128, 32 * 6], F32)
                    psk = ca.enter_context(nc.psum_tensor("a_psk", [128, 2 * 1024], F32))
                    a_x = [mksem(octx, "a_x0"), mksem(octx, "a_x1")]
                    a_vst = mksem(octx, "a_vst")
                    a_sr = mksem(octx, "a_sr")
                    a_sev = mksem(octx, "a_sev")
                    a_lnw = [mksem(octx, "a_lnw0"), mksem(octx, "a_lnw1")]
                    a_tr = mksem(octx, "a_tr")
                    a_kmm = mksem(octx, "a_kmm")
                    a_kev = mksem(octx, "a_kev")
                    a_kro = mksem(octx, "a_kro")
                    a_cnw = [mkdsem("a_cnw0"), mkdsem("a_cnw1")]
                    a_wk = mkdsem("a_wk")

                    def ss(k, j):
                        return ast[:, k * 6 + j:k * 6 + j + 1]

                    def _g(e):
                        a_wk.inc(e.dma_start(
                            out=wkvab[:].rearrange("p (dc f) -> p dc f", f=KVL + DR),
                            in_=wkva.rearrange("(dc p) f -> p dc f", p=128)), 16)
                        for k in range(32):
                            e.wait_ge(a_kev.h, k + 1)
                            e.wait_ge(a_kro.h, k + 1)
                            a_cnw[k % 2].inc(e.dma_start(
                                out=cn_b[k * 128:(k + 1) * 128, :],
                                in_=cnsb[:, (k % 2) * KVL:(k % 2 + 1) * KVL]), 16)
                            a_cnw[k % 2].inc(e.dma_start(
                                out=kr_b[k * 128:(k + 1) * 128, :],
                                in_=krsb[:, (k % 2) * 128:(k % 2 + 1) * 128]), 16)
                    block.gpsimd(_g)

                    def _sy(e):
                        for sc in range(8):
                            if sc >= 2:
                                e.wait_ge(a_sev.h, 4 * (sc - 1))
                            hx = (sc % 2) * 4
                            a_x[sc % 2].inc(e.dma_start(
                                out=xc[:, hx * D:(hx + 4) * D].rearrange("p (t d) -> p t d", d=D),
                                in_=x1s[sc * 512:(sc + 1) * 512, :].rearrange("(t q) d -> q t d", q=128)), 16)
                            for ti in range(4):
                                k = 4 * sc + ti
                                e.wait_ge(a_sev.h, k + 1)
                                a_lnw[k % 2].inc(e.dma_start(
                                    out=ln2_b[k * 128:(k + 1) * 128, :],
                                    in_=lnsb2[:, (k % 2) * D:(k % 2 + 1) * D]), 16)
                        e.wait_ge(a_cnw[0].h, 32 * 16)
                        e.wait_ge(a_cnw[1].h, 32 * 16)
                    block.sync(_sy)

                    def _v(e):
                        e.wait_ge(a_c.h, 64)
                        e.memset(krsb[:, DR:128], 0.0)
                        e.memset(krsb[:, 128 + DR:256], 0.0)
                        for sc in range(8):
                            e.wait_ge(a_x[sc % 2].h, 16 * (sc // 2 + 1))
                            hx = (sc % 2) * 4
                            for ti in range(4):
                                k = 4 * sc + ti
                                e.tensor_reduce(ss(k, 0), xc[:, (hx + ti) * D:(hx + ti + 1) * D],
                                                axis=AX.X, op=ALU.add)
                                a_vst.inc(e.tensor_scalar(
                                    out=ss(k, 1), in0=ss(k, 0),
                                    scalar1=-1.0 / D, scalar2=None, op0=ALU.mult), 1)
                                e.wait_ge(a_sr.h, k + 1)
                                a_vst.inc(e.tensor_tensor(
                                    out=ss(k, 2), in0=ss(k, 1), in1=ss(k, 3), op=ALU.mult), 1)
                        for k in range(32):
                            e.wait_ge(a_kev.h, k + 1)
                            if k >= 2:
                                e.wait_ge(a_cnw[k % 2].h, 32 * (k // 2))
                            kbf = (k % 2) * DR
                            kb = (k % 2) * 128
                            cs_ = s_cosf[:, k * DR:(k + 1) * DR]
                            sn_ = s_sinf[:, k * DR:(k + 1) * DR]
                            a_kro.inc(rope6(
                                e,
                                krf[:, kbf + 0:kbf + DR:2], krf[:, kbf + 1:kbf + DR:2],
                                cs_[:, 0:DR:2], cs_[:, 1:DR:2],
                                sn_[:, 0:DR:2], sn_[:, 1:DR:2],
                                krsb[:, kb + 0:kb + DR:2], krsb[:, kb + 1:kb + DR:2],
                                DR // 2), 1)
                    bvector(_v)

                    def _s(e):
                        def atrans(e, sc):
                            e.wait_ge(a_lnw[0].h, 16 * 2 * (sc + 1))
                            e.wait_ge(a_lnw[1].h, 16 * 2 * (sc + 1))
                            if sc >= 1:
                                e.wait_ge(a_tr.h, 256 * sc)
                            if sc >= 2:
                                e.wait_ge(a_kmm.h, 4 * (sc - 1))
                            for dc in range(16):
                                a_tr.inc(e.dma_start(
                                    out=l2t[:, (sc % 2) * 8192 + dc * 512:(sc % 2) * 8192 + (dc + 1) * 512],
                                    in_=ln2_b[sc * 512:(sc + 1) * 512, dc * 128:(dc + 1) * 128],
                                    transpose=True), 16)
                        for sc in range(8):
                            hx = (sc % 2) * 4
                            for ti in range(4):
                                k = 4 * sc + ti
                                e.wait_ge(a_x[sc % 2].h, 16 * (sc // 2 + 1))
                                e.wait_ge(a_vst.h, 2 * k + 1)
                                e.activation(out=junk2[:, :], in_=xc[:, (hx + ti) * D:(hx + ti + 1) * D],
                                             func=AF.Square, bias=ss(k, 1), accum_out=ss(k, 4))
                                e.activation(out=ss(k, 5), in_=ss(k, 4), func=AF.Ln,
                                             scale=1.0 / D, bias=s_eps[:, 0:1])
                                a_sr.inc(e.activation(out=ss(k, 3), in_=ss(k, 5),
                                                      func=AF.Exp, scale=-0.5), 1)
                                e.wait_ge(a_vst.h, 2 * k + 2)
                                if k >= 2:
                                    e.wait_ge(a_lnw[k % 2].h, 16 * (k // 2))
                                a_sev.inc(e.activation(
                                    out=lnsb2[:, (k % 2) * D:(k % 2 + 1) * D],
                                    in_=xc[:, (hx + ti) * D:(hx + ti + 1) * D],
                                    func=AF.Identity, scale=ss(k, 3), bias=ss(k, 2)), 1)
                        atrans(e, 0)
                        atrans(e, 1)
                        for k in range(32):
                            e.wait_ge(a_kmm.h, k + 1)
                            if k >= 2:
                                e.wait_ge(a_cnw[k % 2].h, 32 * (k // 2))
                            kp = (k % 2) * 1024
                            e.activation(out=junk2[:, 0:KVL], in_=psk[:, kp:kp + KVL],
                                         func=AF.Square, accum_out=ss(k, 0))
                            e.activation(out=ss(k, 1), in_=ss(k, 0), func=AF.Ln,
                                         scale=1.0 / KVL, bias=s_eps[:, 1:2])
                            e.activation(out=ss(k, 2), in_=ss(k, 1), func=AF.Exp, scale=-0.5)
                            e.activation(out=cnsb[:, (k % 2) * KVL:(k % 2 + 1) * KVL],
                                         in_=psk[:, kp:kp + KVL],
                                         func=AF.Identity, scale=ss(k, 2))
                            if k >= 2:
                                e.wait_ge(a_kro.h, k - 1)
                            a_kev.inc(e.activation(
                                out=krf[:, (k % 2) * DR:(k % 2 + 1) * DR],
                                in_=psk[:, kp + KVL:kp + KVL + DR], func=AF.Copy), 1)
                            if k % 4 == 3 and k // 4 + 2 < 8:
                                atrans(e, k // 4 + 2)
                    bscalar(_s)

                    def _t(e):
                        e.wait_ge(a_wk.h, 16)
                        for sc in range(8):
                            e.wait_ge(a_tr.h, 16 * 16 * (sc + 1))
                            for ti in range(4):
                                k = 4 * sc + ti
                                if k >= 2:
                                    e.wait_ge(a_kev.h, k - 1)
                                kp = (k % 2) * 1024
                                base = (sc % 2) * 8192
                                for dc in range(16):
                                    lh = l2t[:, base + dc * 512 + ti * 128:base + dc * 512 + (ti + 1) * 128]
                                    e.matmul(psk[:, kp:kp + 512], lh,
                                             wkvab[:, dc * (KVL + DR):dc * (KVL + DR) + 512],
                                             start=(dc == 0), stop=(dc == 15))
                                    ins = e.matmul(psk[:, kp + 512:kp + 576], lh,
                                                   wkvab[:, dc * (KVL + DR) + 512:dc * (KVL + DR) + 576],
                                                   start=(dc == 0), stop=(dc == 15))
                                a_kmm.inc(ins, 1)
                    block.tensor(_t)
                    pop_scope()

                cbc = ExitStack()
                qaT = cbc.enter_context(nc.sbuf_tensor("qaT", [128, 6 * TOWN], BF16))

                with ExitStack() as cb:
                    push_scope(cb)
                    sbb = lambda *a, **kw: cb.enter_context(nc.sbuf_tensor(*a, **kw))
                    lgo = sbb("b_lgo", [128, 2 * D], BF16)
                    l2ot = sbb("b_l2ot", [128, 16 * 1024], BF16, is_transpose_dest=True)
                    wqab = sbb("b_wqab", [128, 16 * QL], BF16)
                    sqaT = sbb("b_sqaT", [128, 6 * TOWN], BF16)
                    qnwb = sbb("b_qnwb", [128, 6], F32)
                    sc1 = sbb("b_sc1", [128, 16], F32)
                    psq = cb.enter_context(nc.psum_tensor("b_psq", [128, 2 * 512], F32))
                    pcs = cb.enter_context(nc.psum_tensor("b_pcs", [128, 16], F32))
                    b_tr = mksem(octx, "b_tr")
                    b_gth = [mkdsem("b_gth0"), mkdsem("b_gth1")]
                    b_gw = [mksem(octx, "b_gw0"), mksem(octx, "b_gw1")]
                    b_wq = mkdsem("b_wq")
                    b_mm = mksem(octx, "b_mm")
                    b_ev = mksem(octx, "b_ev")
                    b_cs = mksem(octx, "b_cs")
                    b_sc = mksem(octx, "b_sc")

                    def _sy(e):
                        for j in range(4):
                            b_tr.inc(e.dma_start(out=cnT[:, j * S:(j + 1) * S],
                                                 in_=cn_b[:, j * 128:(j + 1) * 128],
                                                 transpose=True), 16)
                        b_tr.inc(e.dma_start(out=krT[:, :], in_=kr_b[:, :],
                                             transpose=True), 16)
                        for j in range(16):
                            e.wait_ge(b_gth[j % 2].h, 16 * (j // 2 + 1))
                            b_gw[j % 2].inc(e.dma_start(
                                out=lno_b[j * 128:(j + 1) * 128, :],
                                in_=lgo[:, (j % 2) * D:(j % 2 + 1) * D]), 16)
                        for half in range(2):
                            e.wait_ge(b_gw[0].h, 16 * 8)
                            e.wait_ge(b_gw[1].h, 16 * 8)
                            if half == 1:
                                e.wait_ge(b_mm.h, 12)
                            for dc in range(16):
                                b_tr.inc(e.dma_start(
                                    out=l2ot[:, dc * 1024:(dc + 1) * 1024],
                                    in_=lno_b[half * 1024:(half + 1) * 1024,
                                              dc * 128:(dc + 1) * 128],
                                    transpose=True), 16)
                        e.wait_ge(b_tr.h, 16 * 37)
                    block.sync(_sy)

                    def _g(e):
                        b_wq.inc(e.dma_start(
                            out=wqab[:].rearrange("p (dc f) -> p dc f", f=QL),
                            in_=wqa.rearrange("(dc p) f -> p dc f", p=128)), 16)
                        b_wq.inc(e.dma_start(out=qnwb[:, :], in_=qnw[:, :]), 16)
                        for j in range(16):
                            if j >= 2:
                                e.wait_ge(b_gw[j % 2].h, 16 * (j // 2))
                            b_gth[j % 2].inc(e.indirect_dma_start(
                                out=lgo[:, (j % 2) * D:(j % 2 + 1) * D], out_offset=None,
                                in_=ln2_b[:, :],
                                in_offset=IOffs(ap=s_ridx[:, j:j + 1], axis=0)), 16)
                    block.gpsimd(_g)

                    def _t(e):
                        e.wait_ge(b_wq.h, 32)
                        idx = 0
                        for half in range(2):
                            e.wait_ge(b_tr.h, 16 * (5 + 16 * (half + 1)))
                            for m6 in range(6):
                                for ncx in range(2):
                                    if idx >= 2:
                                        e.wait_ge(b_ev.h, 2 * (idx - 1))
                                    for dc in range(16):
                                        ins = e.matmul(
                                            psq[:, (idx % 2) * 512:(idx % 2 + 1) * 512],
                                            wqab[:, dc * QL + m6 * 128:dc * QL + (m6 + 1) * 128],
                                            l2ot[:, dc * 1024 + ncx * 512:dc * 1024 + (ncx + 1) * 512],
                                            start=(dc == 0), stop=(dc == 15))
                                    b_mm.inc(ins, 1)
                                    idx += 1
                        e.wait_ge(b_ev.h, 48)
                        for sub in range(16):
                            for k6 in range(6):
                                ins = e.matmul(pcs[:, sub:sub + 1],
                                               sqaT[:, k6 * TOWN + sub * 128:k6 * TOWN + (sub + 1) * 128],
                                               s_1b[:, 0:1],
                                               start=(k6 == 0), stop=(k6 == 5))
                        b_cs.inc(ins, 1)
                    block.tensor(_t)

                    def _s(e):
                        idx = 0
                        for half in range(2):
                            for m6 in range(6):
                                for ncx in range(2):
                                    cidx = 2 * half + ncx
                                    e.wait_ge(b_mm.h, idx + 1)
                                    e.activation(
                                        out=qaT[:, m6 * TOWN + cidx * 512:m6 * TOWN + (cidx + 1) * 512],
                                        in_=psq[:, (idx % 2) * 512:(idx % 2 + 1) * 512],
                                        func=AF.Copy)
                                    b_ev.inc(e.activation(
                                        out=sqaT[:, m6 * TOWN + cidx * 512:m6 * TOWN + (cidx + 1) * 512],
                                        in_=psq[:, (idx % 2) * 512:(idx % 2 + 1) * 512],
                                        func=AF.Square), 2)
                                    idx += 1
                        e.wait_ge(b_cs.h, 1)
                        e.activation(out=sc1[:, :], in_=pcs[:, :], func=AF.Ln,
                                     scale=1.0 / QL, bias=s_eps[:, 1:2])
                        b_sc.inc(e.activation(out=scol[:, :], in_=sc1[:, :],
                                              func=AF.Exp, scale=-0.5, bias=s_eps[:, 2:3]), 1)
                    bscalar(_s)

                    def _v(e):
                        e.wait_ge(b_cs.h, 1)
                        e.wait_ge(b_wq.h, 32)
                        for k6 in range(6):
                            ins = e.tensor_scalar(out=qaT[:, k6 * TOWN:(k6 + 1) * TOWN],
                                                  in0=qaT[:, k6 * TOWN:(k6 + 1) * TOWN],
                                                  scalar1=qnwb[:, k6:k6 + 1], scalar2=None,
                                                  op0=ALU.mult)
                        b_fold.inc(ins, 1)
                        e.memset(krT[64:65, :], 1.0)
                    bvector(_v)
                    pop_scope()

                with ExitStack() as cg:
                    push_scope(cg)
                    sbg = lambda *a, **kw: cg.enter_context(nc.sbuf_tensor(*a, **kw))
                    wkvbb = sbg("g_wkvbb", [128, 4 * NH * 256], BF16)
                    kvnwb = sbg("g_kvnwb", [128, 4], F32)
                    wqbg = sbg("g_wqbg", [128, 2 * 6 * 384], BF16)
                    qg = sbg("g_qg", [128, 16 * 384], BF16)
                    qrc = sbg("g_qrc", [128, 16 * 256], BF16)
                    qs2 = sbg("g_qs2", [128, 32], F32)
                    sqq = sbg("g_sqq", [128, 32], F32)
                    cn_ = sbg("g_cn", [128, 32], F32)
                    junk3 = sbg("g_junk", [128, S], BF16)
                    vjunk = sbg("g_vjunk", [128, 192], BF16)
                    qnT = sbg("g_qnT", [128, 2 * TOWN], BF16, is_transpose_dest=True)
                    qrT = sbg("g_qrT", [128, 2 * TOWN], BF16, is_transpose_dest=True)
                    knT = sbg("g_knT", [128, S], BF16)
                    Vh = sbg("g_Vh", [128, 32 * 132], BF16)
                    PT = sbg("g_PT", [128, 2 * 512], BF16)
                    og = sbg("g_og", [128, 2 * 16 * 128], BF16)
                    kq = sbg("g_kq", [128, 1], F32)
                    kr64 = sbg("g_kr64", [128, 1], F32)
                    krs = sbg("g_krs", [1, 1], F32)
                    tta = sbg("g_tta", [1, 1], F32)
                    rks = sbg("g_rks", [1, 8], F32)
                    rkc = sbg("g_rkc", [128, 1], F32)
                    rcp = sbg("g_rcp", [128, 2], F32)
                    PS = cg.enter_context(nc.psum_tensor("g_ps", [128, 4096], F32))
                    def vS(ii):
                        return PS[:, (ii % 2) * 512:(ii % 2) * 512 + 256]

                    def vA(s2):
                        return PS[:, 1024 + s2 * 512:1024 + s2 * 512 + 129]

                    def vQ(sub):
                        return PS[:, 2048 + (sub % 2) * 512:2048 + (sub % 2) * 512 + 384]

                    def vV(vs):
                        return PS[:, 2560 + (vs % 2) * 512:2560 + (vs % 2) * 512 + 128]

                    psn = PS[:, 2048:2560]
                    ptiny = PS[:, 3584:3600]
                    g_wk = mkdsem("g_wk")
                    g_wkf = mksem(octx, "g_wkf")
                    g_wq = [mkdsem("g_wq0"), mkdsem("g_wq1")]
                    g_qmm = mksem(octx, "g_qmm")
                    g_qev = mksem(octx, "g_qev")
                    g_qs = mksem(octx, "g_qs")
                    g_sq = mksem(octx, "g_sq")
                    g_rope = mksem(octx, "g_rope")
                    g_cw = mksem(octx, "g_cw")
                    g_qw = mksem(octx, "g_qw")
                    g_qtr = mksem(octx, "g_qtr")
                    g_kmm = mksem(octx, "g_kmm")
                    g_kev = mksem(octx, "g_kev")
                    g_vev = mksem(octx, "g_vev")
                    g_bc = mksem(octx, "g_bc")
                    g_rk = mksem(octx, "g_rk")
                    g_rkb = mksem(octx, "g_rkb")
                    g_smm = mksem(octx, "g_smm")
                    g_sex = mksem(octx, "g_sex")
                    g_msk = mksem(octx, "g_msk")
                    g_amm = mksem(octx, "g_amm")
                    g_rc = mksem(octx, "g_rc")
                    g_oev = mksem(octx, "g_oev")
                    g_ow = mksem(octx, "g_ow")
                    g_kr0 = mksem(octx, "g_kr0")
                    g_kr1 = mksem(octx, "g_kr1")

                    P_pair, P_amm, P_mskp, P_plast = {}, {}, {}, {}
                    P_kmmK, P_kmmV = {}, {}
                    n_pair = n_msk = n_kmm = n_amm = 0
                    for grp in range(NG):
                        for h2 in range(2):
                            for ncx in range(8):
                                n_kmm += 1
                                P_kmmK[(grp, h2, ncx)] = n_kmm
                            for vs in range(32):
                                n_kmm += 1
                                P_kmmV[(grp, h2, vs)] = n_kmm
                            for g8 in range(8):
                                for jj in range(2 * (g8 + 1)):
                                    n_pair += 1
                                    P_pair[(grp, h2, g8, jj)] = n_pair
                                    if 2 * jj >= 4 * g8:
                                        n_msk += 1
                                        P_mskp[(grp, h2, g8, jj)] = n_msk
                                    for k_ in range(2):
                                        n_amm += 1
                                        P_amm[(grp, h2, g8, 2 * jj + k_)] = n_amm
                                    P_plast[n_pair] = n_amm

                    def P_kev(grp, h2, j):
                        return 9 * (2 * grp + h2) + j + 1

                    def P_vev(grp, h2, vs):
                        return 32 * (2 * grp + h2) + vs + 1

                    def P_oev(grp, h2, g8, s2):
                        return 16 * (2 * grp + h2) + 2 * g8 + s2 + 1

                    def P_rc(grp, h2, g8):
                        return 8 * (2 * grp + h2) + g8 + 1

                    def _g(e):
                        g_wk.inc(e.dma_start(out=kvnwb[:, :], in_=kvnw[:, :]), 16)
                        g_wk.inc(e.dma_start(
                            out=wkvbb[:].rearrange("p (kc f) -> p kc f", f=NH * 256),
                            in_=wkvb.rearrange("(kc p) f -> p kc f", p=128)), 16)
                        wqbr = wqb.rearrange("(k p) f -> p k f", p=128)
                        for grp in range(NG):
                            if grp >= 2:
                                e.wait_ge(g_qmm.h, 16 * (grp - 1))
                            g_wq[grp % 2].inc(e.dma_start(
                                out=wqbg[:, (grp % 2) * 2304:(grp % 2 + 1) * 2304]
                                    .rearrange("p (k f) -> p k f", f=384),
                                in_=wqbr[:, :, grp * 384:(grp + 1) * 384]), 16)
                    block.gpsimd(_g)

                    def _vw(e):
                        e.wait_ge(g_wk.h, 32)
                        for kc in range(4):
                            ins = e.tensor_scalar(
                                out=wkvbb[:, kc * NH * 256:(kc + 1) * NH * 256],
                                in0=wkvbb[:, kc * NH * 256:(kc + 1) * NH * 256],
                                scalar1=kvnwb[:, kc:kc + 1], scalar2=None, op0=ALU.mult)
                        g_wkf.inc(ins, 1)
                    bvector(_vw)

                    def _t(e):
                        e.wait_ge(g_wkf.h, 1)
                        e.wait_ge(g_kr0.h, 1)
                        ins = e.matmul(ptiny[0:1, 3:4], kr64[0:64, 0:1], s_1f[0:64, 0:1],
                                       start=True, stop=True)
                        g_kr1.inc(ins, 1)
                        for grp in range(NG):
                            e.wait_ge(g_wq[grp % 2].h, 16 * (grp // 2 + 1))
                            e.wait_ge(b_fold.h, 1)
                            for sub in range(16):
                                gi = 16 * grp + sub
                                if gi >= 2:
                                    e.wait_ge(g_qev.h, gi - 1)
                                for k6 in range(6):
                                    ins = e.matmul(
                                        vQ(sub),
                                        qaT[:, k6 * TOWN + sub * 128:k6 * TOWN + (sub + 1) * 128],
                                        wqbg[:, (grp % 2) * 2304 + k6 * 384:(grp % 2) * 2304 + (k6 + 1) * 384],
                                        start=(k6 == 0), stop=(k6 == 5))
                                g_qmm.inc(ins, 1)
                            for h2 in range(2):
                                h = 2 * grp + h2
                                if h2 == 0:
                                    e.wait_ge(g_qev.h, 16 * (grp + 1))
                                for ncx in range(8):
                                    e.wait_ge(g_kev.h, 9 * h + ncx)
                                    for kc in range(4):
                                        ins = e.matmul(
                                            psn[:, 0:512],
                                            wkvbb[:, kc * NH * 256 + h * 256:kc * NH * 256 + h * 256 + 128],
                                            cnT[:, kc * S + ncx * 512:kc * S + (ncx + 1) * 512],
                                            start=(kc == 0), stop=(kc == 3))
                                    g_kmm.inc(ins, 1)
                                for vs in range(32):
                                    if vs >= 2:
                                        e.wait_ge(g_vev.h, P_vev(grp, h2, vs - 2))
                                    for kc in range(4):
                                        ins = e.matmul(
                                            vV(vs),
                                            cnT[:, kc * S + vs * 128:kc * S + (vs + 1) * 128],
                                            wkvbb[:, kc * NH * 256 + h * 256 + 128:kc * NH * 256 + (h + 1) * 256],
                                            start=(kc == 0), stop=(kc == 3))
                                    g_kmm.inc(ins, 1)
                                if h2 == 0:
                                    e.wait_ge(g_kev.h, P_kev(grp, 0, 8))
                                    if grp >= 1:
                                        e.wait_ge(g_rkb.h, 2 * grp - 1)
                                    ins = e.matmul(ptiny[0:1, 0:1], kq[:, 0:1], s_1f[:, 0:1],
                                                   start=True, stop=True)
                                    g_bc.inc(ins, 1)
                                    e.wait_ge(g_rk.h, grp + 1)
                                    ins = e.matmul(ptiny[:, 2:3], s_1f[0:1, :], rks[0:1, grp:grp + 1],
                                                   start=True, stop=True)
                                    g_bc.inc(ins, 1)
                                    e.wait_ge(g_qtr.h, 16 * 4 * (grp + 1))
                                e.wait_ge(g_kev.h, 9 * (2 * grp + h2) + 8)
                                e.wait_ge(g_vev.h, 32 * (2 * grp + h2) + 32)
                                for g8 in range(8):
                                    ntt = 4 * (g8 + 1)
                                    qsl = qnT[:, h2 * TOWN + g8 * 256:h2 * TOWN + (g8 + 1) * 256]
                                    qrl = qrT[0:65, h2 * TOWN + g8 * 256:h2 * TOWN + (g8 + 1) * 256]
                                    for jj in range(2 * (g8 + 1)):
                                        pi = P_pair[(grp, h2, g8, jj)]
                                        if pi >= 3:
                                            e.wait_ge(g_sex.h, pi - 2)
                                        for k_ in range(2):
                                            tt = 2 * jj + k_
                                            sl = (pi % 2) * 512 + k_ * 256
                                            e.matmul(PS[:, sl:sl + 256],
                                                     knT[:, tt * 128:(tt + 1) * 128],
                                                     qsl, start=True, stop=False)
                                            ins = e.matmul(PS[:, sl:sl + 256],
                                                           krT[0:65, tt * 128:(tt + 1) * 128],
                                                           qrl, start=False, stop=True)
                                        g_smm.inc(ins, 1)
                                        if 2 * jj >= 4 * g8:
                                            e.wait_ge(g_msk.h, P_mskp[(grp, h2, g8, jj)])
                                        else:
                                            e.wait_ge(g_sex.h, pi)
                                        for k_ in range(2):
                                            tt = 2 * jj + k_
                                            if tt == 0 and (grp, h2, g8) != (0, 0, 0):
                                                pg8, ph2, pgrp = g8 - 1, h2, grp
                                                if pg8 < 0:
                                                    pg8 = 7
                                                    ph2 -= 1
                                                    if ph2 < 0:
                                                        ph2, pgrp = 1, grp - 1
                                                e.wait_ge(g_oev.h, P_oev(pgrp, ph2, pg8, 1))
                                            for s2 in range(2):
                                                ins = e.matmul(
                                                    vA(s2),
                                                    PT[:, (pi % 2) * 512 + k_ * 256 + s2 * 128:(pi % 2) * 512 + k_ * 256 + (s2 + 1) * 128],
                                                    Vh[:, tt * 132:tt * 132 + 129],
                                                    start=(tt == 0), stop=(tt == ntt - 1))
                                            g_amm.inc(ins, 1)
                    block.tensor(_t)

                    def _s(e):
                        e.wait_ge(b_fold.h, 1)
                        g_kr0.inc(e.activation(out=junk3[0:64, 0:S], in_=krT[0:64, :],
                                               func=AF.Square, accum_out=kr64[0:64, 0:1]), 1)

                        def kv_evacs(e, grp, h2):
                            for ncx in range(8):
                                e.wait_ge(g_kmm.h, P_kmmK[(grp, h2, ncx)])
                                g_kev.inc(e.activation(
                                    out=knT[:, ncx * 512:(ncx + 1) * 512],
                                    in_=psn[:, 0:512], func=AF.Copy), 1)
                            g_kev.inc(e.activation(out=junk3[:, 0:S], in_=knT[:, :],
                                                   func=AF.Square, accum_out=kq[:, 0:1]), 1)

                        def exps_oev(e, grp, h2):
                            if grp >= 1:
                                e.wait_ge(g_ow.h, 32 * grp)
                            for g8 in range(8):
                                ntt = 4 * (g8 + 1)
                                for jj in range(2 * (g8 + 1)):
                                    pi = P_pair[(grp, h2, g8, jj)]
                                    e.wait_ge(g_smm.h, pi)
                                    if pi >= 3:
                                        e.wait_ge(g_amm.h, P_plast[pi - 2])
                                    ins = e.raw.activation(
                                        out=PT[:, (pi % 2) * 512:(pi % 2 + 1) * 512],
                                        in_=PS[:, (pi % 2) * 512:(pi % 2 + 1) * 512],
                                        func=AF.Exp)
                                    ins.then_inc(g_sex.h, 1)
                                    g_sex.n += 1
                                e.wait_ge(g_amm.h, P_amm[(grp, h2, g8, ntt - 1)])
                                e.wait_ge(g_rc.h, P_rc(grp, h2, g8))
                                for s2 in range(2):
                                    g_oev.inc(e.activation(
                                        out=og[:, (h2 * 16 + g8 * 2 + s2) * 128:(h2 * 16 + g8 * 2 + s2 + 1) * 128],
                                        in_=vA(s2)[:, 0:128],
                                        func=AF.Identity, scale=rcp[:, s2:s2 + 1]), 1)

                        for grp in range(NG):
                            for sub in range(16):
                                gi = 16 * grp + sub
                                e.wait_ge(g_qmm.h, gi + 1)
                                if grp >= 1 and sub == 0:
                                    e.wait_ge(g_qw.h, 64 * grp)
                                g_qev.inc(e.activation(
                                    out=qg[:, sub * 384:(sub + 1) * 384],
                                    in_=vQ(sub),
                                    func=AF.Identity, scale=scol[:, sub:sub + 1]), 1)
                            kv_evacs(e, grp, 0)
                            e.wait_ge(g_qs.h, grp + 1)
                            g_sq.inc(e.activation(out=sqq[:, :], in_=qs2[:, :], func=AF.Sqrt), 1)
                            e.wait_ge(g_rkb.h, 2 * grp + 1)
                            g_rk.inc(e.activation(out=rks[0:1, grp:grp + 1], in_=tta[0:1, 0:1],
                                                  func=AF.Sqrt, scale=1.0 / S), 1)
                            exps_oev(e, grp, 0)
                            kv_evacs(e, grp, 1)
                            exps_oev(e, grp, 1)
                    bscalar(_s)

                    def _v(e):
                        e.wait_ge(a_c.h, 64)
                        e.wait_ge(c_ld.h, 16 * 6)
                        e.memset(Vh[:, :].rearrange("p (v c) -> p v c", c=132)[:, :, 128:132], 1.0)
                        e.memset(qrc[:, :].rearrange("p (s f) -> p s f", f=128)[:, :, 65:128], 0.0)
                        e.wait_ge(g_kr1.h, 1)
                        e.tensor_copy(out=krs[0:1, 0:1], in_=ptiny[0:1, 3:4])

                        def v_evacs(e, grp, h2):
                            for vs in range(32):
                                e.wait_ge(g_kmm.h, P_kmmV[(grp, h2, vs)])
                                g_vev.inc(e.tensor_copy(
                                    out=Vh[:, vs * 132:vs * 132 + 128],
                                    in_=vV(vs)), 1)

                        def masks_recips(e, grp, h2):
                            for g8 in range(8):
                                ntt = 4 * (g8 + 1)
                                for jj in (2 * g8, 2 * g8 + 1):
                                    pi = P_pair[(grp, h2, g8, jj)]
                                    e.wait_ge(g_sex.h, pi)
                                    o0 = 2 * jj - 4 * g8
                                    ins = e.raw.tensor_tensor(
                                        out=PT[:, (pi % 2) * 512:(pi % 2 + 1) * 512],
                                        in0=PT[:, (pi % 2) * 512:(pi % 2 + 1) * 512],
                                        in1=s_mask[:, o0 * 256:(o0 + 2) * 256],
                                        op=ALU.mult)
                                    ins.then_inc(g_msk.h, 1)
                                    g_msk.n += 1
                                e.wait_ge(g_amm.h, P_amm[(grp, h2, g8, ntt - 1)])
                                e.reciprocal(out=rcp[:, 0:1], in_=vA(0)[:, 128:129])
                                g_rc.inc(e.reciprocal(out=rcp[:, 1:2], in_=vA(1)[:, 128:129]), 1)

                        qv = qg[:, :].rearrange("p (s f) -> p s f", f=384)
                        qrv = qrc[:, :].rearrange("p (s f) -> p s f", f=256)
                        co = s_coso[:, :].rearrange("p (s f) -> p s f", f=DR)
                        si_ = s_sino[:, :].rearrange("p (s f) -> p s f", f=DR)
                        for grp in range(NG):
                            for sub in range(16):
                                e.wait_ge(g_qev.h, 16 * grp + sub + 1)
                                for h2 in range(2):
                                    e.tensor_tensor(
                                        out=vjunk[:, 0:192],
                                        in0=qg[:, sub * 384 + h2 * 192:sub * 384 + (h2 + 1) * 192],
                                        in1=qg[:, sub * 384 + h2 * 192:sub * 384 + (h2 + 1) * 192],
                                        op=ALU.mult)
                                    ins = e.tensor_reduce(
                                        qs2[:, h2 * 16 + sub:h2 * 16 + sub + 1],
                                        vjunk[:, 0:192], axis=AX.X, op=ALU.add)
                            g_qs.inc(ins, 1)
                            if grp >= 1:
                                e.wait_ge(g_qw.h, 64 * grp)
                            for h2 in range(2):
                                g_rope.inc(rope6(
                                    e,
                                    qv[:, :, h2 * 192 + 128 + 0:h2 * 192 + 192:2],
                                    qv[:, :, h2 * 192 + 128 + 1:h2 * 192 + 192:2],
                                    co[:, :, 0:DR:2], co[:, :, 1:DR:2],
                                    si_[:, :, 0:DR:2], si_[:, :, 1:DR:2],
                                    qrv[:, :, h2 * 128 + 0:h2 * 128 + 64:2],
                                    qrv[:, :, h2 * 128 + 1:h2 * 128 + 64:2],
                                    512), 1)
                            v_evacs(e, grp, 0)
                            e.wait_ge(g_bc.h, 2 * grp + 1)
                            e.tensor_copy(out=tta[0:1, 0:1], in_=ptiny[0:1, 0:1])
                            g_rkb.inc(e.tensor_tensor(out=tta[0:1, 0:1], in0=tta[0:1, 0:1],
                                                      in1=krs[0:1, 0:1], op=ALU.add), 1)
                            e.wait_ge(g_bc.h, 2 * grp + 2)
                            g_rkb.inc(e.tensor_scalar(out=rkc[:, 0:1], in0=ptiny[:, 2:3],
                                                      scalar1=-C_COEF, scalar2=None,
                                                      op0=ALU.mult), 1)
                            e.wait_ge(g_sq.h, grp + 1)
                            e.tensor_scalar(out=cn_[:, :], in0=sqq[:, :],
                                            scalar1=rkc[:, 0:1], scalar2=-80.0,
                                            op0=ALU.mult, op1=ALU.max)
                            for h2 in range(2):
                                g_cw.inc(e.tensor_copy(
                                    out=qrv[:, :, h2 * 128 + 64:h2 * 128 + 65],
                                    in_=cn_[:, :].rearrange("p (h s) -> p s h", h=2)[:, :, h2:h2 + 1]), 1)
                            masks_recips(e, grp, 0)
                            v_evacs(e, grp, 1)
                            masks_recips(e, grp, 1)
                    bvector(_v)

                    def _sy(e):
                        for grp in range(NG):
                            e.wait_ge(g_qev.h, 16 * (grp + 1))
                            e.wait_ge(g_rope.h, 2 * (grp + 1))
                            e.wait_ge(g_cw.h, 2 * (grp + 1))
                            if grp >= 1:
                                e.wait_ge(g_amm.h, P_amm[(grp - 1, 1, 7, 31)])
                            qv = qg[:, :].rearrange("p (s f) -> p s f", f=384)
                            qrv = qrc[:, :].rearrange("p (s f) -> p s f", f=256)
                            for h2 in range(2):
                                g_qw.inc(e.dma_start(
                                    out=qn_b[h2, :, :].rearrange("(s p) f -> p s f", p=128),
                                    in_=qv[:, :, h2 * 192:h2 * 192 + 128]), 16)
                                g_qw.inc(e.dma_start(
                                    out=qr_b[h2, :, :].rearrange("(s p) f -> p s f", p=128),
                                    in_=qrv[:, :, h2 * 128:(h2 + 1) * 128]), 16)
                            e.wait_ge(g_qw.h, 64 * (grp + 1))
                            for h2 in range(2):
                                g_qtr.inc(e.dma_start(
                                    out=qnT[:, h2 * TOWN:(h2 + 1) * TOWN],
                                    in_=qn_b[h2, :, :], transpose=True), 16)
                                g_qtr.inc(e.dma_start(
                                    out=qrT[:, h2 * TOWN:(h2 + 1) * TOWN],
                                    in_=qr_b[h2, :, :], transpose=True), 16)
                            for h2 in range(2):
                                e.wait_ge(g_oev.h, P_oev(grp, h2, 7, 1))
                                g_ow.inc(e.dma_start(
                                    out=o_b[2 * grp + h2, :, :].rearrange("(s p) f -> p s f", p=128),
                                    in_=og[:, :].rearrange("p (hs f) -> p hs f", f=128)[:, h2 * 16:(h2 + 1) * 16, :]), 16)
                        e.wait_ge(g_ow.h, 16 * 2 * NG)
                    block.sync(_sy)
                    pop_scope()

                cbc.close()

                with ExitStack() as cd:
                    push_scope(cd)
                    sbd = lambda *a, **kw: cd.enter_context(nc.sbuf_tensor(*a, **kw))
                    oT = sbd("d_oT", [128, 16 * TOWN], BF16, is_transpose_dest=True)
                    wob = sbd("d_wob", [128, 16 * 1024], BF16)
                    xg = sbd("d_xg", [128, 2 * D], F32)
                    x2t = sbd("d_x2t", [128, 2 * 1024], F32)
                    pso = cd.enter_context(nc.psum_tensor("d_pso", [128, 2 * 1024], F32))
                    d_tr = mksem(octx, "d_tr")
                    d_wo = mkdsem("d_wo")
                    d_gx = [mkdsem("d_gx0"), mkdsem("d_gx1")]
                    d_mm = mksem(octx, "d_mm")
                    d_ad = mksem(octx, "d_ad")
                    d_xw = [mksem(octx, "d_xw0"), mksem(octx, "d_xw1")]

                    def _sy(e):
                        for hd in range(16):
                            d_tr.inc(e.dma_start(
                                out=oT[:, hd * TOWN:(hd + 1) * TOWN],
                                in_=o_b[hd, :, :], transpose=True), 16)
                        for dh in range(2):
                            for sub in range(16):
                                k = 16 * dh + sub
                                e.wait_ge(d_ad.h, k + 1)
                                d_xw[k % 2].inc(e.dma_start(
                                    out=x2s[sub * 128:(sub + 1) * 128, dh * 1024:(dh + 1) * 1024],
                                    in_=x2t[:, (k % 2) * 1024:(k % 2 + 1) * 1024]), 16)
                        e.wait_ge(d_xw[0].h, 16 * 16)
                        e.wait_ge(d_xw[1].h, 16 * 16)
                    block.sync(_sy)

                    def _g(e):
                        wor = wo.rearrange("(hd p) d -> p hd d", p=128)
                        for dh in range(2):
                            if dh == 1:
                                e.wait_ge(d_mm.h, 16)
                            d_wo.inc(e.dma_start(
                                out=wob[:].rearrange("p (hd f) -> p hd f", f=1024),
                                in_=wor[:, :, dh * 1024:(dh + 1) * 1024]), 16)
                            for sub in range(16):
                                k = 16 * dh + sub
                                if k >= 2:
                                    e.wait_ge(d_ad.h, k - 1)
                                d_gx[k % 2].inc(e.indirect_dma_start(
                                    out=xg[:, (k % 2) * D:(k % 2 + 1) * D], out_offset=None,
                                    in_=x1s[:, :],
                                    in_offset=IOffs(ap=s_ridx[:, sub:sub + 1], axis=0)), 16)
                    block.gpsimd(_g)

                    def _t(e):
                        e.wait_ge(d_tr.h, 16 * 16)
                        for dh in range(2):
                            e.wait_ge(d_wo.h, 16 * (dh + 1))
                            for sub in range(16):
                                k = 16 * dh + sub
                                if k >= 2:
                                    e.wait_ge(d_ad.h, k - 1)
                                for hd in range(16):
                                    for ncx in range(2):
                                        ins = e.matmul(
                                            pso[:, (k % 2) * 1024 + ncx * 512:(k % 2) * 1024 + (ncx + 1) * 512],
                                            oT[:, hd * TOWN + sub * 128:hd * TOWN + (sub + 1) * 128],
                                            wob[:, hd * 1024 + ncx * 512:hd * 1024 + (ncx + 1) * 512],
                                            start=(hd == 0), stop=(hd == 15))
                                d_mm.inc(ins, 1)
                    block.tensor(_t)

                    def _v(e):
                        for dh in range(2):
                            for sub in range(16):
                                k = 16 * dh + sub
                                e.wait_ge(d_mm.h, k + 1)
                                e.wait_ge(d_gx[k % 2].h, 16 * (k // 2 + 1))
                                if k >= 2:
                                    e.wait_ge(d_xw[k % 2].h, 16 * (k // 2))
                                d_ad.inc(e.tensor_tensor(
                                    out=x2t[:, (k % 2) * 1024:(k % 2 + 1) * 1024],
                                    in0=pso[:, (k % 2) * 1024:(k % 2 + 1) * 1024],
                                    in1=xg[:, (k % 2) * D + dh * 1024:(k % 2) * D + (dh + 1) * 1024],
                                    op=ALU.add), 1)
                    bvector(_v)
                    pop_scope()

        if 3 in stages:
            emit_ffn(x2s, out_t, TOWN)

    return nc


def _host_consts():
    inv = 1.0 / (10000.0 ** (np.arange(0, DR, 2, dtype=np.float32) / DR))
    pos = np.arange(S, dtype=np.float32)
    freqs = np.outer(pos, inv).astype(np.float32)
    emb = np.concatenate([freqs, freqs], axis=-1)
    cosf = np.cos(emb).astype(np.float32)
    sinf = np.sin(emb).astype(np.float32)
    H64 = _hadamard(64)
    H128 = _hadamard(128)
    HA = np.kron(np.eye(2), H64)
    hc = np.zeros((128, 256), np.float32)
    hc[:, 0:128] = HA
    hc[:, 128:256] = H128
    hconst = hc.astype(ml_dtypes.bfloat16)
    onesf = np.ones((128, 128), np.float32)
    onesb = np.ones((128, 1), np.float32).astype(ml_dtypes.bfloat16)
    return cosf, sinf, hconst, onesf, onesb


def make_in_maps(inputs):
    x = np.ascontiguousarray(np.asarray(inputs["L0_attn_x"], dtype=np.float32))
    W1 = np.asarray(inputs["L0_ffn_W1"], dtype=np.float32)
    W2 = np.asarray(inputs["L0_ffn_W2"], dtype=np.float32)
    W3 = np.asarray(inputs["L0_ffn_W3"], dtype=np.float32)

    w1tg = np.ascontiguousarray(W1.reshape(128, 64, 16, 128).transpose(1, 2, 3, 0))
    w3tg = np.ascontiguousarray(W3.reshape(128, 64, 16, 128).transpose(1, 2, 3, 0))
    w2t = np.ascontiguousarray(W2.T)

    cosf, sinf, hconst, onesf, onesb = _host_consts()

    base = {
        "w1tg": w1tg, "w3tg": w3tg, "w2t": w2t,
        "wqa": np.ascontiguousarray(inputs["L0_attn_w_qa"], dtype=np.float32),
        "wqb": np.ascontiguousarray(inputs["L0_attn_w_qb"], dtype=np.float32),
        "wkva": np.ascontiguousarray(inputs["L0_attn_w_kva"], dtype=np.float32),
        "wkvb": np.ascontiguousarray(inputs["L0_attn_w_kvb"], dtype=np.float32),
        "wo": np.ascontiguousarray(inputs["L0_attn_w_o"], dtype=np.float32),
        "qnw": np.ascontiguousarray(np.asarray(inputs["L0_attn_q_a_norm_w"], np.float32).reshape(6, 128).T),
        "kvnw": np.ascontiguousarray(np.asarray(inputs["L0_attn_kv_a_norm_w"], np.float32).reshape(4, 128).T),
        "cos_f": cosf, "sin_f": sinf,
        "hconst": hconst, "ones_f": onesf, "ones_b": onesb,
        "epsc": np.broadcast_to(
            np.array([LNEPS, RMSEPS, QSC_BIAS, 0.0], np.float32), (128, 4)).copy(),
    }

    in_maps = []
    own_rows_all = []
    for c in range(8):
        b, half = c // 2, c % 2
        own = np.concatenate([np.arange(g * 512 + half * 256, g * 512 + half * 256 + 256)
                              for g in range(8)]).astype(np.int64)
        own_rows_all.append(own)
        rix = own.reshape(16, 128).T.astype(np.uint32)
        mask = np.zeros((128, 4, 256), np.float32)
        ti = np.arange(128)[:, None]
        qi = np.arange(256)[None, :]
        for o in range(4):
            mask[:, o, :] = (o * 128 + ti - half * 256 <= qi).astype(np.float32)
        im = dict(base)
        im["x"] = np.ascontiguousarray(x[b])
        im["cos_o"] = np.ascontiguousarray(cosf[own])
        im["sin_o"] = np.ascontiguousarray(sinf[own])
        im["maskc"] = np.ascontiguousarray(mask.reshape(128, 4 * 256).astype(ml_dtypes.bfloat16))
        im["ridx"] = np.ascontiguousarray(rix)
        in_maps.append(im)
    return in_maps, own_rows_all


def kernel(**inputs):
    in_maps, own_rows_all = make_in_maps(inputs)
    nc = build_nc()
    res = run_bass_kernel_spmd(nc, in_maps, core_ids=list(range(8)))
    out = np.empty((BATCH, S, D), np.float32)
    for c in range(8):
        b = c // 2
        out[b, own_rows_all[c]] = np.asarray(res.results[c]["out"], np.float32)
    return out


# version 08
# speedup vs baseline: 1.3154x; 1.0947x over previous
import math
from contextlib import ExitStack

import numpy as np
import ml_dtypes
import concourse.bass as bass
import concourse.mybir as mybir
import bass_rust
from concourse.bass_utils import run_bass_kernel_spmd

F32 = mybir.dt.float32
BF16 = mybir.dt.bfloat16
U32 = mybir.dt.uint32
AF = mybir.ActivationFunctionType
ALU = mybir.AluOpType
AX = mybir.AxisListType
IOffs = bass_rust.IndirectOffsetOnAxis

BATCH, S, D, HID, NH = 4, 4096, 2048, 8192, 16
DN, DR, DV, DQK = 128, 64, 128, 192
QL, KVL = 768, 512
TOWN = 2048
LNEPS = 1e-5
RMSEPS = 1e-6
W2SCALE = 0.5 / math.sqrt(HID)
QSC_BIAS = math.log(1.0 / math.sqrt(DQK))
C_COEF = 8.0 / math.sqrt(DQK)
NG = 8
NTT_G = sum(4 * (g8 + 1) for g8 in range(8))


def _hadamard(n):
    h = np.array([[1.0]])
    while h.shape[0] < n:
        h = np.block([[h, h], [h, -h]])
    return h


class Sem:
    def __init__(self, ctx, nc, name):
        self.h = ctx.enter_context(nc.semaphore(name))
        self.n = 0

    def inc(self, instr, k=1):
        instr.then_inc(self.h, k)
        self.n += k
        return self.n


_WRAP_OPS = frozenset((
    "activation", "copy", "add", "mul", "tensor_tensor", "tensor_scalar",
    "tensor_copy", "tensor_reduce", "reduce_sum", "reduce_max", "reciprocal",
    "memset", "tensor_tensor_reduce", "select", "transpose", "iota",
))


class _Shim:
    def __init__(self, e):
        self._e = e

    def then_inc(self, h, k):
        self._e.sem_inc(h, k)
        return self


class SafeEng:

    def __init__(self, e, prog):
        self._e = e
        self._p = prog

    @property
    def raw(self):
        return self._e

    def __getattr__(self, nm):
        a = getattr(self._e, nm)
        if nm in _WRAP_OPS:
            def wrap(*args, **kw):
                ins = a(*args, **kw)
                self._p.n += 1
                ins.then_inc(self._p.h, 1)
                self._e.wait_ge(self._p.h, self._p.n)
                return _Shim(self._e)
            return wrap
        return a


def build_nc(stages=(0, 1, 2, 3), debug=False):
    nc = bass.Bass("TRN2", target_bir_lowering=False)

    x_in = nc.dram_tensor("x", [S, D], F32, kind="ExternalInput").ap()
    w1tg = nc.dram_tensor("w1tg", [64, 16, 128, 128], F32, kind="ExternalInput").ap()
    w3tg = nc.dram_tensor("w3tg", [64, 16, 128, 128], F32, kind="ExternalInput").ap()
    w2t = nc.dram_tensor("w2t", [HID, D], F32, kind="ExternalInput").ap()
    wqa = nc.dram_tensor("wqa", [D, QL], F32, kind="ExternalInput").ap()
    wqb = nc.dram_tensor("wqb", [QL, NH * DQK], F32, kind="ExternalInput").ap()
    wkva = nc.dram_tensor("wkva", [D, KVL + DR], F32, kind="ExternalInput").ap()
    wkvb = nc.dram_tensor("wkvb", [KVL, NH * 256], F32, kind="ExternalInput").ap()
    wo = nc.dram_tensor("wo", [NH * DV, D], F32, kind="ExternalInput").ap()
    qnw = nc.dram_tensor("qnw", [128, 6], F32, kind="ExternalInput").ap()
    kvnw = nc.dram_tensor("kvnw", [128, 4], F32, kind="ExternalInput").ap()
    cos_f = nc.dram_tensor("cos_f", [S, DR], F32, kind="ExternalInput").ap()
    sin_f = nc.dram_tensor("sin_f", [S, DR], F32, kind="ExternalInput").ap()
    cos_o = nc.dram_tensor("cos_o", [TOWN, DR], F32, kind="ExternalInput").ap()
    sin_o = nc.dram_tensor("sin_o", [TOWN, DR], F32, kind="ExternalInput").ap()
    hconst = nc.dram_tensor("hconst", [128, 256], BF16, kind="ExternalInput").ap()
    ones_f = nc.dram_tensor("ones_f", [128, 128], F32, kind="ExternalInput").ap()
    ones_b = nc.dram_tensor("ones_b", [128, 1], BF16, kind="ExternalInput").ap()
    maskc = nc.dram_tensor("maskc", [128, 4 * 256], BF16, kind="ExternalInput").ap()
    epsc = nc.dram_tensor("epsc", [128, 4], F32, kind="ExternalInput").ap()
    ridx = nc.dram_tensor("ridx", [128, 16], U32, kind="ExternalInput").ap()

    out_t = nc.dram_tensor("out", [TOWN, D], F32, kind="ExternalOutput").ap()

    dbg = "ExternalOutput" if debug else "Internal"
    zsc = nc.dram_tensor("zsc", [HID, D], BF16, kind="Internal").ap()
    wf = nc.dram_tensor("wfs", [64, 128, D], BF16, kind=dbg).ap()
    x1s = nc.dram_tensor("x1s", [S, D], F32, kind=dbg).ap()
    x2s = nc.dram_tensor("x2s", [TOWN, D], F32, kind=dbg).ap()
    lnt_b = nc.dram_tensor("lnt_b", [1024, D], BF16, kind="Internal").ap()
    ln2_b = nc.dram_tensor("ln2_b", [S, D], BF16, kind="Internal").ap()
    lno_b = nc.dram_tensor("lno_b", [TOWN, D], BF16, kind="Internal").ap()
    cn_b = nc.dram_tensor("cn_b", [S, KVL], BF16, kind="Internal").ap()
    kr_b = nc.dram_tensor("kr_b", [S, 128], BF16, kind="Internal").ap()
    qn_b = nc.dram_tensor("qn_b", [2, TOWN, 128], BF16, kind="Internal").ap()
    qr_b = nc.dram_tensor("qr_b", [2, TOWN, 128], BF16, kind="Internal").ap()
    o_b = nc.dram_tensor("o_b", [NH, TOWN, DV], BF16, kind="Internal").ap()

    octx = ExitStack()
    with octx:
        osb = lambda *a, **kw: octx.enter_context(nc.sbuf_tensor(*a, **kw))
        s_hc = osb("s_hc", [128, 256], BF16)
        s_1f = osb("s_1f", [128, 128], F32)
        s_1b = osb("s_1b", [128, 1], BF16)
        s_mask = osb("s_mask", [128, 4 * 256], BF16)
        s_ridx = osb("s_ridx", [128, 16], U32)
        s_eps = osb("s_eps", [128, 4], F32)

        ph = Sem(octx, nc, "ph")
        block = octx.enter_context(nc.Block())
        prog_v = Sem(octx, nc, "prog_v")
        prog_s = Sem(octx, nc, "prog_s")
        def bvector(fn):
            block.vector(lambda e: fn(SafeEng(e, prog_v)))

        def bscalar(fn):
            block.scalar(lambda e: fn(SafeEng(e, prog_s)))

        sem_id = [0]

        def push_scope(ctx_):
            pass

        def mksem(_ctx, label=None):
            sem_id[0] += 1
            return Sem(octx, nc, label or ("s%d" % sem_id[0]))

        mkdsem = lambda label: mksem(None, label)

        def pop_scope():
            barrier()

        def barrier():
            tgt = ph.n + 5
            ph.n = tgt
            for name in ("sync", "gpsimd", "tensor", "vector", "scalar"):
                def _f(e, tgt=tgt):
                    e.sem_inc(ph.h, 1)
                    e.wait_ge(ph.h, tgt)
                getattr(block, name)(_f)

        push_scope(octx)
        c_ld = Sem(octx, nc, "c_ld")

        def _consts(e):
            for dst, src in ((s_hc, hconst), (s_1f, ones_f), (s_1b, ones_b),
                             (s_mask, maskc), (s_ridx, ridx), (s_eps, epsc)):
                c_ld.inc(e.dma_start(out=dst[:], in_=src[:, :]), 16)
        block.sync(_consts)

        if 0 in stages:
            with ExitStack() as c0:
                push_scope(c0)
                sb0 = lambda *a, **kw: c0.enter_context(nc.sbuf_tensor(*a, **kw))
                s_w2 = sb0("s_w2", [128, 2 * D], BF16)
                s_z = sb0("s_z", [128, 2 * D], BF16)
                s_zs = sb0("s_zs", [128, 2 * D], BF16)
                s_wfo = sb0("s_wfo", [128, 2 * D], BF16)
                pA = c0.enter_context(nc.psum_tensor("p0a", [128, D], F32))
                pB = c0.enter_context(nc.psum_tensor("p0b", [128, D], F32))
                w_ld = [mkdsem("w_ld0"), mkdsem("w_ld1")]
                w_mm = mksem(octx, "w_mm")
                w_ev = mksem(octx, "w_ev")
                w_st = [mksem(octx, "w_st0"), mksem(octx, "w_st1")]
                w_fst = [mksem(octx, "w_fst0"), mksem(octx, "w_fst1")]
                w_zl = [mksem(octx, "w_zl0"), mksem(octx, "w_zl1")]

                def _g(e):
                    for hc in range(64):
                        if hc >= 2:
                            e.wait_ge(w_mm.h, hc - 1)
                        w_ld[hc % 2].inc(e.dma_start(
                            out=s_w2[:, (hc % 2) * D:(hc % 2 + 1) * D],
                            in_=w2t[hc * 128:(hc + 1) * 128, :]), 16)
                block.gpsimd(_g)

                def _t(e):
                    e.wait_ge(c_ld.h, 16 * 6)
                    for hc in range(64):
                        e.wait_ge(w_ld[hc % 2].h, 16 * (hc // 2 + 1))
                        if hc >= 2:
                            e.wait_ge(w_ev.h, hc - 1)
                        pp = pA if hc % 2 == 0 else pB
                        for ncx in range(4):
                            ins = e.matmul(
                                pp[:, ncx * 512:(ncx + 1) * 512],
                                s_hc[:, 0:128],
                                s_w2[:, (hc % 2) * D + ncx * 512:(hc % 2) * D + (ncx + 1) * 512],
                                start=True, stop=True)
                        w_mm.inc(ins, 1)
                    for si in range(64):
                        e.wait_ge(w_zl[si % 2].h, 16 * (si // 2 + 1))
                        if si >= 2:
                            e.wait_ge(w_ev.h, 64 + si - 1)
                        pp = pA if si % 2 == 0 else pB
                        for ncx in range(4):
                            ins = e.matmul(
                                pp[:, ncx * 512:(ncx + 1) * 512],
                                s_hc[:, 128:256],
                                s_zs[:, (si % 2) * D + ncx * 512:(si % 2) * D + (ncx + 1) * 512],
                                start=True, stop=True)
                        w_mm.inc(ins, 1)
                block.tensor(_t)

                def _s(e):
                    for hc in range(64):
                        e.wait_ge(w_mm.h, hc + 1)
                        if hc >= 2:
                            e.wait_ge(w_st[hc % 2].h, 16 * (hc // 2))
                        pp = pA if hc % 2 == 0 else pB
                        w_ev.inc(e.activation(
                            out=s_z[:, (hc % 2) * D:(hc % 2 + 1) * D],
                            in_=pp[:, :], func=AF.Copy), 1)
                        w_st[hc % 2].inc(e.dma_start(
                            out=zsc[hc * 128:(hc + 1) * 128, :],
                            in_=s_z[:, (hc % 2) * D:(hc % 2 + 1) * D]), 16)
                    for si in range(64):
                        e.wait_ge(w_mm.h, 64 + si + 1)
                        if si >= 2:
                            e.wait_ge(w_fst[si % 2].h, 16 * (si // 2))
                        pp = pA if si % 2 == 0 else pB
                        w_ev.inc(e.activation(
                            out=s_wfo[:, (si % 2) * D:(si % 2 + 1) * D],
                            in_=pp[:, :], func=AF.Identity, scale=W2SCALE), 1)
                        w_fst[si % 2].inc(e.dma_start(
                            out=wf[si, :, :],
                            in_=s_wfo[:, (si % 2) * D:(si % 2 + 1) * D]), 16)
                bscalar(_s)

                def _sy(e):
                    zr = zsc.rearrange("(j s) d -> s j d", s=64)
                    for si in range(64):
                        if si == 0:
                            e.wait_ge(w_st[0].h, 16 * 32)
                            e.wait_ge(w_st[1].h, 16 * 32)
                        if si >= 2:
                            e.wait_ge(w_mm.h, 64 + si - 1)
                        w_zl[si % 2].inc(e.dma_start(
                            out=s_zs[:, (si % 2) * D:(si % 2 + 1) * D],
                            in_=zr[si, :, :]), 16)
                    e.wait_ge(w_fst[0].h, 16 * 32)
                    e.wait_ge(w_fst[1].h, 16 * 32)
                block.sync(_sy)
                pop_scope()

        FFN_ID = [0]
        FFN_SEMS = {}

        def emit_ffn(x_src, x_dst, T):
            npass = T // 1024
            FFN_ID[0] += 1
            tag = "f%d_" % FFN_ID[0]
            with ExitStack() as cf:
                push_scope(cf)
                sbf = lambda nm, *a, **kw: cf.enter_context(nc.sbuf_tensor(tag + nm, *a, **kw))
                acc = sbf("acc", [128, 8 * D], F32)
                l1t = sbf("l1t", [128, 16 * 1024], BF16, is_transpose_dest=True)
                hT = sbf("hT", [128, 8 * 1024], BF16)
                wfg = sbf("wfg", [128, 8 * D], BF16)
                sW1 = sbf("w1", [128, 2 * 2048], BF16)
                sW3 = sbf("w3", [128, 2 * 2048], BF16)
                sg = sbf("sg", [128, 2 * 1024], BF16)
                junk = sbf("junk", [128, D], BF16)
                lnsb = sbf("lnsb", [128, 2 * D], BF16)
                xl = sbf("xl", [128, 2 * D], F32)
                stat = sbf("stat", [128, 8 * 6], F32)
                psg = cf.enter_context(nc.psum_tensor(tag + "psg", [128, 1024], F32))
                psl = cf.enter_context(nc.psum_tensor(tag + "psl", [128, 1024], F32))
                psda = cf.enter_context(nc.psum_tensor(tag + "psda", [128, 1024], F32))
                psdb = cf.enter_context(nc.psum_tensor(tag + "psdb", [128, 1024], F32))
                psdh = [psda, psdb]
                if not FFN_SEMS:
                    FFN_SEMS["pb"] = 0
                    for nm_ in ("f_x", "f_tr", "f_wf", "f_mmu", "f_mmd", "f_se",
                                "f_vm", "f_da", "f_acw"):
                        FFN_SEMS[nm_] = mksem(octx, nm_)
                    FFN_SEMS["f_lnw"] = [mksem(octx, "f_lnw0"), mksem(octx, "f_lnw1")]
                    FFN_SEMS["f_xl"] = [mksem(octx, "f_xl0"), mksem(octx, "f_xl1")]
                    FFN_SEMS["f_wl"] = [mkdsem("f_wl0"), mkdsem("f_wl1")]
                pb = FFN_SEMS["pb"]
                FFN_SEMS["pb"] = pb + npass
                f_x = FFN_SEMS["f_x"]
                f_lnw = FFN_SEMS["f_lnw"]
                f_tr = FFN_SEMS["f_tr"]
                f_wf = FFN_SEMS["f_wf"]
                f_wl = FFN_SEMS["f_wl"]
                f_mmu = FFN_SEMS["f_mmu"]
                f_mmd = FFN_SEMS["f_mmd"]
                f_se = FFN_SEMS["f_se"]
                f_vm = FFN_SEMS["f_vm"]
                f_da = FFN_SEMS["f_da"]
                f_xl = FFN_SEMS["f_xl"]
                f_acw = FFN_SEMS["f_acw"]

                accv = acc[:, :].rearrange("p (t d) -> p t d", d=D)

                def stt(ti, j):
                    return stat[:, ti * 6 + j:ti * 6 + j + 1]

                def _sy(e):
                    for p0 in range(npass):
                        p = pb + p0
                        if p > 0:
                            e.wait_ge(f_da.h, 128 * p)
                            e.wait_ge(f_acw.h, 16 * p)
                        f_x.inc(e.dma_start(
                            out=accv[:, :, :],
                            in_=x_src[p0 * 1024:(p0 + 1) * 1024, :]
                                .rearrange("(t q) d -> q t d", q=128)), 16)
                        for u in range(64):
                            g = u // 8
                            e.wait_ge(f_mmd.h, 128 * p + 16 * g)
                            f_wf.inc(e.dma_start(
                                out=wfg[:, (u % 8) * D:(u % 8 + 1) * D],
                                in_=wf[u, :, :]), 16)
                        e.wait_ge(f_da.h, 128 * (p + 1))
                        f_acw.inc(e.dma_start(
                            out=x_dst[p0 * 1024:(p0 + 1) * 1024, :]
                                .rearrange("(t q) d -> q t d", q=128),
                            in_=accv[:, :, :]), 16)
                    e.wait_ge(f_acw.h, 16 * (pb + npass))
                block.sync(_sy)

                def _g(e):
                    for p0 in range(npass):
                        p = pb + p0
                        for u in range(64):
                            uu = 64 * p + u
                            if uu >= 2:
                                e.wait_ge(f_mmu.h, 2 * uu - 2)
                            sl = u % 2
                            f_wl[sl].inc(e.dma_start(
                                out=sW1[:, sl * 2048:(sl + 1) * 2048]
                                    .rearrange("p (dc j) -> p dc j", j=128),
                                in_=w1tg[u, :, :, :].rearrange("dc p j -> p dc j")), 16)
                            f_wl[sl].inc(e.dma_start(
                                out=sW3[:, sl * 2048:(sl + 1) * 2048]
                                    .rearrange("p (dc j) -> p dc j", j=128),
                                in_=w3tg[u, :, :, :].rearrange("dc p j -> p dc j")), 16)
                block.gpsimd(_g)

                def _t(e):
                    for p0 in range(npass):
                        p = pb + p0
                        e.wait_ge(f_tr.h, 16 * 16 * (p + 1))
                        for g in range(8):
                            for ul in range(8):
                                u = 8 * g + ul
                                uu = 64 * p + u
                                e.wait_ge(f_wl[uu % 2].h, 32 * (uu // 2 + 1))
                                e.wait_ge(f_se.h, 64 * p + u)
                                sl = u % 2
                                for dc in range(16):
                                    for ncx in range(2):
                                        ins = e.matmul(
                                            psg[:, ncx * 512:(ncx + 1) * 512],
                                            sW1[:, sl * 2048 + dc * 128:sl * 2048 + (dc + 1) * 128],
                                            l1t[:, dc * 1024 + ncx * 512:dc * 1024 + (ncx + 1) * 512],
                                            start=(dc == 0), stop=(dc == 15))
                                f_mmu.inc(ins, 1)
                                e.wait_ge(f_vm.h, 64 * p + u)
                                for dc in range(16):
                                    for ncx in range(2):
                                        ins = e.matmul(
                                            psl[:, ncx * 512:(ncx + 1) * 512],
                                            sW3[:, sl * 2048 + dc * 128:sl * 2048 + (dc + 1) * 128],
                                            l1t[:, dc * 1024 + ncx * 512:dc * 1024 + (ncx + 1) * 512],
                                            start=(dc == 0), stop=(dc == 15))
                                f_mmu.inc(ins, 1)
                            e.wait_ge(f_vm.h, 64 * p + 8 * (g + 1))
                            e.wait_ge(f_wf.h, 16 * (64 * p + 8 * (g + 1)))
                            for ti in range(8):
                                for hf in range(2):
                                    wv = 128 * p + 16 * g + 2 * ti - 1 + hf
                                    if wv > 0:
                                        e.wait_ge(f_da.h, wv)
                                    for ul2 in range(8):
                                        for nc2 in range(2):
                                            ins = e.matmul(
                                                psdh[hf][:, nc2 * 512:(nc2 + 1) * 512],
                                                hT[:, ul2 * 1024 + ti * 128:ul2 * 1024 + (ti + 1) * 128],
                                                wfg[:, ul2 * D + (2 * hf + nc2) * 512:ul2 * D + (2 * hf + nc2 + 1) * 512],
                                                start=(ul2 == 0), stop=(ul2 == 7))
                                    f_mmd.inc(ins, 1)
                block.tensor(_t)

                def ln_tile(e, q, ti):
                    def xload(ti):
                        kk = 8 * q + ti
                        f_xl[kk % 2].inc(e.dma_start(
                            out=xl[:, (kk % 2) * D:(kk % 2 + 1) * D],
                            in_=x_src[(q - pb) * 1024 + ti * 128:
                                      (q - pb) * 1024 + (ti + 1) * 128, :]), 16)
                    if ti == 0:
                        xload(0)
                        xload(1)
                    if True:
                        kk = 8 * q + ti
                        sl = (kk % 2) * D
                        e.wait_ge(f_xl[kk % 2].h, 16 * (kk // 2 + 1))
                        e.activation(out=junk[:, :], in_=xl[:, sl:sl + D],
                                     func=AF.Identity, accum_out=stt(ti, 0))
                        e.activation(out=stt(ti, 1), in_=stt(ti, 0),
                                     func=AF.Identity, scale=-1.0 / D)
                        e.activation(out=junk[:, :], in_=xl[:, sl:sl + D],
                                     func=AF.Square, bias=stt(ti, 1),
                                     accum_out=stt(ti, 4))
                        e.activation(out=stt(ti, 5), in_=stt(ti, 4),
                                     func=AF.Ln, scale=1.0 / D, bias=s_eps[:, 0:1])
                        e.activation(out=stt(ti, 3), in_=stt(ti, 5),
                                     func=AF.Exp, scale=-0.5)
                        e.activation(out=stt(ti, 2), in_=stt(ti, 1),
                                     func=AF.Identity, scale=stt(ti, 3))
                        if kk >= 2:
                            e.wait_ge(f_lnw[kk % 2].h, 16 * (kk // 2))
                        e.activation(
                            out=lnsb[:, sl:sl + D],
                            in_=xl[:, sl:sl + D], func=AF.Identity,
                            scale=stt(ti, 3), bias=stt(ti, 2))
                        f_lnw[kk % 2].inc(e.dma_start(
                            out=lnt_b[ti * 128:(ti + 1) * 128, :],
                            in_=lnsb[:, sl:sl + D]), 16)
                        if ti + 2 < 8:
                            xload(ti + 2)

                def ln_trans(e, q):
                    e.wait_ge(f_lnw[0].h, 64 * (q + 1))
                    e.wait_ge(f_lnw[1].h, 64 * (q + 1))
                    if q > 0:
                        e.wait_ge(f_mmu.h, 128 * q)
                    for dc in range(16):
                        f_tr.inc(e.dma_start(
                            out=l1t[:, dc * 1024:(dc + 1) * 1024],
                            in_=lnt_b[:, dc * 128:(dc + 1) * 128],
                            transpose=True), 16)

                def _s(e):
                    for ti in range(8):
                        ln_tile(e, pb, ti)
                    ln_trans(e, pb)
                    for p0 in range(npass):
                        p = pb + p0
                        for u in range(64):
                            uu = 64 * p + u
                            e.wait_ge(f_mmu.h, 2 * uu + 1)
                            if uu >= 2:
                                e.wait_ge(f_vm.h, uu - 1)
                            f_se.inc(e.activation(
                                out=sg[:, (u % 2) * 1024:(u % 2 + 1) * 1024],
                                in_=psg[:, :], func=AF.Silu), 1)
                            if p0 + 1 < npass and u % 8 == 7:
                                ln_tile(e, p + 1, u // 8)
                        if p0 + 1 < npass:
                            ln_trans(e, p + 1)
                bscalar(_s)

                def _v(e):
                    for p0 in range(npass):
                        p = pb + p0
                        for u in range(64):
                            uu = 64 * p + u
                            e.wait_ge(f_se.h, uu + 1)
                            e.wait_ge(f_mmu.h, 2 * uu + 2)
                            f_vm.inc(e.tensor_tensor(
                                out=hT[:, (u % 8) * 1024:(u % 8 + 1) * 1024],
                                in0=sg[:, (u % 2) * 1024:(u % 2 + 1) * 1024],
                                in1=psl[:, :], op=ALU.mult), 1)
                            if u % 8 == 7:
                                g = u // 8
                                for ti in range(8):
                                    for hf in range(2):
                                        e.wait_ge(f_mmd.h, 128 * p + 16 * g + 2 * ti + hf + 1)
                                        if g == 0 and ti == 0 and hf == 0:
                                            e.wait_ge(f_x.h, 16 * (p + 1))
                                        f_da.inc(e.tensor_tensor(
                                            out=accv[:, ti, hf * 1024:(hf + 1) * 1024],
                                            in0=accv[:, ti, hf * 1024:(hf + 1) * 1024],
                                            in1=psdh[hf][:, :], op=ALU.add), 1)
                bvector(_v)
                pop_scope()

        if 1 in stages:
            emit_ffn(x_in, x1s, S)

        if 2 in stages:
            with ExitStack() as c2:
                push_scope(c2)
                sb2 = lambda *a, **kw: c2.enter_context(nc.sbuf_tensor(*a, **kw))
                s_cosf = sb2("s_cosf", [128, 32 * DR], F32)
                s_sinf = sb2("s_sinf", [128, 32 * DR], F32)
                s_coso = sb2("s_coso", [128, 16 * DR], F32)
                s_sino = sb2("s_sino", [128, 16 * DR], F32)
                cnT = sb2("cnT", [128, 4 * S], BF16, is_transpose_dest=True)
                krT = sb2("krT", [128, S], BF16, is_transpose_dest=True)
                scol = sb2("scol", [128, 16], F32)
                tmp1 = sb2("tmp1", [128, 512], F32)
                tmp2 = sb2("tmp2", [128, 512], F32)
                a_c = mksem(octx, "a_c")
                b_fold = mksem(octx, "b_fold")

                def _cl(e):
                    for dst, src in ((s_cosf, cos_f), (s_sinf, sin_f),
                                     (s_coso, cos_o), (s_sino, sin_o)):
                        a_c.inc(e.dma_start(
                            out=dst[:].rearrange("p (t f) -> p t f", f=DR),
                            in_=src.rearrange("(t q) f -> q t f", q=128)), 16)
                block.sync(_cl)

                def rope6(e, srcE, srcO, cE, cO, sE, sO, dE, dO, w):
                    e.tensor_tensor(out=tmp1[:, 0:w], in0=srcE, in1=cE, op=ALU.mult)
                    e.tensor_tensor(out=tmp2[:, 0:w], in0=srcO, in1=sE, op=ALU.mult)
                    e.tensor_tensor(out=dE, in0=tmp1[:, 0:w], in1=tmp2[:, 0:w], op=ALU.subtract)
                    e.tensor_tensor(out=tmp1[:, 0:w], in0=srcO, in1=cO, op=ALU.mult)
                    e.tensor_tensor(out=tmp2[:, 0:w], in0=srcE, in1=sO, op=ALU.mult)
                    return e.tensor_tensor(out=dO, in0=tmp1[:, 0:w], in1=tmp2[:, 0:w], op=ALU.add)

                with ExitStack() as ca:
                    push_scope(ca)
                    sba = lambda *a, **kw: ca.enter_context(nc.sbuf_tensor(*a, **kw))
                    xc = sba("a_xc", [128, 8 * D], F32)
                    l2t = sba("a_l2t", [128, 2 * 16 * 512], BF16, is_transpose_dest=True)
                    lnsb2 = sba("a_lnsb", [128, 2 * D], BF16)
                    junk2 = sba("a_junk", [128, D], BF16)
                    wkvab = sba("a_wkvab", [128, 16 * (KVL + DR)], BF16)
                    cnsb = sba("a_cnsb", [128, 2 * KVL], BF16)
                    krf = sba("a_krf", [128, 2 * DR], F32)
                    krsb = sba("a_krsb", [128, 2 * 128], BF16)
                    ast = sba("a_stat", [128, 32 * 6], F32)
                    psk = ca.enter_context(nc.psum_tensor("a_psk", [128, 2 * 1024], F32))
                    a_x = [mksem(octx, "a_x0"), mksem(octx, "a_x1")]
                    a_vst = mksem(octx, "a_vst")
                    a_sr = mksem(octx, "a_sr")
                    a_sev = mksem(octx, "a_sev")
                    a_lnw = [mksem(octx, "a_lnw0"), mksem(octx, "a_lnw1")]
                    a_tr = mksem(octx, "a_tr")
                    a_kmm = mksem(octx, "a_kmm")
                    a_kev = mksem(octx, "a_kev")
                    a_kro = mksem(octx, "a_kro")
                    a_cnw = [mkdsem("a_cnw0"), mkdsem("a_cnw1")]
                    a_wk = mkdsem("a_wk")

                    def ss(k, j):
                        return ast[:, k * 6 + j:k * 6 + j + 1]

                    def _g(e):
                        a_wk.inc(e.dma_start(
                            out=wkvab[:].rearrange("p (dc f) -> p dc f", f=KVL + DR),
                            in_=wkva.rearrange("(dc p) f -> p dc f", p=128)), 16)
                        for k in range(32):
                            e.wait_ge(a_kev.h, k + 1)
                            e.wait_ge(a_kro.h, k + 1)
                            a_cnw[k % 2].inc(e.dma_start(
                                out=cn_b[k * 128:(k + 1) * 128, :],
                                in_=cnsb[:, (k % 2) * KVL:(k % 2 + 1) * KVL]), 16)
                            a_cnw[k % 2].inc(e.dma_start(
                                out=kr_b[k * 128:(k + 1) * 128, :],
                                in_=krsb[:, (k % 2) * 128:(k % 2 + 1) * 128]), 16)
                    block.gpsimd(_g)

                    def _sy(e):
                        for sc in range(8):
                            if sc >= 2:
                                e.wait_ge(a_sev.h, 4 * (sc - 1))
                            hx = (sc % 2) * 4
                            a_x[sc % 2].inc(e.dma_start(
                                out=xc[:, hx * D:(hx + 4) * D].rearrange("p (t d) -> p t d", d=D),
                                in_=x1s[sc * 512:(sc + 1) * 512, :].rearrange("(t q) d -> q t d", q=128)), 16)
                            for ti in range(4):
                                k = 4 * sc + ti
                                e.wait_ge(a_sev.h, k + 1)
                                a_lnw[k % 2].inc(e.dma_start(
                                    out=ln2_b[k * 128:(k + 1) * 128, :],
                                    in_=lnsb2[:, (k % 2) * D:(k % 2 + 1) * D]), 16)
                        e.wait_ge(a_cnw[0].h, 32 * 16)
                        e.wait_ge(a_cnw[1].h, 32 * 16)
                    block.sync(_sy)

                    def _v(e):
                        e.wait_ge(a_c.h, 64)
                        e.memset(krsb[:, DR:128], 0.0)
                        e.memset(krsb[:, 128 + DR:256], 0.0)
                        for sc in range(8):
                            e.wait_ge(a_x[sc % 2].h, 16 * (sc // 2 + 1))
                            hx = (sc % 2) * 4
                            for ti in range(4):
                                k = 4 * sc + ti
                                e.tensor_reduce(ss(k, 0), xc[:, (hx + ti) * D:(hx + ti + 1) * D],
                                                axis=AX.X, op=ALU.add)
                                a_vst.inc(e.tensor_scalar(
                                    out=ss(k, 1), in0=ss(k, 0),
                                    scalar1=-1.0 / D, scalar2=None, op0=ALU.mult), 1)
                                e.wait_ge(a_sr.h, k + 1)
                                a_vst.inc(e.tensor_tensor(
                                    out=ss(k, 2), in0=ss(k, 1), in1=ss(k, 3), op=ALU.mult), 1)
                        for k in range(32):
                            e.wait_ge(a_kev.h, k + 1)
                            if k >= 2:
                                e.wait_ge(a_cnw[k % 2].h, 32 * (k // 2))
                            kbf = (k % 2) * DR
                            kb = (k % 2) * 128
                            cs_ = s_cosf[:, k * DR:(k + 1) * DR]
                            sn_ = s_sinf[:, k * DR:(k + 1) * DR]
                            a_kro.inc(rope6(
                                e,
                                krf[:, kbf + 0:kbf + DR:2], krf[:, kbf + 1:kbf + DR:2],
                                cs_[:, 0:DR:2], cs_[:, 1:DR:2],
                                sn_[:, 0:DR:2], sn_[:, 1:DR:2],
                                krsb[:, kb + 0:kb + DR:2], krsb[:, kb + 1:kb + DR:2],
                                DR // 2), 1)
                    bvector(_v)

                    def _s(e):
                        def atrans(e, sc):
                            e.wait_ge(a_lnw[0].h, 16 * 2 * (sc + 1))
                            e.wait_ge(a_lnw[1].h, 16 * 2 * (sc + 1))
                            if sc >= 1:
                                e.wait_ge(a_tr.h, 256 * sc)
                            if sc >= 2:
                                e.wait_ge(a_kmm.h, 4 * (sc - 1))
                            for dc in range(16):
                                a_tr.inc(e.dma_start(
                                    out=l2t[:, (sc % 2) * 8192 + dc * 512:(sc % 2) * 8192 + (dc + 1) * 512],
                                    in_=ln2_b[sc * 512:(sc + 1) * 512, dc * 128:(dc + 1) * 128],
                                    transpose=True), 16)
                        for sc in range(8):
                            hx = (sc % 2) * 4
                            for ti in range(4):
                                k = 4 * sc + ti
                                e.wait_ge(a_x[sc % 2].h, 16 * (sc // 2 + 1))
                                e.wait_ge(a_vst.h, 2 * k + 1)
                                e.activation(out=junk2[:, :], in_=xc[:, (hx + ti) * D:(hx + ti + 1) * D],
                                             func=AF.Square, bias=ss(k, 1), accum_out=ss(k, 4))
                                e.activation(out=ss(k, 5), in_=ss(k, 4), func=AF.Ln,
                                             scale=1.0 / D, bias=s_eps[:, 0:1])
                                a_sr.inc(e.activation(out=ss(k, 3), in_=ss(k, 5),
                                                      func=AF.Exp, scale=-0.5), 1)
                                e.wait_ge(a_vst.h, 2 * k + 2)
                                if k >= 2:
                                    e.wait_ge(a_lnw[k % 2].h, 16 * (k // 2))
                                a_sev.inc(e.activation(
                                    out=lnsb2[:, (k % 2) * D:(k % 2 + 1) * D],
                                    in_=xc[:, (hx + ti) * D:(hx + ti + 1) * D],
                                    func=AF.Identity, scale=ss(k, 3), bias=ss(k, 2)), 1)
                        atrans(e, 0)
                        atrans(e, 1)
                        for k in range(32):
                            e.wait_ge(a_kmm.h, k + 1)
                            if k >= 2:
                                e.wait_ge(a_cnw[k % 2].h, 32 * (k // 2))
                            kp = (k % 2) * 1024
                            e.activation(out=junk2[:, 0:KVL], in_=psk[:, kp:kp + KVL],
                                         func=AF.Square, accum_out=ss(k, 0))
                            e.activation(out=ss(k, 1), in_=ss(k, 0), func=AF.Ln,
                                         scale=1.0 / KVL, bias=s_eps[:, 1:2])
                            e.activation(out=ss(k, 2), in_=ss(k, 1), func=AF.Exp, scale=-0.5)
                            e.activation(out=cnsb[:, (k % 2) * KVL:(k % 2 + 1) * KVL],
                                         in_=psk[:, kp:kp + KVL],
                                         func=AF.Identity, scale=ss(k, 2))
                            if k >= 2:
                                e.wait_ge(a_kro.h, k - 1)
                            a_kev.inc(e.activation(
                                out=krf[:, (k % 2) * DR:(k % 2 + 1) * DR],
                                in_=psk[:, kp + KVL:kp + KVL + DR], func=AF.Copy), 1)
                            if k % 4 == 3 and k // 4 + 2 < 8:
                                atrans(e, k // 4 + 2)
                    bscalar(_s)

                    def _t(e):
                        e.wait_ge(a_wk.h, 16)
                        for sc in range(8):
                            e.wait_ge(a_tr.h, 16 * 16 * (sc + 1))
                            for ti in range(4):
                                k = 4 * sc + ti
                                if k >= 2:
                                    e.wait_ge(a_kev.h, k - 1)
                                kp = (k % 2) * 1024
                                base = (sc % 2) * 8192
                                for dc in range(16):
                                    lh = l2t[:, base + dc * 512 + ti * 128:base + dc * 512 + (ti + 1) * 128]
                                    e.matmul(psk[:, kp:kp + 512], lh,
                                             wkvab[:, dc * (KVL + DR):dc * (KVL + DR) + 512],
                                             start=(dc == 0), stop=(dc == 15))
                                    ins = e.matmul(psk[:, kp + 512:kp + 576], lh,
                                                   wkvab[:, dc * (KVL + DR) + 512:dc * (KVL + DR) + 576],
                                                   start=(dc == 0), stop=(dc == 15))
                                a_kmm.inc(ins, 1)
                    block.tensor(_t)
                    pop_scope()

                cbc = ExitStack()
                qaT = cbc.enter_context(nc.sbuf_tensor("qaT", [128, 6 * TOWN], BF16))

                with ExitStack() as cb:
                    push_scope(cb)
                    sbb = lambda *a, **kw: cb.enter_context(nc.sbuf_tensor(*a, **kw))
                    lgo = sbb("b_lgo", [128, 2 * D], BF16)
                    l2ot = sbb("b_l2ot", [128, 16 * 1024], BF16, is_transpose_dest=True)
                    wqab = sbb("b_wqab", [128, 16 * QL], BF16)
                    sqaT = sbb("b_sqaT", [128, 6 * TOWN], BF16)
                    qnwb = sbb("b_qnwb", [128, 6], F32)
                    sc1 = sbb("b_sc1", [128, 16], F32)
                    psq = cb.enter_context(nc.psum_tensor("b_psq", [128, 2 * 512], F32))
                    pcs = cb.enter_context(nc.psum_tensor("b_pcs", [128, 16], F32))
                    b_tr = mksem(octx, "b_tr")
                    b_gth = [mkdsem("b_gth0"), mkdsem("b_gth1")]
                    b_gw = [mksem(octx, "b_gw0"), mksem(octx, "b_gw1")]
                    b_wq = mkdsem("b_wq")
                    b_mm = mksem(octx, "b_mm")
                    b_ev = mksem(octx, "b_ev")
                    b_cs = mksem(octx, "b_cs")
                    b_sc = mksem(octx, "b_sc")

                    def _sy(e):
                        for j in range(4):
                            b_tr.inc(e.dma_start(out=cnT[:, j * S:(j + 1) * S],
                                                 in_=cn_b[:, j * 128:(j + 1) * 128],
                                                 transpose=True), 16)
                        b_tr.inc(e.dma_start(out=krT[:, :], in_=kr_b[:, :],
                                             transpose=True), 16)
                        for j in range(16):
                            e.wait_ge(b_gth[j % 2].h, 16 * (j // 2 + 1))
                            b_gw[j % 2].inc(e.dma_start(
                                out=lno_b[j * 128:(j + 1) * 128, :],
                                in_=lgo[:, (j % 2) * D:(j % 2 + 1) * D]), 16)
                        for half in range(2):
                            e.wait_ge(b_gw[0].h, 16 * 8)
                            e.wait_ge(b_gw[1].h, 16 * 8)
                            if half == 1:
                                e.wait_ge(b_mm.h, 12)
                            for dc in range(16):
                                b_tr.inc(e.dma_start(
                                    out=l2ot[:, dc * 1024:(dc + 1) * 1024],
                                    in_=lno_b[half * 1024:(half + 1) * 1024,
                                              dc * 128:(dc + 1) * 128],
                                    transpose=True), 16)
                        e.wait_ge(b_tr.h, 16 * 37)
                    block.sync(_sy)

                    def _g(e):
                        b_wq.inc(e.dma_start(
                            out=wqab[:].rearrange("p (dc f) -> p dc f", f=QL),
                            in_=wqa.rearrange("(dc p) f -> p dc f", p=128)), 16)
                        b_wq.inc(e.dma_start(out=qnwb[:, :], in_=qnw[:, :]), 16)
                        for j in range(16):
                            if j >= 2:
                                e.wait_ge(b_gw[j % 2].h, 16 * (j // 2))
                            b_gth[j % 2].inc(e.indirect_dma_start(
                                out=lgo[:, (j % 2) * D:(j % 2 + 1) * D], out_offset=None,
                                in_=ln2_b[:, :],
                                in_offset=IOffs(ap=s_ridx[:, j:j + 1], axis=0)), 16)
                    block.gpsimd(_g)

                    def _t(e):
                        e.wait_ge(b_wq.h, 32)
                        idx = 0
                        for half in range(2):
                            e.wait_ge(b_tr.h, 16 * (5 + 16 * (half + 1)))
                            for m6 in range(6):
                                for ncx in range(2):
                                    if idx >= 2:
                                        e.wait_ge(b_ev.h, 2 * (idx - 1))
                                    for dc in range(16):
                                        ins = e.matmul(
                                            psq[:, (idx % 2) * 512:(idx % 2 + 1) * 512],
                                            wqab[:, dc * QL + m6 * 128:dc * QL + (m6 + 1) * 128],
                                            l2ot[:, dc * 1024 + ncx * 512:dc * 1024 + (ncx + 1) * 512],
                                            start=(dc == 0), stop=(dc == 15))
                                    b_mm.inc(ins, 1)
                                    idx += 1
                        e.wait_ge(b_ev.h, 48)
                        for sub in range(16):
                            for k6 in range(6):
                                ins = e.matmul(pcs[:, sub:sub + 1],
                                               sqaT[:, k6 * TOWN + sub * 128:k6 * TOWN + (sub + 1) * 128],
                                               s_1b[:, 0:1],
                                               start=(k6 == 0), stop=(k6 == 5))
                        b_cs.inc(ins, 1)
                    block.tensor(_t)

                    def _s(e):
                        idx = 0
                        for half in range(2):
                            for m6 in range(6):
                                for ncx in range(2):
                                    cidx = 2 * half + ncx
                                    e.wait_ge(b_mm.h, idx + 1)
                                    e.activation(
                                        out=qaT[:, m6 * TOWN + cidx * 512:m6 * TOWN + (cidx + 1) * 512],
                                        in_=psq[:, (idx % 2) * 512:(idx % 2 + 1) * 512],
                                        func=AF.Copy)
                                    b_ev.inc(e.activation(
                                        out=sqaT[:, m6 * TOWN + cidx * 512:m6 * TOWN + (cidx + 1) * 512],
                                        in_=psq[:, (idx % 2) * 512:(idx % 2 + 1) * 512],
                                        func=AF.Square), 2)
                                    idx += 1
                        e.wait_ge(b_cs.h, 1)
                        e.activation(out=sc1[:, :], in_=pcs[:, :], func=AF.Ln,
                                     scale=1.0 / QL, bias=s_eps[:, 1:2])
                        b_sc.inc(e.activation(out=scol[:, :], in_=sc1[:, :],
                                              func=AF.Exp, scale=-0.5, bias=s_eps[:, 2:3]), 1)
                    bscalar(_s)

                    def _v(e):
                        e.wait_ge(b_cs.h, 1)
                        e.wait_ge(b_wq.h, 32)
                        for k6 in range(6):
                            ins = e.tensor_scalar(out=qaT[:, k6 * TOWN:(k6 + 1) * TOWN],
                                                  in0=qaT[:, k6 * TOWN:(k6 + 1) * TOWN],
                                                  scalar1=qnwb[:, k6:k6 + 1], scalar2=None,
                                                  op0=ALU.mult)
                        b_fold.inc(ins, 1)
                        e.memset(krT[64:65, :], 1.0)
                    bvector(_v)
                    pop_scope()

                with ExitStack() as cg:
                    push_scope(cg)
                    sbg = lambda *a, **kw: cg.enter_context(nc.sbuf_tensor(*a, **kw))
                    wkvbb = sbg("g_wkvbb", [128, 4 * NH * 256], BF16)
                    kvnwb = sbg("g_kvnwb", [128, 4], F32)
                    wqbg = sbg("g_wqbg", [128, 2 * 6 * 384], BF16)
                    qg = sbg("g_qg", [128, 16 * 384], BF16)
                    qrc = sbg("g_qrc", [128, 16 * 256], BF16)
                    qs2 = sbg("g_qs2", [128, 32], F32)
                    sqq = sbg("g_sqq", [128, 32], F32)
                    cn_ = sbg("g_cn", [128, 32], F32)
                    junk3 = sbg("g_junk", [128, 512], BF16)
                    kst = sbg("g_kst", [128, 8], F32)
                    vjunk = sbg("g_vjunk", [128, 192], BF16)
                    qnT = sbg("g_qnT", [128, 2 * TOWN], BF16, is_transpose_dest=True)
                    qrT = sbg("g_qrT", [128, 2 * TOWN], BF16, is_transpose_dest=True)
                    knT = sbg("g_knT", [128, S], BF16)
                    Vh = sbg("g_Vh", [128, 32 * 132], BF16)
                    PT = sbg("g_PT", [128, 2 * 1024], BF16)
                    og = sbg("g_og", [128, 2 * 16 * 128], BF16)
                    kq = sbg("g_kq", [128, 1], F32)
                    kr64 = sbg("g_kr64", [128, 1], F32)
                    krs = sbg("g_krs", [1, 1], F32)
                    tta = sbg("g_tta", [1, 1], F32)
                    rks = sbg("g_rks", [1, 8], F32)
                    rkc = sbg("g_rkc", [128, 1], F32)
                    rcp = sbg("g_rcp", [128, 2], F32)
                    PS = cg.enter_context(nc.psum_tensor("g_ps", [128, 4096], F32))
                    def vA(s2):
                        return PS[:, 2048 + s2 * 512:2048 + s2 * 512 + 129]

                    def vQ(sub):
                        return PS[:, 3072 + (sub % 2) * 512:3072 + (sub % 2) * 512 + 384]

                    psn_s = [PS[:, 0:512], PS[:, 1024:1536]]
                    vVs = [PS[:, 512:640], PS[:, 1536:1664]]
                    ptiny = PS[:, 2304:2320]
                    g_wk = mkdsem("g_wk")
                    g_wkf = mksem(octx, "g_wkf")
                    g_wq = [mkdsem("g_wq0"), mkdsem("g_wq1")]
                    g_qmm = mksem(octx, "g_qmm")
                    g_qev = mksem(octx, "g_qev")
                    g_qs = mksem(octx, "g_qs")
                    g_sq = mksem(octx, "g_sq")
                    g_rope = mksem(octx, "g_rope")
                    g_cw = mksem(octx, "g_cw")
                    g_qw = mksem(octx, "g_qw")
                    g_qtr = mksem(octx, "g_qtr")
                    g_kmm = mksem(octx, "g_kmm")
                    g_kev = mksem(octx, "g_kev")
                    g_vev = mksem(octx, "g_vev")
                    g_bc = mksem(octx, "g_bc")
                    g_rk = mksem(octx, "g_rk")
                    g_rkb = mksem(octx, "g_rkb")
                    g_smm = mksem(octx, "g_smm")
                    g_sex = mksem(octx, "g_sex")
                    g_msk = mksem(octx, "g_msk")
                    g_amm = mksem(octx, "g_amm")
                    g_rc = mksem(octx, "g_rc")
                    g_oev = mksem(octx, "g_oev")
                    g_ow = mksem(octx, "g_ow")
                    g_kr0 = mksem(octx, "g_kr0")
                    g_kr1 = mksem(octx, "g_kr1")

                    P_B, P_MSK, P_HLAST = {}, {}, {}
                    P_kmmK, P_kmmV = {}, {}
                    n_b = n_mskb = n_kmm = 0
                    for grp in range(NG):
                        for h2 in range(2):
                            for ncx in range(8):
                                n_kmm += 1
                                P_kmmK[(grp, h2, ncx)] = n_kmm
                            for vs in range(32):
                                n_kmm += 1
                                P_kmmV[(grp, h2, vs)] = n_kmm
                            for g8 in range(8):
                                for bj in range(g8 + 1):
                                    n_b += 1
                                    P_B[(grp, h2, g8, bj)] = n_b
                                    if bj == g8:
                                        n_mskb += 1
                                        P_MSK[(grp, h2, g8)] = n_mskb
                            P_HLAST[(grp, h2)] = P_B[(grp, h2, 7, 7)]

                    def P_kev(grp, h2, j):
                        return 9 * (2 * grp + h2) + j + 1

                    def P_vev(grp, h2, vs):
                        return 32 * (2 * grp + h2) + vs + 1

                    def P_oev(grp, h2, g8, s2):
                        return 16 * (2 * grp + h2) + 2 * g8 + s2 + 1

                    def P_rc(grp, h2, g8):
                        return 8 * (2 * grp + h2) + g8 + 1

                    def _g(e):
                        g_wk.inc(e.dma_start(out=kvnwb[:, :], in_=kvnw[:, :]), 16)
                        g_wk.inc(e.dma_start(
                            out=wkvbb[:].rearrange("p (kc f) -> p kc f", f=NH * 256),
                            in_=wkvb.rearrange("(kc p) f -> p kc f", p=128)), 16)
                        wqbr = wqb.rearrange("(k p) f -> p k f", p=128)
                        for grp in range(NG):
                            if grp >= 2:
                                e.wait_ge(g_qmm.h, 16 * (grp - 1))
                            g_wq[grp % 2].inc(e.dma_start(
                                out=wqbg[:, (grp % 2) * 2304:(grp % 2 + 1) * 2304]
                                    .rearrange("p (k f) -> p k f", f=384),
                                in_=wqbr[:, :, grp * 384:(grp + 1) * 384]), 16)
                    block.gpsimd(_g)

                    def _vw(e):
                        e.wait_ge(g_wk.h, 32)
                        for kc in range(4):
                            ins = e.tensor_scalar(
                                out=wkvbb[:, kc * NH * 256:(kc + 1) * NH * 256],
                                in0=wkvbb[:, kc * NH * 256:(kc + 1) * NH * 256],
                                scalar1=kvnwb[:, kc:kc + 1], scalar2=None, op0=ALU.mult)
                        g_wkf.inc(ins, 1)
                    bvector(_vw)

                    def _t(e):
                        e.wait_ge(g_wkf.h, 1)
                        e.wait_ge(g_kr0.h, 1)
                        ins = e.matmul(ptiny[0:1, 3:4], kr64[0:64, 0:1], s_1f[0:64, 0:1],
                                       start=True, stop=True)
                        g_kr1.inc(ins, 1)

                        def emit_av(e, key):
                            grp_, h2_, g8_, bj_ = key
                            b = P_B[key]
                            ntt = 4 * (g8_ + 1)
                            if bj_ == g8_:
                                e.wait_ge(g_msk.h, P_MSK[(grp_, h2_, g8_)])
                            else:
                                e.wait_ge(g_sex.h, b)
                            if bj_ == 0 and (grp_, h2_, g8_) != (0, 0, 0):
                                pg8, ph2, pgrp = g8_ - 1, h2_, grp_
                                if pg8 < 0:
                                    pg8 = 7
                                    ph2 -= 1
                                    if ph2 < 0:
                                        ph2, pgrp = 1, pgrp - 1
                                e.wait_ge(g_oev.h, P_oev(pgrp, ph2, pg8, 1))
                            sb = (b % 2) * 1024
                            for p_ in range(2):
                                for k_ in range(2):
                                    tt = 4 * bj_ + 2 * p_ + k_
                                    for s2 in range(2):
                                        ins = e.matmul(
                                            vA(s2),
                                            PT[:, sb + p_ * 512 + k_ * 256 + s2 * 128:
                                               sb + p_ * 512 + k_ * 256 + (s2 + 1) * 128],
                                            Vh[:, tt * 132:tt * 132 + 129],
                                            start=(tt == 0), stop=(tt == ntt - 1))
                            g_amm.inc(ins, 1)

                        for grp in range(NG):
                            e.wait_ge(g_wq[grp % 2].h, 16 * (grp // 2 + 1))
                            e.wait_ge(b_fold.h, 1)
                            for sub in range(16):
                                gi = 16 * grp + sub
                                if gi >= 2:
                                    e.wait_ge(g_qev.h, gi - 1)
                                for k6 in range(6):
                                    ins = e.matmul(
                                        vQ(sub),
                                        qaT[:, k6 * TOWN + sub * 128:k6 * TOWN + (sub + 1) * 128],
                                        wqbg[:, (grp % 2) * 2304 + k6 * 384:(grp % 2) * 2304 + (k6 + 1) * 384],
                                        start=(k6 == 0), stop=(k6 == 5))
                                g_qmm.inc(ins, 1)
                            for h2 in range(2):
                                h = 2 * grp + h2
                                if h > 0:
                                    e.wait_ge(g_sex.h, P_HLAST[(grp, h2 - 1) if h2 else (grp - 1, 1)])
                                for ncx in range(8):
                                    e.wait_ge(g_kev.h, 9 * h + max(0, ncx - 1))
                                    for kc in range(4):
                                        ins = e.matmul(
                                            psn_s[ncx % 2],
                                            wkvbb[:, kc * NH * 256 + h * 256:kc * NH * 256 + h * 256 + 128],
                                            cnT[:, kc * S + ncx * 512:kc * S + (ncx + 1) * 512],
                                            start=(kc == 0), stop=(kc == 3))
                                    g_kmm.inc(ins, 1)
                                for vs in range(32):
                                    if vs >= 2:
                                        e.wait_ge(g_vev.h, P_vev(grp, h2, vs - 2))
                                    for kc in range(4):
                                        ins = e.matmul(
                                            vVs[vs % 2],
                                            cnT[:, kc * S + vs * 128:kc * S + (vs + 1) * 128],
                                            wkvbb[:, kc * NH * 256 + h * 256 + 128:kc * NH * 256 + (h + 1) * 256],
                                            start=(kc == 0), stop=(kc == 3))
                                    g_kmm.inc(ins, 1)
                                if h2 == 0:
                                    e.wait_ge(g_kev.h, P_kev(grp, 0, 8))
                                    if grp >= 1:
                                        e.wait_ge(g_rkb.h, 2 * grp - 1)
                                        e.wait_ge(g_oev.h, P_oev(grp - 1, 1, 7, 1))
                                    ins = e.matmul(ptiny[0:1, 0:1], kq[:, 0:1], s_1f[:, 0:1],
                                                   start=True, stop=True)
                                    g_bc.inc(ins, 1)
                                    e.wait_ge(g_rk.h, grp + 1)
                                    ins = e.matmul(ptiny[:, 2:3], s_1f[0:1, :], rks[0:1, grp:grp + 1],
                                                   start=True, stop=True)
                                    g_bc.inc(ins, 1)
                                    e.wait_ge(g_qtr.h, 16 * 4 * (grp + 1))
                                e.wait_ge(g_kev.h, 9 * (2 * grp + h2) + 8)
                                e.wait_ge(g_vev.h, 32 * (2 * grp + h2) + 32)
                                pend = None
                                for g8 in range(8):
                                    qsl = qnT[:, h2 * TOWN + g8 * 256:h2 * TOWN + (g8 + 1) * 256]
                                    qrl = qrT[0:65, h2 * TOWN + g8 * 256:h2 * TOWN + (g8 + 1) * 256]
                                    for bj in range(g8 + 1):
                                        b = P_B[(grp, h2, g8, bj)]
                                        if b >= 3:
                                            e.wait_ge(g_sex.h, b - 2)
                                        sb = (b % 2) * 1024
                                        for p_ in range(2):
                                            for k_ in range(2):
                                                tt = 4 * bj + 2 * p_ + k_
                                                sl = sb + p_ * 512 + k_ * 256
                                                e.matmul(PS[:, sl:sl + 256],
                                                         knT[:, tt * 128:(tt + 1) * 128],
                                                         qsl, start=True, stop=False)
                                                ins = e.matmul(PS[:, sl:sl + 256],
                                                               krT[0:65, tt * 128:(tt + 1) * 128],
                                                               qrl, start=False, stop=True)
                                        g_smm.inc(ins, 1)
                                        if pend is not None:
                                            emit_av(e, pend)
                                        pend = (grp, h2, g8, bj)
                                emit_av(e, pend)
                    block.tensor(_t)

                    def _s(e):
                        e.wait_ge(b_fold.h, 1)
                        for ncx in range(8):
                            e.activation(out=junk3[0:64, 0:512],
                                         in_=krT[0:64, ncx * 512:(ncx + 1) * 512],
                                         func=AF.Square, accum_out=kst[0:64, ncx:ncx + 1])
                        g_kr0.inc(e.activation(out=junk3[0:64, 0:8], in_=kst[0:64, 0:8],
                                               func=AF.Identity, accum_out=kr64[0:64, 0:1]), 1)

                        def kv_evacs(e, grp, h2):
                            for ncx in range(8):
                                e.wait_ge(g_kmm.h, P_kmmK[(grp, h2, ncx)])
                                g_kev.inc(e.activation(
                                    out=knT[:, ncx * 512:(ncx + 1) * 512],
                                    in_=psn_s[ncx % 2], func=AF.Copy), 1)
                            for ncx in range(8):
                                e.activation(out=junk3[:, 0:512],
                                             in_=knT[:, ncx * 512:(ncx + 1) * 512],
                                             func=AF.Square, accum_out=kst[:, ncx:ncx + 1])
                            g_kev.inc(e.activation(out=junk3[:, 0:8], in_=kst[:, 0:8],
                                                   func=AF.Identity, accum_out=kq[:, 0:1]), 1)

                        def exps_oev(e, grp, h2):
                            if grp >= 1:
                                e.wait_ge(g_ow.h, 32 * grp)
                            for g8 in range(8):
                                for bj in range(g8 + 1):
                                    b = P_B[(grp, h2, g8, bj)]
                                    e.wait_ge(g_smm.h, b)
                                    if b >= 3:
                                        e.wait_ge(g_amm.h, b - 2)
                                    ins = e.raw.activation(
                                        out=PT[:, (b % 2) * 1024:(b % 2 + 1) * 1024],
                                        in_=PS[:, (b % 2) * 1024:(b % 2 + 1) * 1024],
                                        func=AF.Exp)
                                    ins.then_inc(g_sex.h, 1)
                                    g_sex.n += 1
                                e.wait_ge(g_amm.h, P_B[(grp, h2, g8, g8)])
                                e.wait_ge(g_rc.h, P_rc(grp, h2, g8))
                                for s2 in range(2):
                                    g_oev.inc(e.activation(
                                        out=og[:, (h2 * 16 + g8 * 2 + s2) * 128:(h2 * 16 + g8 * 2 + s2 + 1) * 128],
                                        in_=vA(s2)[:, 0:128],
                                        func=AF.Identity, scale=rcp[:, s2:s2 + 1]), 1)

                        for grp in range(NG):
                            for sub in range(16):
                                gi = 16 * grp + sub
                                e.wait_ge(g_qmm.h, gi + 1)
                                if grp >= 1 and sub == 0:
                                    e.wait_ge(g_qw.h, 64 * grp)
                                g_qev.inc(e.activation(
                                    out=qg[:, sub * 384:(sub + 1) * 384],
                                    in_=vQ(sub),
                                    func=AF.Identity, scale=scol[:, sub:sub + 1]), 1)
                            kv_evacs(e, grp, 0)
                            e.wait_ge(g_qs.h, grp + 1)
                            g_sq.inc(e.activation(out=sqq[:, :], in_=qs2[:, :], func=AF.Sqrt), 1)
                            e.wait_ge(g_rkb.h, 2 * grp + 1)
                            g_rk.inc(e.activation(out=rks[0:1, grp:grp + 1], in_=tta[0:1, 0:1],
                                                  func=AF.Sqrt, scale=1.0 / S), 1)
                            exps_oev(e, grp, 0)
                            kv_evacs(e, grp, 1)
                            exps_oev(e, grp, 1)
                    bscalar(_s)

                    def _v(e):
                        e.wait_ge(a_c.h, 64)
                        e.wait_ge(c_ld.h, 16 * 6)
                        e.memset(Vh[:, :].rearrange("p (v c) -> p v c", c=132)[:, :, 128:132], 1.0)
                        e.memset(qrc[:, :].rearrange("p (s f) -> p s f", f=128)[:, :, 65:128], 0.0)
                        e.wait_ge(g_kr1.h, 1)
                        e.tensor_copy(out=krs[0:1, 0:1], in_=ptiny[0:1, 3:4])

                        def v_evacs(e, grp, h2):
                            for vs in range(32):
                                e.wait_ge(g_kmm.h, P_kmmV[(grp, h2, vs)])
                                g_vev.inc(e.tensor_copy(
                                    out=Vh[:, vs * 132:vs * 132 + 128],
                                    in_=vVs[vs % 2]), 1)

                        def masks_recips(e, grp, h2):
                            for g8 in range(8):
                                bd = P_B[(grp, h2, g8, g8)]
                                e.wait_ge(g_sex.h, bd)
                                ins = e.raw.tensor_tensor(
                                    out=PT[:, (bd % 2) * 1024:(bd % 2 + 1) * 1024],
                                    in0=PT[:, (bd % 2) * 1024:(bd % 2 + 1) * 1024],
                                    in1=s_mask[:, 0:1024],
                                    op=ALU.mult)
                                ins.then_inc(g_msk.h, 1)
                                g_msk.n += 1
                                e.wait_ge(g_amm.h, bd)
                                e.reciprocal(out=rcp[:, 0:1], in_=vA(0)[:, 128:129])
                                g_rc.inc(e.reciprocal(out=rcp[:, 1:2], in_=vA(1)[:, 128:129]), 1)

                        qv = qg[:, :].rearrange("p (s f) -> p s f", f=384)
                        qrv = qrc[:, :].rearrange("p (s f) -> p s f", f=256)
                        co = s_coso[:, :].rearrange("p (s f) -> p s f", f=DR)
                        si_ = s_sino[:, :].rearrange("p (s f) -> p s f", f=DR)
                        for grp in range(NG):
                            for sub in range(16):
                                e.wait_ge(g_qev.h, 16 * grp + sub + 1)
                                for h2 in range(2):
                                    e.tensor_tensor(
                                        out=vjunk[:, 0:192],
                                        in0=qg[:, sub * 384 + h2 * 192:sub * 384 + (h2 + 1) * 192],
                                        in1=qg[:, sub * 384 + h2 * 192:sub * 384 + (h2 + 1) * 192],
                                        op=ALU.mult)
                                    ins = e.tensor_reduce(
                                        qs2[:, h2 * 16 + sub:h2 * 16 + sub + 1],
                                        vjunk[:, 0:192], axis=AX.X, op=ALU.add)
                            g_qs.inc(ins, 1)
                            if grp >= 1:
                                e.wait_ge(g_qw.h, 64 * grp)
                            for h2 in range(2):
                                g_rope.inc(rope6(
                                    e,
                                    qv[:, :, h2 * 192 + 128 + 0:h2 * 192 + 192:2],
                                    qv[:, :, h2 * 192 + 128 + 1:h2 * 192 + 192:2],
                                    co[:, :, 0:DR:2], co[:, :, 1:DR:2],
                                    si_[:, :, 0:DR:2], si_[:, :, 1:DR:2],
                                    qrv[:, :, h2 * 128 + 0:h2 * 128 + 64:2],
                                    qrv[:, :, h2 * 128 + 1:h2 * 128 + 64:2],
                                    512), 1)
                            v_evacs(e, grp, 0)
                            e.wait_ge(g_bc.h, 2 * grp + 1)
                            e.tensor_copy(out=tta[0:1, 0:1], in_=ptiny[0:1, 0:1])
                            g_rkb.inc(e.tensor_tensor(out=tta[0:1, 0:1], in0=tta[0:1, 0:1],
                                                      in1=krs[0:1, 0:1], op=ALU.add), 1)
                            e.wait_ge(g_bc.h, 2 * grp + 2)
                            g_rkb.inc(e.tensor_scalar(out=rkc[:, 0:1], in0=ptiny[:, 2:3],
                                                      scalar1=-C_COEF, scalar2=None,
                                                      op0=ALU.mult), 1)
                            e.wait_ge(g_sq.h, grp + 1)
                            e.tensor_scalar(out=cn_[:, :], in0=sqq[:, :],
                                            scalar1=rkc[:, 0:1], scalar2=-80.0,
                                            op0=ALU.mult, op1=ALU.max)
                            for h2 in range(2):
                                g_cw.inc(e.tensor_copy(
                                    out=qrv[:, :, h2 * 128 + 64:h2 * 128 + 65],
                                    in_=cn_[:, :].rearrange("p (h s) -> p s h", h=2)[:, :, h2:h2 + 1]), 1)
                            masks_recips(e, grp, 0)
                            v_evacs(e, grp, 1)
                            masks_recips(e, grp, 1)
                    bvector(_v)

                    def _sy(e):
                        for grp in range(NG):
                            e.wait_ge(g_qev.h, 16 * (grp + 1))
                            e.wait_ge(g_rope.h, 2 * (grp + 1))
                            e.wait_ge(g_cw.h, 2 * (grp + 1))
                            if grp >= 1:
                                e.wait_ge(g_amm.h, P_HLAST[(grp - 1, 1)])
                            qv = qg[:, :].rearrange("p (s f) -> p s f", f=384)
                            qrv = qrc[:, :].rearrange("p (s f) -> p s f", f=256)
                            for h2 in range(2):
                                g_qw.inc(e.dma_start(
                                    out=qn_b[h2, :, :].rearrange("(s p) f -> p s f", p=128),
                                    in_=qv[:, :, h2 * 192:h2 * 192 + 128]), 16)
                                g_qw.inc(e.dma_start(
                                    out=qr_b[h2, :, :].rearrange("(s p) f -> p s f", p=128),
                                    in_=qrv[:, :, h2 * 128:(h2 + 1) * 128]), 16)
                            e.wait_ge(g_qw.h, 64 * (grp + 1))
                            for h2 in range(2):
                                g_qtr.inc(e.dma_start(
                                    out=qnT[:, h2 * TOWN:(h2 + 1) * TOWN],
                                    in_=qn_b[h2, :, :], transpose=True), 16)
                                g_qtr.inc(e.dma_start(
                                    out=qrT[:, h2 * TOWN:(h2 + 1) * TOWN],
                                    in_=qr_b[h2, :, :], transpose=True), 16)
                            for h2 in range(2):
                                e.wait_ge(g_oev.h, P_oev(grp, h2, 7, 1))
                                g_ow.inc(e.dma_start(
                                    out=o_b[2 * grp + h2, :, :].rearrange("(s p) f -> p s f", p=128),
                                    in_=og[:, :].rearrange("p (hs f) -> p hs f", f=128)[:, h2 * 16:(h2 + 1) * 16, :]), 16)
                        e.wait_ge(g_ow.h, 16 * 2 * NG)
                    block.sync(_sy)
                    pop_scope()

                cbc.close()

                with ExitStack() as cd:
                    push_scope(cd)
                    sbd = lambda *a, **kw: cd.enter_context(nc.sbuf_tensor(*a, **kw))
                    oT = sbd("d_oT", [128, 16 * TOWN], BF16, is_transpose_dest=True)
                    wob = sbd("d_wob", [128, 16 * 1024], BF16)
                    xg = sbd("d_xg", [128, 2 * D], F32)
                    x2t = sbd("d_x2t", [128, 2 * 1024], F32)
                    pso = cd.enter_context(nc.psum_tensor("d_pso", [128, 2 * 1024], F32))
                    d_tr = mksem(octx, "d_tr")
                    d_wo = mkdsem("d_wo")
                    d_gx = [mkdsem("d_gx0"), mkdsem("d_gx1")]
                    d_mm = mksem(octx, "d_mm")
                    d_ad = mksem(octx, "d_ad")
                    d_xw = [mksem(octx, "d_xw0"), mksem(octx, "d_xw1")]

                    def _sy(e):
                        for hd in range(16):
                            d_tr.inc(e.dma_start(
                                out=oT[:, hd * TOWN:(hd + 1) * TOWN],
                                in_=o_b[hd, :, :], transpose=True), 16)
                        for dh in range(2):
                            for sub in range(16):
                                k = 16 * dh + sub
                                e.wait_ge(d_ad.h, k + 1)
                                d_xw[k % 2].inc(e.dma_start(
                                    out=x2s[sub * 128:(sub + 1) * 128, dh * 1024:(dh + 1) * 1024],
                                    in_=x2t[:, (k % 2) * 1024:(k % 2 + 1) * 1024]), 16)
                        e.wait_ge(d_xw[0].h, 16 * 16)
                        e.wait_ge(d_xw[1].h, 16 * 16)
                    block.sync(_sy)

                    def _g(e):
                        wor = wo.rearrange("(hd p) d -> p hd d", p=128)
                        for dh in range(2):
                            if dh == 1:
                                e.wait_ge(d_mm.h, 16)
                            d_wo.inc(e.dma_start(
                                out=wob[:].rearrange("p (hd f) -> p hd f", f=1024),
                                in_=wor[:, :, dh * 1024:(dh + 1) * 1024]), 16)
                            for sub in range(16):
                                k = 16 * dh + sub
                                if k >= 2:
                                    e.wait_ge(d_ad.h, k - 1)
                                d_gx[k % 2].inc(e.indirect_dma_start(
                                    out=xg[:, (k % 2) * D:(k % 2 + 1) * D], out_offset=None,
                                    in_=x1s[:, :],
                                    in_offset=IOffs(ap=s_ridx[:, sub:sub + 1], axis=0)), 16)
                    block.gpsimd(_g)

                    def _t(e):
                        e.wait_ge(d_tr.h, 16 * 16)
                        for dh in range(2):
                            e.wait_ge(d_wo.h, 16 * (dh + 1))
                            for sub in range(16):
                                k = 16 * dh + sub
                                if k >= 2:
                                    e.wait_ge(d_ad.h, k - 1)
                                for hd in range(16):
                                    for ncx in range(2):
                                        ins = e.matmul(
                                            pso[:, (k % 2) * 1024 + ncx * 512:(k % 2) * 1024 + (ncx + 1) * 512],
                                            oT[:, hd * TOWN + sub * 128:hd * TOWN + (sub + 1) * 128],
                                            wob[:, hd * 1024 + ncx * 512:hd * 1024 + (ncx + 1) * 512],
                                            start=(hd == 0), stop=(hd == 15))
                                d_mm.inc(ins, 1)
                    block.tensor(_t)

                    def _v(e):
                        for dh in range(2):
                            for sub in range(16):
                                k = 16 * dh + sub
                                e.wait_ge(d_mm.h, k + 1)
                                e.wait_ge(d_gx[k % 2].h, 16 * (k // 2 + 1))
                                if k >= 2:
                                    e.wait_ge(d_xw[k % 2].h, 16 * (k // 2))
                                d_ad.inc(e.tensor_tensor(
                                    out=x2t[:, (k % 2) * 1024:(k % 2 + 1) * 1024],
                                    in0=pso[:, (k % 2) * 1024:(k % 2 + 1) * 1024],
                                    in1=xg[:, (k % 2) * D + dh * 1024:(k % 2) * D + (dh + 1) * 1024],
                                    op=ALU.add), 1)
                    bvector(_v)
                    pop_scope()

        if 3 in stages:
            emit_ffn(x2s, out_t, TOWN)

    return nc


def _host_consts():
    inv = 1.0 / (10000.0 ** (np.arange(0, DR, 2, dtype=np.float32) / DR))
    pos = np.arange(S, dtype=np.float32)
    freqs = np.outer(pos, inv).astype(np.float32)
    emb = np.concatenate([freqs, freqs], axis=-1)
    cosf = np.cos(emb).astype(np.float32)
    sinf = np.sin(emb).astype(np.float32)
    H64 = _hadamard(64)
    H128 = _hadamard(128)
    HA = np.kron(np.eye(2), H64)
    hc = np.zeros((128, 256), np.float32)
    hc[:, 0:128] = HA
    hc[:, 128:256] = H128
    hconst = hc.astype(ml_dtypes.bfloat16)
    onesf = np.ones((128, 128), np.float32)
    onesb = np.ones((128, 1), np.float32).astype(ml_dtypes.bfloat16)
    return cosf, sinf, hconst, onesf, onesb


def make_in_maps(inputs):
    x = np.ascontiguousarray(np.asarray(inputs["L0_attn_x"], dtype=np.float32))
    W1 = np.asarray(inputs["L0_ffn_W1"], dtype=np.float32)
    W2 = np.asarray(inputs["L0_ffn_W2"], dtype=np.float32)
    W3 = np.asarray(inputs["L0_ffn_W3"], dtype=np.float32)

    w1tg = np.ascontiguousarray(W1.reshape(128, 64, 16, 128).transpose(1, 2, 3, 0))
    w3tg = np.ascontiguousarray(W3.reshape(128, 64, 16, 128).transpose(1, 2, 3, 0))
    w2t = np.ascontiguousarray(W2.T)

    cosf, sinf, hconst, onesf, onesb = _host_consts()

    base = {
        "w1tg": w1tg, "w3tg": w3tg, "w2t": w2t,
        "wqa": np.ascontiguousarray(inputs["L0_attn_w_qa"], dtype=np.float32),
        "wqb": np.ascontiguousarray(inputs["L0_attn_w_qb"], dtype=np.float32),
        "wkva": np.ascontiguousarray(inputs["L0_attn_w_kva"], dtype=np.float32),
        "wkvb": np.ascontiguousarray(inputs["L0_attn_w_kvb"], dtype=np.float32),
        "wo": np.ascontiguousarray(inputs["L0_attn_w_o"], dtype=np.float32),
        "qnw": np.ascontiguousarray(np.asarray(inputs["L0_attn_q_a_norm_w"], np.float32).reshape(6, 128).T),
        "kvnw": np.ascontiguousarray(np.asarray(inputs["L0_attn_kv_a_norm_w"], np.float32).reshape(4, 128).T),
        "cos_f": cosf, "sin_f": sinf,
        "hconst": hconst, "ones_f": onesf, "ones_b": onesb,
        "epsc": np.broadcast_to(
            np.array([LNEPS, RMSEPS, QSC_BIAS, 0.0], np.float32), (128, 4)).copy(),
    }

    in_maps = []
    own_rows_all = []
    for c in range(8):
        b, half = c // 2, c % 2
        own = np.concatenate([np.arange(g * 512 + half * 256, g * 512 + half * 256 + 256)
                              for g in range(8)]).astype(np.int64)
        own_rows_all.append(own)
        rix = own.reshape(16, 128).T.astype(np.uint32)
        mask = np.zeros((128, 4, 256), np.float32)
        ti = np.arange(128)[:, None]
        qi = np.arange(256)[None, :]
        for o in range(4):
            mask[:, o, :] = (o * 128 + ti - half * 256 <= qi).astype(np.float32)
        im = dict(base)
        im["x"] = np.ascontiguousarray(x[b])
        im["cos_o"] = np.ascontiguousarray(cosf[own])
        im["sin_o"] = np.ascontiguousarray(sinf[own])
        im["maskc"] = np.ascontiguousarray(mask.reshape(128, 4 * 256).astype(ml_dtypes.bfloat16))
        im["ridx"] = np.ascontiguousarray(rix)
        in_maps.append(im)
    return in_maps, own_rows_all


def kernel(**inputs):
    in_maps, own_rows_all = make_in_maps(inputs)
    nc = build_nc()
    res = run_bass_kernel_spmd(nc, in_maps, core_ids=list(range(8)))
    out = np.empty((BATCH, S, D), np.float32)
    for c in range(8):
        b = c // 2
        out[b, own_rows_all[c]] = np.asarray(res.results[c]["out"], np.float32)
    return out


# version 09
# speedup vs baseline: 1.3831x; 1.0514x over previous
import math
from contextlib import ExitStack

import numpy as np
import ml_dtypes
import concourse.bass as bass
import concourse.mybir as mybir
import bass_rust
from concourse.bass_utils import run_bass_kernel_spmd

F32 = mybir.dt.float32
BF16 = mybir.dt.bfloat16
U32 = mybir.dt.uint32
AF = mybir.ActivationFunctionType
ALU = mybir.AluOpType
AX = mybir.AxisListType
IOffs = bass_rust.IndirectOffsetOnAxis

BATCH, S, D, HID, NH = 4, 4096, 2048, 8192, 16
DN, DR, DV, DQK = 128, 64, 128, 192
QL, KVL = 768, 512
TOWN = 2048
LNEPS = 1e-5
RMSEPS = 1e-6
W2SCALE = 0.5 / math.sqrt(HID)
QSC_BIAS = math.log(1.0 / math.sqrt(DQK))
C_COEF = 8.0 / math.sqrt(DQK)
NG = 8
NTT_G = sum(4 * (g8 + 1) for g8 in range(8))


def _hadamard(n):
    h = np.array([[1.0]])
    while h.shape[0] < n:
        h = np.block([[h, h], [h, -h]])
    return h


class Sem:
    def __init__(self, ctx, nc, name):
        self.h = ctx.enter_context(nc.semaphore(name))
        self.n = 0

    def inc(self, instr, k=1):
        instr.then_inc(self.h, k)
        self.n += k
        return self.n


_WRAP_OPS = frozenset((
    "activation", "copy", "add", "mul", "tensor_tensor", "tensor_scalar",
    "tensor_copy", "tensor_reduce", "reduce_sum", "reduce_max", "reciprocal",
    "memset", "tensor_tensor_reduce", "select", "transpose", "iota",
))


class _Shim:
    def __init__(self, e):
        self._e = e

    def then_inc(self, h, k):
        self._e.sem_inc(h, k)
        return self


class SafeEng:

    def __init__(self, e, prog):
        self._e = e
        self._p = prog

    @property
    def raw(self):
        return self._e

    def __getattr__(self, nm):
        a = getattr(self._e, nm)
        if nm in _WRAP_OPS:
            def wrap(*args, **kw):
                ins = a(*args, **kw)
                self._p.n += 1
                ins.then_inc(self._p.h, 1)
                self._e.wait_ge(self._p.h, self._p.n)
                return _Shim(self._e)
            return wrap
        return a


def build_nc(stages=(0, 1, 2, 3), debug=False):
    nc = bass.Bass("TRN2", target_bir_lowering=False)

    x_in = nc.dram_tensor("x", [S, D], F32, kind="ExternalInput").ap()
    w1tg = nc.dram_tensor("w1tg", [64, 16, 128, 128], F32, kind="ExternalInput").ap()
    w3tg = nc.dram_tensor("w3tg", [64, 16, 128, 128], F32, kind="ExternalInput").ap()
    w2t = nc.dram_tensor("w2t", [HID, D], F32, kind="ExternalInput").ap()
    wqa = nc.dram_tensor("wqa", [D, QL], F32, kind="ExternalInput").ap()
    wqb = nc.dram_tensor("wqb", [QL, NH * DQK], F32, kind="ExternalInput").ap()
    wkva = nc.dram_tensor("wkva", [D, KVL + DR], F32, kind="ExternalInput").ap()
    wkvb = nc.dram_tensor("wkvb", [KVL, NH * 256], F32, kind="ExternalInput").ap()
    wo = nc.dram_tensor("wo", [NH * DV, D], F32, kind="ExternalInput").ap()
    qnw = nc.dram_tensor("qnw", [128, 6], F32, kind="ExternalInput").ap()
    kvnw = nc.dram_tensor("kvnw", [128, 4], F32, kind="ExternalInput").ap()
    cos_f = nc.dram_tensor("cos_f", [S, DR], F32, kind="ExternalInput").ap()
    sin_f = nc.dram_tensor("sin_f", [S, DR], F32, kind="ExternalInput").ap()
    cos_o = nc.dram_tensor("cos_o", [TOWN, DR], F32, kind="ExternalInput").ap()
    sin_o = nc.dram_tensor("sin_o", [TOWN, DR], F32, kind="ExternalInput").ap()
    hconst = nc.dram_tensor("hconst", [128, 256], BF16, kind="ExternalInput").ap()
    ones_f = nc.dram_tensor("ones_f", [128, 128], F32, kind="ExternalInput").ap()
    ones_b = nc.dram_tensor("ones_b", [128, 1], BF16, kind="ExternalInput").ap()
    maskc = nc.dram_tensor("maskc", [128, 4 * 256], BF16, kind="ExternalInput").ap()
    epsc = nc.dram_tensor("epsc", [128, 4], F32, kind="ExternalInput").ap()
    ridx = nc.dram_tensor("ridx", [128, 16], U32, kind="ExternalInput").ap()

    out_t = nc.dram_tensor("out", [TOWN, D], F32, kind="ExternalOutput").ap()

    dbg = "ExternalOutput" if debug else "Internal"
    zsc = nc.dram_tensor("zsc", [HID, D], BF16, kind="Internal").ap()
    wf = nc.dram_tensor("wfs", [64, 128, D], BF16, kind=dbg).ap()
    x1s = nc.dram_tensor("x1s", [S, D], F32, kind=dbg).ap()
    x2s = nc.dram_tensor("x2s", [TOWN, D], F32, kind=dbg).ap()
    lnt_b = nc.dram_tensor("lnt_b", [1024, D], BF16, kind="Internal").ap()
    ln2_b = nc.dram_tensor("ln2_b", [S, D], BF16, kind="Internal").ap()
    lno_b = nc.dram_tensor("lno_b", [TOWN, D], BF16, kind="Internal").ap()
    cn_b = nc.dram_tensor("cn_b", [S, KVL], BF16, kind="Internal").ap()
    kr_b = nc.dram_tensor("kr_b", [S, 128], BF16, kind="Internal").ap()
    qn_b = nc.dram_tensor("qn_b", [2, TOWN, 128], BF16, kind="Internal").ap()
    qr_b = nc.dram_tensor("qr_b", [2, TOWN, 128], BF16, kind="Internal").ap()
    o_b = nc.dram_tensor("o_b", [NH, TOWN, DV], BF16, kind="Internal").ap()

    octx = ExitStack()
    with octx:
        osb = lambda *a, **kw: octx.enter_context(nc.sbuf_tensor(*a, **kw))
        s_hc = osb("s_hc", [128, 256], BF16)
        s_1f = osb("s_1f", [128, 128], F32)
        s_1b = osb("s_1b", [128, 1], BF16)
        s_mask = osb("s_mask", [128, 4 * 256], BF16)
        s_ridx = osb("s_ridx", [128, 16], U32)
        s_eps = osb("s_eps", [128, 4], F32)

        ph = Sem(octx, nc, "ph")
        block = octx.enter_context(nc.Block())
        prog_v = Sem(octx, nc, "prog_v")
        prog_s = Sem(octx, nc, "prog_s")
        def bvector(fn):
            block.vector(lambda e: fn(SafeEng(e, prog_v)))

        def bscalar(fn):
            block.scalar(lambda e: fn(SafeEng(e, prog_s)))

        sem_id = [0]

        def push_scope(ctx_):
            pass

        def mksem(_ctx, label=None):
            sem_id[0] += 1
            return Sem(octx, nc, label or ("s%d" % sem_id[0]))

        mkdsem = lambda label: mksem(None, label)

        def pop_scope():
            barrier()

        def barrier():
            tgt = ph.n + 5
            ph.n = tgt
            for name in ("sync", "gpsimd", "tensor", "vector", "scalar"):
                def _f(e, tgt=tgt):
                    e.sem_inc(ph.h, 1)
                    e.wait_ge(ph.h, tgt)
                getattr(block, name)(_f)

        push_scope(octx)
        c_ld = Sem(octx, nc, "c_ld")

        def _consts(e):
            for dst, src in ((s_hc, hconst), (s_1f, ones_f), (s_1b, ones_b),
                             (s_mask, maskc), (s_ridx, ridx), (s_eps, epsc)):
                c_ld.inc(e.dma_start(out=dst[:], in_=src[:, :]), 16)
        block.sync(_consts)

        if 0 in stages:
            with ExitStack() as c0:
                push_scope(c0)
                sb0 = lambda *a, **kw: c0.enter_context(nc.sbuf_tensor(*a, **kw))
                s_w2 = sb0("s_w2", [128, 4 * D], BF16)
                s_z = sb0("s_z", [128, 2 * D], BF16)
                s_zs = sb0("s_zs", [128, 4 * D], BF16)
                s_wfo = sb0("s_wfo", [128, 2 * D], BF16)
                pA = c0.enter_context(nc.psum_tensor("p0a", [128, D], F32))
                pB = c0.enter_context(nc.psum_tensor("p0b", [128, D], F32))
                w_ld = [mkdsem("w_ld%d" % i) for i in range(4)]
                w_mm = mksem(octx, "w_mm")
                w_ev = mksem(octx, "w_ev")
                w_st = [mksem(octx, "w_st0"), mksem(octx, "w_st1")]
                w_fst = [mksem(octx, "w_fst0"), mksem(octx, "w_fst1")]
                w_zl = [mksem(octx, "w_zl%d" % i) for i in range(4)]

                def _g(e):
                    for hc in range(64):
                        if hc >= 4:
                            e.wait_ge(w_mm.h, hc - 3)
                        w_ld[hc % 4].inc(e.dma_start(
                            out=s_w2[:, (hc % 4) * D:(hc % 4 + 1) * D],
                            in_=w2t[hc * 128:(hc + 1) * 128, :]), 16)
                block.gpsimd(_g)

                def _t(e):
                    e.wait_ge(c_ld.h, 16 * 6)
                    for hc in range(64):
                        e.wait_ge(w_ld[hc % 4].h, 16 * (hc // 4 + 1))
                        if hc >= 2:
                            e.wait_ge(w_ev.h, hc - 1)
                        pp = pA if hc % 2 == 0 else pB
                        for ncx in range(4):
                            ins = e.matmul(
                                pp[:, ncx * 512:(ncx + 1) * 512],
                                s_hc[:, 0:128],
                                s_w2[:, (hc % 4) * D + ncx * 512:(hc % 4) * D + (ncx + 1) * 512],
                                start=True, stop=True)
                        w_mm.inc(ins, 1)
                    for si in range(64):
                        e.wait_ge(w_zl[si % 4].h, 16 * (si // 4 + 1))
                        if si >= 2:
                            e.wait_ge(w_ev.h, 64 + si - 1)
                        pp = pA if si % 2 == 0 else pB
                        for ncx in range(4):
                            ins = e.matmul(
                                pp[:, ncx * 512:(ncx + 1) * 512],
                                s_hc[:, 128:256],
                                s_zs[:, (si % 4) * D + ncx * 512:(si % 4) * D + (ncx + 1) * 512],
                                start=True, stop=True)
                        w_mm.inc(ins, 1)
                block.tensor(_t)

                def _s(e):
                    for hc in range(64):
                        e.wait_ge(w_mm.h, hc + 1)
                        if hc >= 2:
                            e.wait_ge(w_st[hc % 2].h, 16 * (hc // 2))
                        pp = pA if hc % 2 == 0 else pB
                        w_ev.inc(e.activation(
                            out=s_z[:, (hc % 2) * D:(hc % 2 + 1) * D],
                            in_=pp[:, :], func=AF.Copy), 1)
                        w_st[hc % 2].inc(e.dma_start(
                            out=zsc[hc * 128:(hc + 1) * 128, :],
                            in_=s_z[:, (hc % 2) * D:(hc % 2 + 1) * D]), 16)
                    for si in range(64):
                        e.wait_ge(w_mm.h, 64 + si + 1)
                        if si >= 2:
                            e.wait_ge(w_fst[si % 2].h, 16 * (si // 2))
                        pp = pA if si % 2 == 0 else pB
                        w_ev.inc(e.activation(
                            out=s_wfo[:, (si % 2) * D:(si % 2 + 1) * D],
                            in_=pp[:, :], func=AF.Identity, scale=W2SCALE), 1)
                        w_fst[si % 2].inc(e.dma_start(
                            out=wf[si, :, :],
                            in_=s_wfo[:, (si % 2) * D:(si % 2 + 1) * D]), 16)
                bscalar(_s)

                def _sy(e):
                    zr = zsc.rearrange("(j s) d -> s j d", s=64)
                    for si in range(64):
                        if si == 0:
                            e.wait_ge(w_st[0].h, 16 * 32)
                            e.wait_ge(w_st[1].h, 16 * 32)
                        if si >= 4:
                            e.wait_ge(w_mm.h, 64 + si - 3)
                        w_zl[si % 4].inc(e.dma_start(
                            out=s_zs[:, (si % 4) * D:(si % 4 + 1) * D],
                            in_=zr[si, :, :]), 16)
                    e.wait_ge(w_fst[0].h, 16 * 32)
                    e.wait_ge(w_fst[1].h, 16 * 32)
                block.sync(_sy)
                pop_scope()

        FFN_ID = [0]
        FFN_SEMS = {}

        def emit_ffn(x_src, x_dst, T):
            npass = T // 1024
            FFN_ID[0] += 1
            tag = "f%d_" % FFN_ID[0]
            with ExitStack() as cf:
                push_scope(cf)
                sbf = lambda nm, *a, **kw: cf.enter_context(nc.sbuf_tensor(tag + nm, *a, **kw))
                acc = sbf("acc", [128, 8 * D], F32)
                l1t = sbf("l1t", [128, 16 * 1024], BF16, is_transpose_dest=True)
                hT = sbf("hT", [128, 8 * 1024], BF16)
                wfg = sbf("wfg", [128, 8 * D], BF16)
                sW1 = sbf("w1", [128, 2 * 2048], BF16)
                sW3 = sbf("w3", [128, 2 * 2048], BF16)
                sg = sbf("sg", [128, 2 * 1024], BF16)
                junk = sbf("junk", [128, D], BF16)
                lnsb = sbf("lnsb", [128, 2 * D], BF16)
                xl = sbf("xl", [128, 2 * D], F32)
                stat = sbf("stat", [128, 8 * 6], F32)
                psg = cf.enter_context(nc.psum_tensor(tag + "psg", [128, 1024], F32))
                psl = cf.enter_context(nc.psum_tensor(tag + "psl", [128, 1024], F32))
                psda = cf.enter_context(nc.psum_tensor(tag + "psda", [128, 1024], F32))
                psdb = cf.enter_context(nc.psum_tensor(tag + "psdb", [128, 1024], F32))
                psdh = [psda, psdb]
                if not FFN_SEMS:
                    FFN_SEMS["pb"] = 0
                    for nm_ in ("f_x", "f_tr", "f_wf", "f_mmu", "f_mmd", "f_se",
                                "f_vm", "f_da", "f_acw"):
                        FFN_SEMS[nm_] = mksem(octx, nm_)
                    FFN_SEMS["f_lnw"] = [mksem(octx, "f_lnw0"), mksem(octx, "f_lnw1")]
                    FFN_SEMS["f_xl"] = [mksem(octx, "f_xl0"), mksem(octx, "f_xl1")]
                    FFN_SEMS["f_wl"] = [mkdsem("f_wl0"), mkdsem("f_wl1")]
                pb = FFN_SEMS["pb"]
                FFN_SEMS["pb"] = pb + npass
                f_x = FFN_SEMS["f_x"]
                f_lnw = FFN_SEMS["f_lnw"]
                f_tr = FFN_SEMS["f_tr"]
                f_wf = FFN_SEMS["f_wf"]
                f_wl = FFN_SEMS["f_wl"]
                f_mmu = FFN_SEMS["f_mmu"]
                f_mmd = FFN_SEMS["f_mmd"]
                f_se = FFN_SEMS["f_se"]
                f_vm = FFN_SEMS["f_vm"]
                f_da = FFN_SEMS["f_da"]
                f_xl = FFN_SEMS["f_xl"]
                f_acw = FFN_SEMS["f_acw"]

                accv = acc[:, :].rearrange("p (t d) -> p t d", d=D)

                def stt(ti, j):
                    return stat[:, ti * 6 + j:ti * 6 + j + 1]

                def _sy(e):
                    for p0 in range(npass):
                        p = pb + p0
                        if p > 0:
                            e.wait_ge(f_da.h, 128 * p)
                            e.wait_ge(f_acw.h, 16 * p)
                        f_x.inc(e.dma_start(
                            out=accv[:, :, :],
                            in_=x_src[p0 * 1024:(p0 + 1) * 1024, :]
                                .rearrange("(t q) d -> q t d", q=128)), 16)
                        for u in range(64):
                            g = u // 8
                            e.wait_ge(f_mmd.h, 128 * p + 16 * g)
                            f_wf.inc(e.dma_start(
                                out=wfg[:, (u % 8) * D:(u % 8 + 1) * D],
                                in_=wf[u, :, :]), 16)
                        e.wait_ge(f_da.h, 128 * (p + 1))
                        f_acw.inc(e.dma_start(
                            out=x_dst[p0 * 1024:(p0 + 1) * 1024, :]
                                .rearrange("(t q) d -> q t d", q=128),
                            in_=accv[:, :, :]), 16)
                    e.wait_ge(f_acw.h, 16 * (pb + npass))
                block.sync(_sy)

                def _g(e):
                    for p0 in range(npass):
                        p = pb + p0
                        for u in range(64):
                            uu = 64 * p + u
                            if uu >= 2:
                                e.wait_ge(f_mmu.h, 2 * uu - 2)
                            sl = u % 2
                            f_wl[sl].inc(e.dma_start(
                                out=sW1[:, sl * 2048:(sl + 1) * 2048]
                                    .rearrange("p (dc j) -> p dc j", j=128),
                                in_=w1tg[u, :, :, :].rearrange("dc p j -> p dc j")), 16)
                            f_wl[sl].inc(e.dma_start(
                                out=sW3[:, sl * 2048:(sl + 1) * 2048]
                                    .rearrange("p (dc j) -> p dc j", j=128),
                                in_=w3tg[u, :, :, :].rearrange("dc p j -> p dc j")), 16)
                block.gpsimd(_g)

                def _t(e):
                    for p0 in range(npass):
                        p = pb + p0
                        e.wait_ge(f_tr.h, 16 * 16 * (p + 1))
                        for g in range(8):
                            for ul in range(8):
                                u = 8 * g + ul
                                uu = 64 * p + u
                                e.wait_ge(f_wl[uu % 2].h, 32 * (uu // 2 + 1))
                                e.wait_ge(f_se.h, 64 * p + u)
                                sl = u % 2
                                for dc in range(16):
                                    for ncx in range(2):
                                        ins = e.matmul(
                                            psg[:, ncx * 512:(ncx + 1) * 512],
                                            sW1[:, sl * 2048 + dc * 128:sl * 2048 + (dc + 1) * 128],
                                            l1t[:, dc * 1024 + ncx * 512:dc * 1024 + (ncx + 1) * 512],
                                            start=(dc == 0), stop=(dc == 15))
                                f_mmu.inc(ins, 1)
                                e.wait_ge(f_vm.h, 64 * p + u)
                                for dc in range(16):
                                    for ncx in range(2):
                                        ins = e.matmul(
                                            psl[:, ncx * 512:(ncx + 1) * 512],
                                            sW3[:, sl * 2048 + dc * 128:sl * 2048 + (dc + 1) * 128],
                                            l1t[:, dc * 1024 + ncx * 512:dc * 1024 + (ncx + 1) * 512],
                                            start=(dc == 0), stop=(dc == 15))
                                f_mmu.inc(ins, 1)
                            e.wait_ge(f_vm.h, 64 * p + 8 * (g + 1))
                            e.wait_ge(f_wf.h, 16 * (64 * p + 8 * (g + 1)))
                            for ti in range(8):
                                for hf in range(2):
                                    wv = 128 * p + 16 * g + 2 * ti - 1 + hf
                                    if wv > 0:
                                        e.wait_ge(f_da.h, wv)
                                    for ul2 in range(8):
                                        for nc2 in range(2):
                                            ins = e.matmul(
                                                psdh[hf][:, nc2 * 512:(nc2 + 1) * 512],
                                                hT[:, ul2 * 1024 + ti * 128:ul2 * 1024 + (ti + 1) * 128],
                                                wfg[:, ul2 * D + (2 * hf + nc2) * 512:ul2 * D + (2 * hf + nc2 + 1) * 512],
                                                start=(ul2 == 0), stop=(ul2 == 7))
                                    f_mmd.inc(ins, 1)
                block.tensor(_t)

                def ln_tile(e, q, ti):
                    def xload(ti):
                        kk = 8 * q + ti
                        f_xl[kk % 2].inc(e.dma_start(
                            out=xl[:, (kk % 2) * D:(kk % 2 + 1) * D],
                            in_=x_src[(q - pb) * 1024 + ti * 128:
                                      (q - pb) * 1024 + (ti + 1) * 128, :]), 16)
                    if ti == 0:
                        xload(0)
                        xload(1)
                    if True:
                        kk = 8 * q + ti
                        sl = (kk % 2) * D
                        e.wait_ge(f_xl[kk % 2].h, 16 * (kk // 2 + 1))
                        e.activation(out=junk[:, :], in_=xl[:, sl:sl + D],
                                     func=AF.Identity, accum_out=stt(ti, 0))
                        e.activation(out=stt(ti, 1), in_=stt(ti, 0),
                                     func=AF.Identity, scale=-1.0 / D)
                        e.activation(out=junk[:, :], in_=xl[:, sl:sl + D],
                                     func=AF.Square, bias=stt(ti, 1),
                                     accum_out=stt(ti, 4))
                        e.activation(out=stt(ti, 5), in_=stt(ti, 4),
                                     func=AF.Ln, scale=1.0 / D, bias=s_eps[:, 0:1])
                        e.activation(out=stt(ti, 3), in_=stt(ti, 5),
                                     func=AF.Exp, scale=-0.5)
                        e.activation(out=stt(ti, 2), in_=stt(ti, 1),
                                     func=AF.Identity, scale=stt(ti, 3))
                        if kk >= 2:
                            e.wait_ge(f_lnw[kk % 2].h, 16 * (kk // 2))
                        e.activation(
                            out=lnsb[:, sl:sl + D],
                            in_=xl[:, sl:sl + D], func=AF.Identity,
                            scale=stt(ti, 3), bias=stt(ti, 2))
                        f_lnw[kk % 2].inc(e.dma_start(
                            out=lnt_b[ti * 128:(ti + 1) * 128, :],
                            in_=lnsb[:, sl:sl + D]), 16)
                        if ti + 2 < 8:
                            xload(ti + 2)

                def ln_trans(e, q):
                    e.wait_ge(f_lnw[0].h, 64 * (q + 1))
                    e.wait_ge(f_lnw[1].h, 64 * (q + 1))
                    if q > 0:
                        e.wait_ge(f_mmu.h, 128 * q)
                    for dc in range(16):
                        f_tr.inc(e.dma_start(
                            out=l1t[:, dc * 1024:(dc + 1) * 1024],
                            in_=lnt_b[:, dc * 128:(dc + 1) * 128],
                            transpose=True), 16)

                def _s(e):
                    for ti in range(8):
                        ln_tile(e, pb, ti)
                    ln_trans(e, pb)
                    for p0 in range(npass):
                        p = pb + p0
                        for u in range(64):
                            uu = 64 * p + u
                            e.wait_ge(f_mmu.h, 2 * uu + 1)
                            if uu >= 2:
                                e.wait_ge(f_vm.h, uu - 1)
                            f_se.inc(e.activation(
                                out=sg[:, (u % 2) * 1024:(u % 2 + 1) * 1024],
                                in_=psg[:, :], func=AF.Silu), 1)
                            if p0 + 1 < npass and u % 8 == 7:
                                ln_tile(e, p + 1, u // 8)
                        if p0 + 1 < npass:
                            ln_trans(e, p + 1)
                bscalar(_s)

                def _v(e):
                    for p0 in range(npass):
                        p = pb + p0
                        for u in range(64):
                            uu = 64 * p + u
                            e.wait_ge(f_se.h, uu + 1)
                            e.wait_ge(f_mmu.h, 2 * uu + 2)
                            f_vm.inc(e.tensor_tensor(
                                out=hT[:, (u % 8) * 1024:(u % 8 + 1) * 1024],
                                in0=sg[:, (u % 2) * 1024:(u % 2 + 1) * 1024],
                                in1=psl[:, :], op=ALU.mult), 1)
                            if u % 8 == 7:
                                g = u // 8
                                for ti in range(8):
                                    for hf in range(2):
                                        e.wait_ge(f_mmd.h, 128 * p + 16 * g + 2 * ti + hf + 1)
                                        if g == 0 and ti == 0 and hf == 0:
                                            e.wait_ge(f_x.h, 16 * (p + 1))
                                        f_da.inc(e.tensor_tensor(
                                            out=accv[:, ti, hf * 1024:(hf + 1) * 1024],
                                            in0=accv[:, ti, hf * 1024:(hf + 1) * 1024],
                                            in1=psdh[hf][:, :], op=ALU.add), 1)
                bvector(_v)
                pop_scope()

        if 1 in stages:
            emit_ffn(x_in, x1s, S)

        if 2 in stages:
            with ExitStack() as c2:
                push_scope(c2)
                sb2 = lambda *a, **kw: c2.enter_context(nc.sbuf_tensor(*a, **kw))
                s_cosf = sb2("s_cosf", [128, 32 * DR], F32)
                s_sinf = sb2("s_sinf", [128, 32 * DR], F32)
                s_coso = sb2("s_coso", [128, 16 * DR], F32)
                s_sino = sb2("s_sino", [128, 16 * DR], F32)
                cnT = sb2("cnT", [128, 4 * S], BF16, is_transpose_dest=True)
                krT = sb2("krT", [128, S], BF16, is_transpose_dest=True)
                scol = sb2("scol", [128, 16], F32)
                tmp1 = sb2("tmp1", [128, 512], F32)
                tmp2 = sb2("tmp2", [128, 512], F32)
                a_c = mksem(octx, "a_c")
                b_fold = mksem(octx, "b_fold")

                def _cl(e):
                    for dst, src in ((s_cosf, cos_f), (s_sinf, sin_f),
                                     (s_coso, cos_o), (s_sino, sin_o)):
                        a_c.inc(e.dma_start(
                            out=dst[:].rearrange("p (t f) -> p t f", f=DR),
                            in_=src.rearrange("(t q) f -> q t f", q=128)), 16)
                block.sync(_cl)

                def rope6(e, srcE, srcO, cE, cO, sE, sO, dE, dO, w):
                    e.tensor_tensor(out=tmp1[:, 0:w], in0=srcE, in1=cE, op=ALU.mult)
                    e.tensor_tensor(out=tmp2[:, 0:w], in0=srcO, in1=sE, op=ALU.mult)
                    e.tensor_tensor(out=dE, in0=tmp1[:, 0:w], in1=tmp2[:, 0:w], op=ALU.subtract)
                    e.tensor_tensor(out=tmp1[:, 0:w], in0=srcO, in1=cO, op=ALU.mult)
                    e.tensor_tensor(out=tmp2[:, 0:w], in0=srcE, in1=sO, op=ALU.mult)
                    return e.tensor_tensor(out=dO, in0=tmp1[:, 0:w], in1=tmp2[:, 0:w], op=ALU.add)

                with ExitStack() as ca:
                    push_scope(ca)
                    sba = lambda *a, **kw: ca.enter_context(nc.sbuf_tensor(*a, **kw))
                    xc = sba("a_xc", [128, 8 * D], F32)
                    l2t = sba("a_l2t", [128, 2 * 16 * 512], BF16, is_transpose_dest=True)
                    lnsb2 = sba("a_lnsb", [128, 2 * D], BF16)
                    junk2 = sba("a_junk", [128, D], BF16)
                    wkvab = sba("a_wkvab", [128, 16 * (KVL + DR)], BF16)
                    cnsb = sba("a_cnsb", [128, 2 * KVL], BF16)
                    krf = sba("a_krf", [128, 2 * DR], F32)
                    krsb = sba("a_krsb", [128, 2 * 128], BF16)
                    ast = sba("a_stat", [128, 32 * 6], F32)
                    psk = ca.enter_context(nc.psum_tensor("a_psk", [128, 2 * 1024], F32))
                    a_x = [mksem(octx, "a_x0"), mksem(octx, "a_x1")]
                    a_vst = mksem(octx, "a_vst")
                    a_sr = mksem(octx, "a_sr")
                    a_sev = mksem(octx, "a_sev")
                    a_lnw = [mksem(octx, "a_lnw0"), mksem(octx, "a_lnw1")]
                    a_tr = mksem(octx, "a_tr")
                    a_kmm = mksem(octx, "a_kmm")
                    a_sr2 = mksem(octx, "a_sr2")
                    a_kev = mksem(octx, "a_kev")
                    a_kro = mksem(octx, "a_kro")
                    a_cnw = [mkdsem("a_cnw0"), mkdsem("a_cnw1")]
                    a_wk = mkdsem("a_wk")

                    def ss(k, j):
                        return ast[:, k * 6 + j:k * 6 + j + 1]

                    def _g(e):
                        a_wk.inc(e.dma_start(
                            out=wkvab[:].rearrange("p (dc f) -> p dc f", f=KVL + DR),
                            in_=wkva.rearrange("(dc p) f -> p dc f", p=128)), 16)
                        for k in range(32):
                            e.wait_ge(a_kev.h, k + 1)
                            e.wait_ge(a_kro.h, k + 1)
                            a_cnw[k % 2].inc(e.dma_start(
                                out=cn_b[k * 128:(k + 1) * 128, :],
                                in_=cnsb[:, (k % 2) * KVL:(k % 2 + 1) * KVL]), 16)
                            a_cnw[k % 2].inc(e.dma_start(
                                out=kr_b[k * 128:(k + 1) * 128, :],
                                in_=krsb[:, (k % 2) * 128:(k % 2 + 1) * 128]), 16)
                    block.gpsimd(_g)

                    def _sy(e):
                        for sc in range(8):
                            if sc >= 2:
                                e.wait_ge(a_sev.h, 4 * (sc - 1))
                            hx = (sc % 2) * 4
                            a_x[sc % 2].inc(e.dma_start(
                                out=xc[:, hx * D:(hx + 4) * D].rearrange("p (t d) -> p t d", d=D),
                                in_=x1s[sc * 512:(sc + 1) * 512, :].rearrange("(t q) d -> q t d", q=128)), 16)
                            for ti in range(4):
                                k = 4 * sc + ti
                                e.wait_ge(a_sev.h, k + 1)
                                a_lnw[k % 2].inc(e.dma_start(
                                    out=ln2_b[k * 128:(k + 1) * 128, :],
                                    in_=lnsb2[:, (k % 2) * D:(k % 2 + 1) * D]), 16)
                        e.wait_ge(a_cnw[0].h, 32 * 16)
                        e.wait_ge(a_cnw[1].h, 32 * 16)
                    block.sync(_sy)

                    def _v(e):
                        e.wait_ge(a_c.h, 64)
                        e.memset(krsb[:, DR:128], 0.0)
                        e.memset(krsb[:, 128 + DR:256], 0.0)

                        def stats_chunk(e, sc):
                            e.wait_ge(a_x[sc % 2].h, 16 * (sc // 2 + 1))
                            hx = (sc % 2) * 4
                            for ti in range(4):
                                k = 4 * sc + ti
                                e.tensor_reduce(ss(k, 0), xc[:, (hx + ti) * D:(hx + ti + 1) * D],
                                                axis=AX.X, op=ALU.add)
                                a_vst.inc(e.tensor_scalar(
                                    out=ss(k, 1), in0=ss(k, 0),
                                    scalar1=-1.0 / D, scalar2=None, op0=ALU.mult), 1)
                                e.wait_ge(a_sr.h, k + 1)
                                a_vst.inc(e.tensor_tensor(
                                    out=ss(k, 2), in0=ss(k, 1), in1=ss(k, 3), op=ALU.mult), 1)

                        def evac_rope_chunk(e, sc):
                            for ti in range(4):
                                k = 4 * sc + ti
                                e.wait_ge(a_sr2.h, k + 1)
                                if k >= 2:
                                    e.wait_ge(a_cnw[k % 2].h, 32 * (k // 2))
                                kp = (k % 2) * 1024
                                e.tensor_scalar(
                                    out=cnsb[:, (k % 2) * KVL:(k % 2 + 1) * KVL],
                                    in0=psk[:, kp:kp + KVL],
                                    scalar1=ss(k, 2), scalar2=None, op0=ALU.mult)
                                a_kev.inc(e.tensor_copy(
                                    out=krf[:, (k % 2) * DR:(k % 2 + 1) * DR],
                                    in_=psk[:, kp + KVL:kp + KVL + DR]), 1)
                                kbf = (k % 2) * DR
                                kb = (k % 2) * 128
                                cs_ = s_cosf[:, k * DR:(k + 1) * DR]
                                sn_ = s_sinf[:, k * DR:(k + 1) * DR]
                                a_kro.inc(rope6(
                                    e,
                                    krf[:, kbf + 0:kbf + DR:2], krf[:, kbf + 1:kbf + DR:2],
                                    cs_[:, 0:DR:2], cs_[:, 1:DR:2],
                                    sn_[:, 0:DR:2], sn_[:, 1:DR:2],
                                    krsb[:, kb + 0:kb + DR:2], krsb[:, kb + 1:kb + DR:2],
                                    DR // 2), 1)

                        for sc in range(8):
                            stats_chunk(e, sc)
                            if sc >= 2:
                                evac_rope_chunk(e, sc - 2)
                        evac_rope_chunk(e, 6)
                        evac_rope_chunk(e, 7)
                    bvector(_v)

                    def _s(e):
                        def atrans(e, sc):
                            e.wait_ge(a_lnw[0].h, 16 * 2 * (sc + 1))
                            e.wait_ge(a_lnw[1].h, 16 * 2 * (sc + 1))
                            if sc >= 1:
                                e.wait_ge(a_tr.h, 256 * sc)
                            if sc >= 2:
                                e.wait_ge(a_kmm.h, 4 * (sc - 1))
                            for dc in range(16):
                                a_tr.inc(e.dma_start(
                                    out=l2t[:, (sc % 2) * 8192 + dc * 512:(sc % 2) * 8192 + (dc + 1) * 512],
                                    in_=ln2_b[sc * 512:(sc + 1) * 512, dc * 128:(dc + 1) * 128],
                                    transpose=True), 16)

                        def ln_chunk(e, sc):
                            hx = (sc % 2) * 4
                            for ti in range(4):
                                k = 4 * sc + ti
                                e.wait_ge(a_x[sc % 2].h, 16 * (sc // 2 + 1))
                                e.wait_ge(a_vst.h, 2 * k + 1)
                                e.activation(out=junk2[:, :], in_=xc[:, (hx + ti) * D:(hx + ti + 1) * D],
                                             func=AF.Square, bias=ss(k, 1), accum_out=ss(k, 4))
                                e.activation(out=ss(k, 5), in_=ss(k, 4), func=AF.Ln,
                                             scale=1.0 / D, bias=s_eps[:, 0:1])
                                a_sr.inc(e.activation(out=ss(k, 3), in_=ss(k, 5),
                                                      func=AF.Exp, scale=-0.5), 1)
                                e.wait_ge(a_vst.h, 2 * k + 2)
                                if k >= 2:
                                    e.wait_ge(a_lnw[k % 2].h, 16 * (k // 2))
                                a_sev.inc(e.activation(
                                    out=lnsb2[:, (k % 2) * D:(k % 2 + 1) * D],
                                    in_=xc[:, (hx + ti) * D:(hx + ti + 1) * D],
                                    func=AF.Identity, scale=ss(k, 3), bias=ss(k, 2)), 1)

                        def rms_chunk(e, sc):
                            for ti in range(4):
                                k = 4 * sc + ti
                                e.wait_ge(a_kmm.h, k + 1)
                                kp = (k % 2) * 1024
                                e.activation(out=junk2[:, 0:KVL], in_=psk[:, kp:kp + KVL],
                                             func=AF.Square, accum_out=ss(k, 0))
                                e.activation(out=ss(k, 1), in_=ss(k, 0), func=AF.Ln,
                                             scale=1.0 / KVL, bias=s_eps[:, 1:2])
                                a_sr2.inc(e.activation(out=ss(k, 2), in_=ss(k, 1),
                                                       func=AF.Exp, scale=-0.5), 1)

                        for sc in range(8):
                            ln_chunk(e, sc)
                            if sc >= 2:
                                rms_chunk(e, sc - 2)
                            atrans(e, sc)
                        rms_chunk(e, 6)
                        rms_chunk(e, 7)
                    bscalar(_s)

                    def _t(e):
                        e.wait_ge(a_wk.h, 16)
                        for sc in range(8):
                            e.wait_ge(a_tr.h, 16 * 16 * (sc + 1))
                            for ti in range(4):
                                k = 4 * sc + ti
                                if k >= 2:
                                    e.wait_ge(a_kev.h, k - 1)
                                    e.wait_ge(a_sr2.h, k - 1)
                                kp = (k % 2) * 1024
                                base = (sc % 2) * 8192
                                for dc in range(16):
                                    lh = l2t[:, base + dc * 512 + ti * 128:base + dc * 512 + (ti + 1) * 128]
                                    e.matmul(psk[:, kp:kp + 512], lh,
                                             wkvab[:, dc * (KVL + DR):dc * (KVL + DR) + 512],
                                             start=(dc == 0), stop=(dc == 15))
                                    ins = e.matmul(psk[:, kp + 512:kp + 576], lh,
                                                   wkvab[:, dc * (KVL + DR) + 512:dc * (KVL + DR) + 576],
                                                   start=(dc == 0), stop=(dc == 15))
                                a_kmm.inc(ins, 1)
                    block.tensor(_t)
                    pop_scope()

                cbc = ExitStack()
                qaT = cbc.enter_context(nc.sbuf_tensor("qaT", [128, 6 * TOWN], BF16))

                with ExitStack() as cb:
                    push_scope(cb)
                    sbb = lambda *a, **kw: cb.enter_context(nc.sbuf_tensor(*a, **kw))
                    lgo = sbb("b_lgo", [128, 2 * D], BF16)
                    l2ot = sbb("b_l2ot", [128, 16 * 1024], BF16, is_transpose_dest=True)
                    wqab = sbb("b_wqab", [128, 16 * QL], BF16)
                    sqaT = sbb("b_sqaT", [128, 6 * TOWN], BF16)
                    qnwb = sbb("b_qnwb", [128, 6], F32)
                    sc1 = sbb("b_sc1", [128, 16], F32)
                    psq = cb.enter_context(nc.psum_tensor("b_psq", [128, 2 * 512], F32))
                    pcs = cb.enter_context(nc.psum_tensor("b_pcs", [128, 16], F32))
                    b_tr = mksem(octx, "b_tr")
                    b_gth = [mkdsem("b_gth0"), mkdsem("b_gth1")]
                    b_gw = [mksem(octx, "b_gw0"), mksem(octx, "b_gw1")]
                    b_wq = mkdsem("b_wq")
                    b_mm = mksem(octx, "b_mm")
                    b_ev = mksem(octx, "b_ev")
                    b_cs = mksem(octx, "b_cs")
                    b_sc = mksem(octx, "b_sc")

                    def _sy(e):
                        for j in range(4):
                            b_tr.inc(e.dma_start(out=cnT[:, j * S:(j + 1) * S],
                                                 in_=cn_b[:, j * 128:(j + 1) * 128],
                                                 transpose=True), 16)
                        b_tr.inc(e.dma_start(out=krT[:, :], in_=kr_b[:, :],
                                             transpose=True), 16)
                        for j in range(16):
                            e.wait_ge(b_gth[j % 2].h, 16 * (j // 2 + 1))
                            b_gw[j % 2].inc(e.dma_start(
                                out=lno_b[j * 128:(j + 1) * 128, :],
                                in_=lgo[:, (j % 2) * D:(j % 2 + 1) * D]), 16)
                        for half in range(2):
                            e.wait_ge(b_gw[0].h, 16 * 8)
                            e.wait_ge(b_gw[1].h, 16 * 8)
                            if half == 1:
                                e.wait_ge(b_mm.h, 12)
                            for dc in range(16):
                                b_tr.inc(e.dma_start(
                                    out=l2ot[:, dc * 1024:(dc + 1) * 1024],
                                    in_=lno_b[half * 1024:(half + 1) * 1024,
                                              dc * 128:(dc + 1) * 128],
                                    transpose=True), 16)
                        e.wait_ge(b_tr.h, 16 * 37)
                    block.sync(_sy)

                    def _g(e):
                        b_wq.inc(e.dma_start(
                            out=wqab[:].rearrange("p (dc f) -> p dc f", f=QL),
                            in_=wqa.rearrange("(dc p) f -> p dc f", p=128)), 16)
                        b_wq.inc(e.dma_start(out=qnwb[:, :], in_=qnw[:, :]), 16)
                        for j in range(16):
                            if j >= 2:
                                e.wait_ge(b_gw[j % 2].h, 16 * (j // 2))
                            b_gth[j % 2].inc(e.indirect_dma_start(
                                out=lgo[:, (j % 2) * D:(j % 2 + 1) * D], out_offset=None,
                                in_=ln2_b[:, :],
                                in_offset=IOffs(ap=s_ridx[:, j:j + 1], axis=0)), 16)
                    block.gpsimd(_g)

                    def _t(e):
                        e.wait_ge(b_wq.h, 32)
                        idx = 0
                        for half in range(2):
                            e.wait_ge(b_tr.h, 16 * (5 + 16 * (half + 1)))
                            for m6 in range(6):
                                for ncx in range(2):
                                    if idx >= 2:
                                        e.wait_ge(b_ev.h, 2 * (idx - 1))
                                    for dc in range(16):
                                        ins = e.matmul(
                                            psq[:, (idx % 2) * 512:(idx % 2 + 1) * 512],
                                            wqab[:, dc * QL + m6 * 128:dc * QL + (m6 + 1) * 128],
                                            l2ot[:, dc * 1024 + ncx * 512:dc * 1024 + (ncx + 1) * 512],
                                            start=(dc == 0), stop=(dc == 15))
                                    b_mm.inc(ins, 1)
                                    idx += 1
                        e.wait_ge(b_ev.h, 48)
                        for sub in range(16):
                            for k6 in range(6):
                                ins = e.matmul(pcs[:, sub:sub + 1],
                                               sqaT[:, k6 * TOWN + sub * 128:k6 * TOWN + (sub + 1) * 128],
                                               s_1b[:, 0:1],
                                               start=(k6 == 0), stop=(k6 == 5))
                        b_cs.inc(ins, 1)
                    block.tensor(_t)

                    def _s(e):
                        idx = 0
                        for half in range(2):
                            for m6 in range(6):
                                for ncx in range(2):
                                    cidx = 2 * half + ncx
                                    e.wait_ge(b_mm.h, idx + 1)
                                    e.activation(
                                        out=qaT[:, m6 * TOWN + cidx * 512:m6 * TOWN + (cidx + 1) * 512],
                                        in_=psq[:, (idx % 2) * 512:(idx % 2 + 1) * 512],
                                        func=AF.Copy)
                                    b_ev.inc(e.activation(
                                        out=sqaT[:, m6 * TOWN + cidx * 512:m6 * TOWN + (cidx + 1) * 512],
                                        in_=psq[:, (idx % 2) * 512:(idx % 2 + 1) * 512],
                                        func=AF.Square), 2)
                                    idx += 1
                        e.wait_ge(b_cs.h, 1)
                        e.activation(out=sc1[:, :], in_=pcs[:, :], func=AF.Ln,
                                     scale=1.0 / QL, bias=s_eps[:, 1:2])
                        b_sc.inc(e.activation(out=scol[:, :], in_=sc1[:, :],
                                              func=AF.Exp, scale=-0.5, bias=s_eps[:, 2:3]), 1)
                    bscalar(_s)

                    def _v(e):
                        e.wait_ge(b_cs.h, 1)
                        e.wait_ge(b_wq.h, 32)
                        for k6 in range(6):
                            ins = e.tensor_scalar(out=qaT[:, k6 * TOWN:(k6 + 1) * TOWN],
                                                  in0=qaT[:, k6 * TOWN:(k6 + 1) * TOWN],
                                                  scalar1=qnwb[:, k6:k6 + 1], scalar2=None,
                                                  op0=ALU.mult)
                        b_fold.inc(ins, 1)
                        e.memset(krT[64:65, :], 1.0)
                    bvector(_v)
                    pop_scope()

                with ExitStack() as cg:
                    push_scope(cg)
                    sbg = lambda *a, **kw: cg.enter_context(nc.sbuf_tensor(*a, **kw))
                    wkvbb = sbg("g_wkvbb", [128, 4 * NH * 256], BF16)
                    kvnwb = sbg("g_kvnwb", [128, 4], F32)
                    wqbg = sbg("g_wqbg", [128, 2 * 6 * 384], BF16)
                    qg = sbg("g_qg", [128, 16 * 384], BF16)
                    qrc = sbg("g_qrc", [128, 16 * 256], BF16)
                    qs2 = sbg("g_qs2", [128, 32], F32)
                    sqq = sbg("g_sqq", [128, 32], F32)
                    cn_ = sbg("g_cn", [128, 32], F32)
                    junk3 = sbg("g_junk", [128, 512], BF16)
                    kst = sbg("g_kst", [128, 8], F32)
                    vjunk = sbg("g_vjunk", [128, 192], BF16)
                    qnT = sbg("g_qnT", [128, 2 * TOWN], BF16, is_transpose_dest=True)
                    qrT = sbg("g_qrT", [128, 2 * TOWN], BF16, is_transpose_dest=True)
                    knT = sbg("g_knT", [128, S], BF16)
                    Vh = sbg("g_Vh", [128, 32 * 132], BF16)
                    PT = sbg("g_PT", [128, 2 * 1024], BF16)
                    og = sbg("g_og", [128, 2 * 16 * 128], BF16)
                    kq = sbg("g_kq", [128, 1], F32)
                    kr64 = sbg("g_kr64", [128, 1], F32)
                    krs = sbg("g_krs", [1, 1], F32)
                    tta = sbg("g_tta", [1, 1], F32)
                    rks = sbg("g_rks", [1, 8], F32)
                    rkc = sbg("g_rkc", [128, 1], F32)
                    rcp = sbg("g_rcp", [128, 2], F32)
                    PS = cg.enter_context(nc.psum_tensor("g_ps", [128, 4096], F32))
                    def vA(s2):
                        return PS[:, 2048 + s2 * 512:2048 + s2 * 512 + 129]

                    def vQ(sub):
                        return PS[:, 3072 + (sub % 2) * 512:3072 + (sub % 2) * 512 + 384]

                    psn_s = [PS[:, 0:512], PS[:, 1024:1536]]
                    vVs = [PS[:, 512:640], PS[:, 1536:1664]]
                    ptiny = PS[:, 2304:2320]
                    g_wk = mkdsem("g_wk")
                    g_wkf = mksem(octx, "g_wkf")
                    g_wq = [mkdsem("g_wq0"), mkdsem("g_wq1")]
                    g_qmm = mksem(octx, "g_qmm")
                    g_qev = mksem(octx, "g_qev")
                    g_qs = mksem(octx, "g_qs")
                    g_sq = mksem(octx, "g_sq")
                    g_rope = mksem(octx, "g_rope")
                    g_cw = mksem(octx, "g_cw")
                    g_qw = mksem(octx, "g_qw")
                    g_qtr = mksem(octx, "g_qtr")
                    g_kmm = mksem(octx, "g_kmm")
                    g_kev = mksem(octx, "g_kev")
                    g_vev = mksem(octx, "g_vev")
                    g_bc = mksem(octx, "g_bc")
                    g_rk = mksem(octx, "g_rk")
                    g_rkb = mksem(octx, "g_rkb")
                    g_smm = mksem(octx, "g_smm")
                    g_sex = mksem(octx, "g_sex")
                    g_msk = mksem(octx, "g_msk")
                    g_amm = mksem(octx, "g_amm")
                    g_rc = mksem(octx, "g_rc")
                    g_oev = mksem(octx, "g_oev")
                    g_ow = mksem(octx, "g_ow")
                    g_kr0 = mksem(octx, "g_kr0")
                    g_kr1 = mksem(octx, "g_kr1")

                    P_B, P_MSK, P_HLAST = {}, {}, {}
                    P_kmmK, P_kmmV = {}, {}
                    n_b = n_mskb = n_kmm = 0
                    for grp in range(NG):
                        for h2 in range(2):
                            for ncx in range(8):
                                n_kmm += 1
                                P_kmmK[(grp, h2, ncx)] = n_kmm
                            for vs in range(32):
                                n_kmm += 1
                                P_kmmV[(grp, h2, vs)] = n_kmm
                            for g8 in range(8):
                                for bj in range(g8 + 1):
                                    n_b += 1
                                    P_B[(grp, h2, g8, bj)] = n_b
                                    if bj == g8:
                                        n_mskb += 1
                                        P_MSK[(grp, h2, g8)] = n_mskb
                            P_HLAST[(grp, h2)] = P_B[(grp, h2, 7, 7)]

                    def P_kev(grp, h2, j):
                        return 9 * (2 * grp + h2) + j + 1

                    def P_vev(grp, h2, vs):
                        return 32 * (2 * grp + h2) + vs + 1

                    def P_oev(grp, h2, g8, s2):
                        return 16 * (2 * grp + h2) + 2 * g8 + s2 + 1

                    def P_rc(grp, h2, g8):
                        return 8 * (2 * grp + h2) + g8 + 1

                    def _g(e):
                        g_wk.inc(e.dma_start(out=kvnwb[:, :], in_=kvnw[:, :]), 16)
                        g_wk.inc(e.dma_start(
                            out=wkvbb[:].rearrange("p (kc f) -> p kc f", f=NH * 256),
                            in_=wkvb.rearrange("(kc p) f -> p kc f", p=128)), 16)
                        wqbr = wqb.rearrange("(k p) f -> p k f", p=128)
                        for grp in range(NG):
                            if grp >= 2:
                                e.wait_ge(g_qmm.h, 16 * (grp - 1))
                            g_wq[grp % 2].inc(e.dma_start(
                                out=wqbg[:, (grp % 2) * 2304:(grp % 2 + 1) * 2304]
                                    .rearrange("p (k f) -> p k f", f=384),
                                in_=wqbr[:, :, grp * 384:(grp + 1) * 384]), 16)
                    block.gpsimd(_g)

                    def _vw(e):
                        e.wait_ge(g_wk.h, 32)
                        for kc in range(4):
                            ins = e.tensor_scalar(
                                out=wkvbb[:, kc * NH * 256:(kc + 1) * NH * 256],
                                in0=wkvbb[:, kc * NH * 256:(kc + 1) * NH * 256],
                                scalar1=kvnwb[:, kc:kc + 1], scalar2=None, op0=ALU.mult)
                        g_wkf.inc(ins, 1)
                    bvector(_vw)

                    def _t(e):
                        e.wait_ge(g_wkf.h, 1)
                        e.wait_ge(g_kr0.h, 1)
                        ins = e.matmul(ptiny[0:1, 3:4], kr64[0:64, 0:1], s_1f[0:64, 0:1],
                                       start=True, stop=True)
                        g_kr1.inc(ins, 1)

                        def emit_av(e, key):
                            grp_, h2_, g8_, bj_ = key
                            b = P_B[key]
                            ntt = 4 * (g8_ + 1)
                            if bj_ == g8_:
                                e.wait_ge(g_msk.h, P_MSK[(grp_, h2_, g8_)])
                            else:
                                e.wait_ge(g_sex.h, b)
                            if bj_ == 0 and (grp_, h2_, g8_) != (0, 0, 0):
                                pg8, ph2, pgrp = g8_ - 1, h2_, grp_
                                if pg8 < 0:
                                    pg8 = 7
                                    ph2 -= 1
                                    if ph2 < 0:
                                        ph2, pgrp = 1, pgrp - 1
                                e.wait_ge(g_oev.h, P_oev(pgrp, ph2, pg8, 1))
                            sb = (b % 2) * 1024
                            for p_ in range(2):
                                for k_ in range(2):
                                    tt = 4 * bj_ + 2 * p_ + k_
                                    for s2 in range(2):
                                        ins = e.matmul(
                                            vA(s2),
                                            PT[:, sb + p_ * 512 + k_ * 256 + s2 * 128:
                                               sb + p_ * 512 + k_ * 256 + (s2 + 1) * 128],
                                            Vh[:, tt * 132:tt * 132 + 129],
                                            start=(tt == 0), stop=(tt == ntt - 1))
                            g_amm.inc(ins, 1)

                        for grp in range(NG):
                            e.wait_ge(g_wq[grp % 2].h, 16 * (grp // 2 + 1))
                            e.wait_ge(b_fold.h, 1)
                            for sub in range(16):
                                gi = 16 * grp + sub
                                if gi >= 2:
                                    e.wait_ge(g_qev.h, gi - 1)
                                for k6 in range(6):
                                    ins = e.matmul(
                                        vQ(sub),
                                        qaT[:, k6 * TOWN + sub * 128:k6 * TOWN + (sub + 1) * 128],
                                        wqbg[:, (grp % 2) * 2304 + k6 * 384:(grp % 2) * 2304 + (k6 + 1) * 384],
                                        start=(k6 == 0), stop=(k6 == 5))
                                g_qmm.inc(ins, 1)
                            for h2 in range(2):
                                h = 2 * grp + h2
                                if h > 0:
                                    e.wait_ge(g_sex.h, P_HLAST[(grp, h2 - 1) if h2 else (grp - 1, 1)])
                                for ncx in range(8):
                                    e.wait_ge(g_kev.h, 9 * h + max(0, ncx - 1))
                                    for kc in range(4):
                                        ins = e.matmul(
                                            psn_s[ncx % 2],
                                            wkvbb[:, kc * NH * 256 + h * 256:kc * NH * 256 + h * 256 + 128],
                                            cnT[:, kc * S + ncx * 512:kc * S + (ncx + 1) * 512],
                                            start=(kc == 0), stop=(kc == 3))
                                    g_kmm.inc(ins, 1)
                                for vs in range(32):
                                    if vs >= 2:
                                        e.wait_ge(g_vev.h, P_vev(grp, h2, vs - 2))
                                    for kc in range(4):
                                        ins = e.matmul(
                                            vVs[vs % 2],
                                            cnT[:, kc * S + vs * 128:kc * S + (vs + 1) * 128],
                                            wkvbb[:, kc * NH * 256 + h * 256 + 128:kc * NH * 256 + (h + 1) * 256],
                                            start=(kc == 0), stop=(kc == 3))
                                    g_kmm.inc(ins, 1)
                                if h2 == 0:
                                    e.wait_ge(g_kev.h, P_kev(grp, 0, 8))
                                    if grp >= 1:
                                        e.wait_ge(g_rkb.h, 2 * grp - 1)
                                        e.wait_ge(g_oev.h, P_oev(grp - 1, 1, 7, 1))
                                    ins = e.matmul(ptiny[0:1, 0:1], kq[:, 0:1], s_1f[:, 0:1],
                                                   start=True, stop=True)
                                    g_bc.inc(ins, 1)
                                    e.wait_ge(g_rk.h, grp + 1)
                                    ins = e.matmul(ptiny[:, 2:3], s_1f[0:1, :], rks[0:1, grp:grp + 1],
                                                   start=True, stop=True)
                                    g_bc.inc(ins, 1)
                                    e.wait_ge(g_qtr.h, 16 * 4 * (grp + 1))
                                e.wait_ge(g_kev.h, 9 * (2 * grp + h2) + 8)
                                e.wait_ge(g_vev.h, 32 * (2 * grp + h2) + 32)
                                pend = None
                                for g8 in range(8):
                                    qsl = qnT[:, h2 * TOWN + g8 * 256:h2 * TOWN + (g8 + 1) * 256]
                                    qrl = qrT[0:65, h2 * TOWN + g8 * 256:h2 * TOWN + (g8 + 1) * 256]
                                    for bj in range(g8 + 1):
                                        b = P_B[(grp, h2, g8, bj)]
                                        if b >= 3:
                                            e.wait_ge(g_sex.h, b - 2)
                                        sb = (b % 2) * 1024
                                        for p_ in range(2):
                                            for k_ in range(2):
                                                tt = 4 * bj + 2 * p_ + k_
                                                sl = sb + p_ * 512 + k_ * 256
                                                e.matmul(PS[:, sl:sl + 256],
                                                         knT[:, tt * 128:(tt + 1) * 128],
                                                         qsl, start=True, stop=False)
                                                ins = e.matmul(PS[:, sl:sl + 256],
                                                               krT[0:65, tt * 128:(tt + 1) * 128],
                                                               qrl, start=False, stop=True)
                                        g_smm.inc(ins, 1)
                                        if pend is not None:
                                            emit_av(e, pend)
                                        pend = (grp, h2, g8, bj)
                                emit_av(e, pend)
                    block.tensor(_t)

                    def _s(e):
                        e.wait_ge(b_fold.h, 1)
                        for ncx in range(8):
                            e.activation(out=junk3[0:64, 0:512],
                                         in_=krT[0:64, ncx * 512:(ncx + 1) * 512],
                                         func=AF.Square, accum_out=kst[0:64, ncx:ncx + 1])
                        g_kr0.inc(e.activation(out=junk3[0:64, 0:8], in_=kst[0:64, 0:8],
                                               func=AF.Identity, accum_out=kr64[0:64, 0:1]), 1)

                        def kv_evacs(e, grp, h2):
                            for ncx in range(8):
                                e.wait_ge(g_kmm.h, P_kmmK[(grp, h2, ncx)])
                                g_kev.inc(e.activation(
                                    out=knT[:, ncx * 512:(ncx + 1) * 512],
                                    in_=psn_s[ncx % 2], func=AF.Copy), 1)
                            for ncx in range(8):
                                e.activation(out=junk3[:, 0:512],
                                             in_=knT[:, ncx * 512:(ncx + 1) * 512],
                                             func=AF.Square, accum_out=kst[:, ncx:ncx + 1])
                            g_kev.inc(e.activation(out=junk3[:, 0:8], in_=kst[:, 0:8],
                                                   func=AF.Identity, accum_out=kq[:, 0:1]), 1)

                        def exps_oev(e, grp, h2):
                            if grp >= 1:
                                e.wait_ge(g_ow.h, 32 * grp)
                            for g8 in range(8):
                                for bj in range(g8 + 1):
                                    b = P_B[(grp, h2, g8, bj)]
                                    e.wait_ge(g_smm.h, b)
                                    if b >= 3:
                                        e.wait_ge(g_amm.h, b - 2)
                                    ins = e.raw.activation(
                                        out=PT[:, (b % 2) * 1024:(b % 2 + 1) * 1024],
                                        in_=PS[:, (b % 2) * 1024:(b % 2 + 1) * 1024],
                                        func=AF.Exp)
                                    ins.then_inc(g_sex.h, 1)
                                    g_sex.n += 1
                                e.wait_ge(g_amm.h, P_B[(grp, h2, g8, g8)])
                                e.wait_ge(g_rc.h, P_rc(grp, h2, g8))
                                for s2 in range(2):
                                    g_oev.inc(e.activation(
                                        out=og[:, (h2 * 16 + g8 * 2 + s2) * 128:(h2 * 16 + g8 * 2 + s2 + 1) * 128],
                                        in_=vA(s2)[:, 0:128],
                                        func=AF.Identity, scale=rcp[:, s2:s2 + 1]), 1)

                        for grp in range(NG):
                            for sub in range(16):
                                gi = 16 * grp + sub
                                e.wait_ge(g_qmm.h, gi + 1)
                                if grp >= 1 and sub == 0:
                                    e.wait_ge(g_qw.h, 64 * grp)
                                g_qev.inc(e.activation(
                                    out=qg[:, sub * 384:(sub + 1) * 384],
                                    in_=vQ(sub),
                                    func=AF.Identity, scale=scol[:, sub:sub + 1]), 1)
                            kv_evacs(e, grp, 0)
                            e.wait_ge(g_qs.h, grp + 1)
                            g_sq.inc(e.activation(out=sqq[:, :], in_=qs2[:, :], func=AF.Sqrt), 1)
                            e.wait_ge(g_rkb.h, 2 * grp + 1)
                            g_rk.inc(e.activation(out=rks[0:1, grp:grp + 1], in_=tta[0:1, 0:1],
                                                  func=AF.Sqrt, scale=1.0 / S), 1)
                            exps_oev(e, grp, 0)
                            kv_evacs(e, grp, 1)
                            exps_oev(e, grp, 1)
                    bscalar(_s)

                    def _v(e):
                        e.wait_ge(a_c.h, 64)
                        e.wait_ge(c_ld.h, 16 * 6)
                        e.memset(Vh[:, :].rearrange("p (v c) -> p v c", c=132)[:, :, 128:132], 1.0)
                        e.memset(qrc[:, :].rearrange("p (s f) -> p s f", f=128)[:, :, 65:128], 0.0)
                        e.wait_ge(g_kr1.h, 1)
                        e.tensor_copy(out=krs[0:1, 0:1], in_=ptiny[0:1, 3:4])

                        def v_evacs(e, grp, h2):
                            for vs in range(32):
                                e.wait_ge(g_kmm.h, P_kmmV[(grp, h2, vs)])
                                g_vev.inc(e.tensor_copy(
                                    out=Vh[:, vs * 132:vs * 132 + 128],
                                    in_=vVs[vs % 2]), 1)

                        def masks_recips(e, grp, h2):
                            for g8 in range(8):
                                bd = P_B[(grp, h2, g8, g8)]
                                e.wait_ge(g_sex.h, bd)
                                ins = e.raw.tensor_tensor(
                                    out=PT[:, (bd % 2) * 1024:(bd % 2 + 1) * 1024],
                                    in0=PT[:, (bd % 2) * 1024:(bd % 2 + 1) * 1024],
                                    in1=s_mask[:, 0:1024],
                                    op=ALU.mult)
                                ins.then_inc(g_msk.h, 1)
                                g_msk.n += 1
                                e.wait_ge(g_amm.h, bd)
                                e.reciprocal(out=rcp[:, 0:1], in_=vA(0)[:, 128:129])
                                g_rc.inc(e.reciprocal(out=rcp[:, 1:2], in_=vA(1)[:, 128:129]), 1)

                        qv = qg[:, :].rearrange("p (s f) -> p s f", f=384)
                        qrv = qrc[:, :].rearrange("p (s f) -> p s f", f=256)
                        co = s_coso[:, :].rearrange("p (s f) -> p s f", f=DR)
                        si_ = s_sino[:, :].rearrange("p (s f) -> p s f", f=DR)
                        for grp in range(NG):
                            for sub in range(16):
                                e.wait_ge(g_qev.h, 16 * grp + sub + 1)
                                for h2 in range(2):
                                    e.tensor_tensor(
                                        out=vjunk[:, 0:192],
                                        in0=qg[:, sub * 384 + h2 * 192:sub * 384 + (h2 + 1) * 192],
                                        in1=qg[:, sub * 384 + h2 * 192:sub * 384 + (h2 + 1) * 192],
                                        op=ALU.mult)
                                    ins = e.tensor_reduce(
                                        qs2[:, h2 * 16 + sub:h2 * 16 + sub + 1],
                                        vjunk[:, 0:192], axis=AX.X, op=ALU.add)
                            g_qs.inc(ins, 1)
                            if grp >= 1:
                                e.wait_ge(g_qw.h, 64 * grp)
                            for h2 in range(2):
                                g_rope.inc(rope6(
                                    e,
                                    qv[:, :, h2 * 192 + 128 + 0:h2 * 192 + 192:2],
                                    qv[:, :, h2 * 192 + 128 + 1:h2 * 192 + 192:2],
                                    co[:, :, 0:DR:2], co[:, :, 1:DR:2],
                                    si_[:, :, 0:DR:2], si_[:, :, 1:DR:2],
                                    qrv[:, :, h2 * 128 + 0:h2 * 128 + 64:2],
                                    qrv[:, :, h2 * 128 + 1:h2 * 128 + 64:2],
                                    512), 1)
                            v_evacs(e, grp, 0)
                            e.wait_ge(g_bc.h, 2 * grp + 1)
                            e.tensor_copy(out=tta[0:1, 0:1], in_=ptiny[0:1, 0:1])
                            g_rkb.inc(e.tensor_tensor(out=tta[0:1, 0:1], in0=tta[0:1, 0:1],
                                                      in1=krs[0:1, 0:1], op=ALU.add), 1)
                            e.wait_ge(g_bc.h, 2 * grp + 2)
                            g_rkb.inc(e.tensor_scalar(out=rkc[:, 0:1], in0=ptiny[:, 2:3],
                                                      scalar1=-C_COEF, scalar2=None,
                                                      op0=ALU.mult), 1)
                            e.wait_ge(g_sq.h, grp + 1)
                            e.tensor_scalar(out=cn_[:, :], in0=sqq[:, :],
                                            scalar1=rkc[:, 0:1], scalar2=-80.0,
                                            op0=ALU.mult, op1=ALU.max)
                            for h2 in range(2):
                                g_cw.inc(e.tensor_copy(
                                    out=qrv[:, :, h2 * 128 + 64:h2 * 128 + 65],
                                    in_=cn_[:, :].rearrange("p (h s) -> p s h", h=2)[:, :, h2:h2 + 1]), 1)
                            masks_recips(e, grp, 0)
                            v_evacs(e, grp, 1)
                            masks_recips(e, grp, 1)
                    bvector(_v)

                    def _sy(e):
                        for grp in range(NG):
                            e.wait_ge(g_qev.h, 16 * (grp + 1))
                            e.wait_ge(g_rope.h, 2 * (grp + 1))
                            e.wait_ge(g_cw.h, 2 * (grp + 1))
                            if grp >= 1:
                                e.wait_ge(g_amm.h, P_HLAST[(grp - 1, 1)])
                            qv = qg[:, :].rearrange("p (s f) -> p s f", f=384)
                            qrv = qrc[:, :].rearrange("p (s f) -> p s f", f=256)
                            for h2 in range(2):
                                g_qw.inc(e.dma_start(
                                    out=qn_b[h2, :, :].rearrange("(s p) f -> p s f", p=128),
                                    in_=qv[:, :, h2 * 192:h2 * 192 + 128]), 16)
                                g_qw.inc(e.dma_start(
                                    out=qr_b[h2, :, :].rearrange("(s p) f -> p s f", p=128),
                                    in_=qrv[:, :, h2 * 128:(h2 + 1) * 128]), 16)
                            e.wait_ge(g_qw.h, 64 * (grp + 1))
                            for h2 in range(2):
                                g_qtr.inc(e.dma_start(
                                    out=qnT[:, h2 * TOWN:(h2 + 1) * TOWN],
                                    in_=qn_b[h2, :, :], transpose=True), 16)
                                g_qtr.inc(e.dma_start(
                                    out=qrT[:, h2 * TOWN:(h2 + 1) * TOWN],
                                    in_=qr_b[h2, :, :], transpose=True), 16)
                            for h2 in range(2):
                                e.wait_ge(g_oev.h, P_oev(grp, h2, 7, 1))
                                g_ow.inc(e.dma_start(
                                    out=o_b[2 * grp + h2, :, :].rearrange("(s p) f -> p s f", p=128),
                                    in_=og[:, :].rearrange("p (hs f) -> p hs f", f=128)[:, h2 * 16:(h2 + 1) * 16, :]), 16)
                        e.wait_ge(g_ow.h, 16 * 2 * NG)
                    block.sync(_sy)
                    pop_scope()

                cbc.close()

                with ExitStack() as cd:
                    push_scope(cd)
                    sbd = lambda *a, **kw: cd.enter_context(nc.sbuf_tensor(*a, **kw))
                    oT = sbd("d_oT", [128, 16 * TOWN], BF16, is_transpose_dest=True)
                    wob = sbd("d_wob", [128, 16 * 1024], BF16)
                    xg = sbd("d_xg", [128, 2 * D], F32)
                    x2t = sbd("d_x2t", [128, 2 * 1024], F32)
                    pso = cd.enter_context(nc.psum_tensor("d_pso", [128, 2 * 1024], F32))
                    d_tr = mksem(octx, "d_tr")
                    d_wo = mkdsem("d_wo")
                    d_gx = [mkdsem("d_gx0"), mkdsem("d_gx1")]
                    d_mm = mksem(octx, "d_mm")
                    d_ad = mksem(octx, "d_ad")
                    d_xw = [mksem(octx, "d_xw0"), mksem(octx, "d_xw1")]

                    def _sy(e):
                        for hd in range(16):
                            d_tr.inc(e.dma_start(
                                out=oT[:, hd * TOWN:(hd + 1) * TOWN],
                                in_=o_b[hd, :, :], transpose=True), 16)
                        for dh in range(2):
                            for sub in range(16):
                                k = 16 * dh + sub
                                e.wait_ge(d_ad.h, k + 1)
                                d_xw[k % 2].inc(e.dma_start(
                                    out=x2s[sub * 128:(sub + 1) * 128, dh * 1024:(dh + 1) * 1024],
                                    in_=x2t[:, (k % 2) * 1024:(k % 2 + 1) * 1024]), 16)
                        e.wait_ge(d_xw[0].h, 16 * 16)
                        e.wait_ge(d_xw[1].h, 16 * 16)
                    block.sync(_sy)

                    def _g(e):
                        wor = wo.rearrange("(hd p) d -> p hd d", p=128)
                        for dh in range(2):
                            if dh == 1:
                                e.wait_ge(d_mm.h, 16)
                            d_wo.inc(e.dma_start(
                                out=wob[:].rearrange("p (hd f) -> p hd f", f=1024),
                                in_=wor[:, :, dh * 1024:(dh + 1) * 1024]), 16)
                            for sub in range(16):
                                k = 16 * dh + sub
                                if k >= 2:
                                    e.wait_ge(d_ad.h, k - 1)
                                d_gx[k % 2].inc(e.indirect_dma_start(
                                    out=xg[:, (k % 2) * D:(k % 2 + 1) * D], out_offset=None,
                                    in_=x1s[:, :],
                                    in_offset=IOffs(ap=s_ridx[:, sub:sub + 1], axis=0)), 16)
                    block.gpsimd(_g)

                    def _t(e):
                        e.wait_ge(d_tr.h, 16 * 16)
                        for dh in range(2):
                            e.wait_ge(d_wo.h, 16 * (dh + 1))
                            for sub in range(16):
                                k = 16 * dh + sub
                                if k >= 2:
                                    e.wait_ge(d_ad.h, k - 1)
                                for hd in range(16):
                                    for ncx in range(2):
                                        ins = e.matmul(
                                            pso[:, (k % 2) * 1024 + ncx * 512:(k % 2) * 1024 + (ncx + 1) * 512],
                                            oT[:, hd * TOWN + sub * 128:hd * TOWN + (sub + 1) * 128],
                                            wob[:, hd * 1024 + ncx * 512:hd * 1024 + (ncx + 1) * 512],
                                            start=(hd == 0), stop=(hd == 15))
                                d_mm.inc(ins, 1)
                    block.tensor(_t)

                    def _v(e):
                        for dh in range(2):
                            for sub in range(16):
                                k = 16 * dh + sub
                                e.wait_ge(d_mm.h, k + 1)
                                e.wait_ge(d_gx[k % 2].h, 16 * (k // 2 + 1))
                                if k >= 2:
                                    e.wait_ge(d_xw[k % 2].h, 16 * (k // 2))
                                d_ad.inc(e.tensor_tensor(
                                    out=x2t[:, (k % 2) * 1024:(k % 2 + 1) * 1024],
                                    in0=pso[:, (k % 2) * 1024:(k % 2 + 1) * 1024],
                                    in1=xg[:, (k % 2) * D + dh * 1024:(k % 2) * D + (dh + 1) * 1024],
                                    op=ALU.add), 1)
                    bvector(_v)
                    pop_scope()

        if 3 in stages:
            emit_ffn(x2s, out_t, TOWN)

    return nc


def _host_consts():
    inv = 1.0 / (10000.0 ** (np.arange(0, DR, 2, dtype=np.float32) / DR))
    pos = np.arange(S, dtype=np.float32)
    freqs = np.outer(pos, inv).astype(np.float32)
    emb = np.concatenate([freqs, freqs], axis=-1)
    cosf = np.cos(emb).astype(np.float32)
    sinf = np.sin(emb).astype(np.float32)
    H64 = _hadamard(64)
    H128 = _hadamard(128)
    HA = np.kron(np.eye(2), H64)
    hc = np.zeros((128, 256), np.float32)
    hc[:, 0:128] = HA
    hc[:, 128:256] = H128
    hconst = hc.astype(ml_dtypes.bfloat16)
    onesf = np.ones((128, 128), np.float32)
    onesb = np.ones((128, 1), np.float32).astype(ml_dtypes.bfloat16)
    return cosf, sinf, hconst, onesf, onesb


def make_in_maps(inputs):
    x = np.ascontiguousarray(np.asarray(inputs["L0_attn_x"], dtype=np.float32))
    W1 = np.asarray(inputs["L0_ffn_W1"], dtype=np.float32)
    W2 = np.asarray(inputs["L0_ffn_W2"], dtype=np.float32)
    W3 = np.asarray(inputs["L0_ffn_W3"], dtype=np.float32)

    w1tg = np.ascontiguousarray(W1.reshape(128, 64, 16, 128).transpose(1, 2, 3, 0))
    w3tg = np.ascontiguousarray(W3.reshape(128, 64, 16, 128).transpose(1, 2, 3, 0))
    w2t = np.ascontiguousarray(W2.T)

    cosf, sinf, hconst, onesf, onesb = _host_consts()

    base = {
        "w1tg": w1tg, "w3tg": w3tg, "w2t": w2t,
        "wqa": np.ascontiguousarray(inputs["L0_attn_w_qa"], dtype=np.float32),
        "wqb": np.ascontiguousarray(inputs["L0_attn_w_qb"], dtype=np.float32),
        "wkva": np.ascontiguousarray(inputs["L0_attn_w_kva"], dtype=np.float32),
        "wkvb": np.ascontiguousarray(inputs["L0_attn_w_kvb"], dtype=np.float32),
        "wo": np.ascontiguousarray(inputs["L0_attn_w_o"], dtype=np.float32),
        "qnw": np.ascontiguousarray(np.asarray(inputs["L0_attn_q_a_norm_w"], np.float32).reshape(6, 128).T),
        "kvnw": np.ascontiguousarray(np.asarray(inputs["L0_attn_kv_a_norm_w"], np.float32).reshape(4, 128).T),
        "cos_f": cosf, "sin_f": sinf,
        "hconst": hconst, "ones_f": onesf, "ones_b": onesb,
        "epsc": np.broadcast_to(
            np.array([LNEPS, RMSEPS, QSC_BIAS, 0.0], np.float32), (128, 4)).copy(),
    }

    in_maps = []
    own_rows_all = []
    for c in range(8):
        b, half = c // 2, c % 2
        own = np.concatenate([np.arange(g * 512 + half * 256, g * 512 + half * 256 + 256)
                              for g in range(8)]).astype(np.int64)
        own_rows_all.append(own)
        rix = own.reshape(16, 128).T.astype(np.uint32)
        mask = np.zeros((128, 4, 256), np.float32)
        ti = np.arange(128)[:, None]
        qi = np.arange(256)[None, :]
        for o in range(4):
            mask[:, o, :] = (o * 128 + ti - half * 256 <= qi).astype(np.float32)
        im = dict(base)
        im["x"] = np.ascontiguousarray(x[b])
        im["cos_o"] = np.ascontiguousarray(cosf[own])
        im["sin_o"] = np.ascontiguousarray(sinf[own])
        im["maskc"] = np.ascontiguousarray(mask.reshape(128, 4 * 256).astype(ml_dtypes.bfloat16))
        im["ridx"] = np.ascontiguousarray(rix)
        in_maps.append(im)
    return in_maps, own_rows_all


def kernel(**inputs):
    in_maps, own_rows_all = make_in_maps(inputs)
    nc = build_nc()
    res = run_bass_kernel_spmd(nc, in_maps, core_ids=list(range(8)))
    out = np.empty((BATCH, S, D), np.float32)
    for c in range(8):
        b = c // 2
        out[b, own_rows_all[c]] = np.asarray(res.results[c]["out"], np.float32)
    return out


# version 10
# speedup vs baseline: 1.4537x; 1.0511x over previous
import math
from contextlib import ExitStack

import numpy as np
import ml_dtypes
import concourse.bass as bass
import concourse.mybir as mybir
import bass_rust
from concourse.bass_utils import run_bass_kernel_spmd

F32 = mybir.dt.float32
BF16 = mybir.dt.bfloat16
U32 = mybir.dt.uint32
AF = mybir.ActivationFunctionType
ALU = mybir.AluOpType
AX = mybir.AxisListType
IOffs = bass_rust.IndirectOffsetOnAxis

BATCH, S, D, HID, NH = 4, 4096, 2048, 8192, 16
DN, DR, DV, DQK = 128, 64, 128, 192
QL, KVL = 768, 512
TOWN = 2048
LNEPS = 1e-5
RMSEPS = 1e-6
W2SCALE = 0.5 / math.sqrt(HID)
QSC_BIAS = math.log(1.0 / math.sqrt(DQK))
C_COEF = 8.0 / math.sqrt(DQK)
NG = 8
NTT_G = sum(4 * (g8 + 1) for g8 in range(8))


def _hadamard(n):
    h = np.array([[1.0]])
    while h.shape[0] < n:
        h = np.block([[h, h], [h, -h]])
    return h


class Sem:
    def __init__(self, ctx, nc, name):
        self.h = ctx.enter_context(nc.semaphore(name))
        self.n = 0

    def inc(self, instr, k=1):
        instr.then_inc(self.h, k)
        self.n += k
        return self.n


_WRAP_OPS = frozenset((
    "activation", "copy", "add", "mul", "tensor_tensor", "tensor_scalar",
    "tensor_copy", "tensor_reduce", "reduce_sum", "reduce_max", "reciprocal",
    "memset", "tensor_tensor_reduce", "select", "transpose", "iota",
))


class _Shim:
    def __init__(self, e):
        self._e = e

    def then_inc(self, h, k):
        self._e.sem_inc(h, k)
        return self


class SafeEng:

    def __init__(self, e, prog):
        self._e = e
        self._p = prog

    @property
    def raw(self):
        return self._e

    def __getattr__(self, nm):
        a = getattr(self._e, nm)
        if nm in _WRAP_OPS:
            def wrap(*args, **kw):
                ins = a(*args, **kw)
                self._p.n += 1
                ins.then_inc(self._p.h, 1)
                self._e.wait_ge(self._p.h, self._p.n)
                return _Shim(self._e)
            return wrap
        return a


def build_nc(stages=(0, 1, 2, 3), debug=False):
    nc = bass.Bass("TRN2", target_bir_lowering=False)

    x_in = nc.dram_tensor("x", [S, D], F32, kind="ExternalInput").ap()
    w1tg = nc.dram_tensor("w1tg", [64, 16, 128, 128], F32, kind="ExternalInput").ap()
    w3tg = nc.dram_tensor("w3tg", [64, 16, 128, 128], F32, kind="ExternalInput").ap()
    w2t = nc.dram_tensor("w2t", [HID, D], F32, kind="ExternalInput").ap()
    wqa = nc.dram_tensor("wqa", [D, QL], F32, kind="ExternalInput").ap()
    wqb = nc.dram_tensor("wqb", [QL, NH * DQK], F32, kind="ExternalInput").ap()
    wkva = nc.dram_tensor("wkva", [D, KVL + DR], F32, kind="ExternalInput").ap()
    wkvb = nc.dram_tensor("wkvb", [KVL, NH * 256], F32, kind="ExternalInput").ap()
    wo = nc.dram_tensor("wo", [NH * DV, D], F32, kind="ExternalInput").ap()
    qnw = nc.dram_tensor("qnw", [128, 6], F32, kind="ExternalInput").ap()
    kvnw = nc.dram_tensor("kvnw", [128, 4], F32, kind="ExternalInput").ap()
    cos_f = nc.dram_tensor("cos_f", [S, DR], F32, kind="ExternalInput").ap()
    sin_f = nc.dram_tensor("sin_f", [S, DR], F32, kind="ExternalInput").ap()
    cos_o = nc.dram_tensor("cos_o", [TOWN, DR], F32, kind="ExternalInput").ap()
    sin_o = nc.dram_tensor("sin_o", [TOWN, DR], F32, kind="ExternalInput").ap()
    hconst = nc.dram_tensor("hconst", [128, 256], BF16, kind="ExternalInput").ap()
    ones_f = nc.dram_tensor("ones_f", [128, 128], F32, kind="ExternalInput").ap()
    ones_b = nc.dram_tensor("ones_b", [128, 1], BF16, kind="ExternalInput").ap()
    maskc = nc.dram_tensor("maskc", [128, 4 * 256], BF16, kind="ExternalInput").ap()
    epsc = nc.dram_tensor("epsc", [128, 4], F32, kind="ExternalInput").ap()
    ridx = nc.dram_tensor("ridx", [128, 16], U32, kind="ExternalInput").ap()

    out_t = nc.dram_tensor("out", [TOWN, D], F32, kind="ExternalOutput").ap()

    dbg = "ExternalOutput" if debug else "Internal"
    zsc = nc.dram_tensor("zsc", [HID, D], BF16, kind="Internal").ap()
    wf = nc.dram_tensor("wfs", [64, 128, D], BF16, kind=dbg).ap()
    x1s = nc.dram_tensor("x1s", [S, D], F32, kind=dbg).ap()
    x2s = nc.dram_tensor("x2s", [TOWN, D], F32, kind=dbg).ap()
    lnt_b = nc.dram_tensor("lnt_b", [1024, D], BF16, kind="Internal").ap()
    ln2_b = nc.dram_tensor("ln2_b", [S, D], BF16, kind="Internal").ap()
    lno_b = nc.dram_tensor("lno_b", [TOWN, D], BF16, kind="Internal").ap()
    cn_b = nc.dram_tensor("cn_b", [S, KVL], BF16, kind="Internal").ap()
    kr_b = nc.dram_tensor("kr_b", [S, 128], BF16, kind="Internal").ap()
    qn_b = nc.dram_tensor("qn_b", [2, TOWN, 128], BF16, kind="Internal").ap()
    qr_b = nc.dram_tensor("qr_b", [2, TOWN, 128], BF16, kind="Internal").ap()
    o_b = nc.dram_tensor("o_b", [NH, TOWN, DV], BF16, kind="Internal").ap()

    octx = ExitStack()
    with octx:
        osb = lambda *a, **kw: octx.enter_context(nc.sbuf_tensor(*a, **kw))
        s_hc = osb("s_hc", [128, 256], BF16)
        s_1f = osb("s_1f", [128, 128], F32)
        s_1b = osb("s_1b", [128, 1], BF16)
        s_mask = osb("s_mask", [128, 4 * 256], BF16)
        s_ridx = osb("s_ridx", [128, 16], U32)
        s_eps = osb("s_eps", [128, 4], F32)

        ph = Sem(octx, nc, "ph")
        block = octx.enter_context(nc.Block())
        prog_v = Sem(octx, nc, "prog_v")
        prog_s = Sem(octx, nc, "prog_s")
        def bvector(fn):
            block.vector(lambda e: fn(SafeEng(e, prog_v)))

        def bscalar(fn):
            block.scalar(lambda e: fn(SafeEng(e, prog_s)))

        sem_id = [0]

        def push_scope(ctx_):
            pass

        def mksem(_ctx, label=None):
            sem_id[0] += 1
            return Sem(octx, nc, label or ("s%d" % sem_id[0]))

        mkdsem = lambda label: mksem(None, label)

        def pop_scope():
            barrier()

        def barrier():
            tgt = ph.n + 5
            ph.n = tgt
            for name in ("sync", "gpsimd", "tensor", "vector", "scalar"):
                def _f(e, tgt=tgt):
                    e.sem_inc(ph.h, 1)
                    e.wait_ge(ph.h, tgt)
                getattr(block, name)(_f)

        push_scope(octx)
        c_ld = Sem(octx, nc, "c_ld")

        def _consts(e):
            for dst, src in ((s_hc, hconst), (s_1f, ones_f), (s_1b, ones_b),
                             (s_mask, maskc), (s_ridx, ridx), (s_eps, epsc)):
                c_ld.inc(e.dma_start(out=dst[:], in_=src[:, :]), 16)
        block.sync(_consts)

        if 0 in stages:
            with ExitStack() as c0:
                push_scope(c0)
                sb0 = lambda *a, **kw: c0.enter_context(nc.sbuf_tensor(*a, **kw))
                s_w2 = sb0("s_w2", [128, 4 * D], BF16)
                s_z = sb0("s_z", [128, 2 * D], BF16)
                s_zs = sb0("s_zs", [128, 4 * D], BF16)
                s_wfo = sb0("s_wfo", [128, 2 * D], BF16)
                pA = c0.enter_context(nc.psum_tensor("p0a", [128, D], F32))
                pB = c0.enter_context(nc.psum_tensor("p0b", [128, D], F32))
                w_ld = [mkdsem("w_ld%d" % i) for i in range(4)]
                w_mm = mksem(octx, "w_mm")
                w_ev = mksem(octx, "w_ev")
                w_st = [mksem(octx, "w_st0"), mksem(octx, "w_st1")]
                w_fst = [mksem(octx, "w_fst0"), mksem(octx, "w_fst1")]
                w_zl = [mksem(octx, "w_zl%d" % i) for i in range(4)]

                def _g(e):
                    for hc in range(64):
                        if hc >= 4:
                            e.wait_ge(w_mm.h, hc - 3)
                        w_ld[hc % 4].inc(e.dma_start(
                            out=s_w2[:, (hc % 4) * D:(hc % 4 + 1) * D],
                            in_=w2t[hc * 128:(hc + 1) * 128, :]), 16)
                block.gpsimd(_g)

                def _t(e):
                    e.wait_ge(c_ld.h, 16 * 6)
                    for hc in range(64):
                        e.wait_ge(w_ld[hc % 4].h, 16 * (hc // 4 + 1))
                        if hc >= 2:
                            e.wait_ge(w_ev.h, hc - 1)
                        pp = pA if hc % 2 == 0 else pB
                        for ncx in range(4):
                            ins = e.matmul(
                                pp[:, ncx * 512:(ncx + 1) * 512],
                                s_hc[:, 0:128],
                                s_w2[:, (hc % 4) * D + ncx * 512:(hc % 4) * D + (ncx + 1) * 512],
                                start=True, stop=True)
                        w_mm.inc(ins, 1)
                    for si in range(64):
                        e.wait_ge(w_zl[si % 4].h, 16 * (si // 4 + 1))
                        if si >= 2:
                            e.wait_ge(w_ev.h, 64 + si - 1)
                        pp = pA if si % 2 == 0 else pB
                        for ncx in range(4):
                            ins = e.matmul(
                                pp[:, ncx * 512:(ncx + 1) * 512],
                                s_hc[:, 128:256],
                                s_zs[:, (si % 4) * D + ncx * 512:(si % 4) * D + (ncx + 1) * 512],
                                start=True, stop=True)
                        w_mm.inc(ins, 1)
                block.tensor(_t)

                def _s(e):
                    for hc in range(64):
                        e.wait_ge(w_mm.h, hc + 1)
                        if hc >= 2:
                            e.wait_ge(w_st[hc % 2].h, 16 * (hc // 2))
                        pp = pA if hc % 2 == 0 else pB
                        w_ev.inc(e.activation(
                            out=s_z[:, (hc % 2) * D:(hc % 2 + 1) * D],
                            in_=pp[:, :], func=AF.Copy), 1)
                        w_st[hc % 2].inc(e.dma_start(
                            out=zsc[hc * 128:(hc + 1) * 128, :],
                            in_=s_z[:, (hc % 2) * D:(hc % 2 + 1) * D]), 16)
                    for si in range(64):
                        e.wait_ge(w_mm.h, 64 + si + 1)
                        if si >= 2:
                            e.wait_ge(w_fst[si % 2].h, 16 * (si // 2))
                        pp = pA if si % 2 == 0 else pB
                        w_ev.inc(e.activation(
                            out=s_wfo[:, (si % 2) * D:(si % 2 + 1) * D],
                            in_=pp[:, :], func=AF.Identity, scale=W2SCALE), 1)
                        w_fst[si % 2].inc(e.dma_start(
                            out=wf[si, :, :],
                            in_=s_wfo[:, (si % 2) * D:(si % 2 + 1) * D]), 16)
                bscalar(_s)

                def _sy(e):
                    zr = zsc.rearrange("(j s) d -> s j d", s=64)
                    for si in range(64):
                        if si == 0:
                            e.wait_ge(w_st[0].h, 16 * 32)
                            e.wait_ge(w_st[1].h, 16 * 32)
                        if si >= 4:
                            e.wait_ge(w_mm.h, 64 + si - 3)
                        w_zl[si % 4].inc(e.dma_start(
                            out=s_zs[:, (si % 4) * D:(si % 4 + 1) * D],
                            in_=zr[si, :, :]), 16)
                    e.wait_ge(w_fst[0].h, 16 * 32)
                    e.wait_ge(w_fst[1].h, 16 * 32)
                block.sync(_sy)
                pop_scope()

        FFN_ID = [0]
        FFN_SEMS = {}

        def emit_ffn(x_src, x_dst, T):
            npass = T // 1024
            FFN_ID[0] += 1
            tag = "f%d_" % FFN_ID[0]
            with ExitStack() as cf:
                push_scope(cf)
                sbf = lambda nm, *a, **kw: cf.enter_context(nc.sbuf_tensor(tag + nm, *a, **kw))
                acc = sbf("acc", [128, 8 * D], F32)
                l1t = sbf("l1t", [128, 16 * 1024], BF16, is_transpose_dest=True)
                hT = sbf("hT", [128, 8 * 1024], BF16)
                wfg = sbf("wfg", [128, 8 * D], BF16)
                sW1 = sbf("w1", [128, 2 * 2048], BF16)
                sW3 = sbf("w3", [128, 2 * 2048], BF16)
                sg = sbf("sg", [128, 2 * 1024], BF16)
                junk = sbf("junk", [128, D], BF16)
                lnsb = sbf("lnsb", [128, 2 * D], BF16)
                xl = sbf("xl", [128, 2 * D], F32)
                stat = sbf("stat", [128, 8 * 6], F32)
                psg = cf.enter_context(nc.psum_tensor(tag + "psg", [128, 1024], F32))
                psl = cf.enter_context(nc.psum_tensor(tag + "psl", [128, 1024], F32))
                psda = cf.enter_context(nc.psum_tensor(tag + "psda", [128, 1024], F32))
                psdb = cf.enter_context(nc.psum_tensor(tag + "psdb", [128, 1024], F32))
                psdh = [psda, psdb]
                if not FFN_SEMS:
                    FFN_SEMS["pb"] = 0
                    for nm_ in ("f_x", "f_tr", "f_wf", "f_mmu", "f_mmd", "f_se",
                                "f_vm", "f_da", "f_acw"):
                        FFN_SEMS[nm_] = mksem(octx, nm_)
                    FFN_SEMS["f_lnw"] = [mksem(octx, "f_lnw0"), mksem(octx, "f_lnw1")]
                    FFN_SEMS["f_xl"] = [mksem(octx, "f_xl0"), mksem(octx, "f_xl1")]
                    FFN_SEMS["f_wl"] = [mkdsem("f_wl0"), mkdsem("f_wl1")]
                pb = FFN_SEMS["pb"]
                FFN_SEMS["pb"] = pb + npass
                f_x = FFN_SEMS["f_x"]
                f_lnw = FFN_SEMS["f_lnw"]
                f_tr = FFN_SEMS["f_tr"]
                f_wf = FFN_SEMS["f_wf"]
                f_wl = FFN_SEMS["f_wl"]
                f_mmu = FFN_SEMS["f_mmu"]
                f_mmd = FFN_SEMS["f_mmd"]
                f_se = FFN_SEMS["f_se"]
                f_vm = FFN_SEMS["f_vm"]
                f_da = FFN_SEMS["f_da"]
                f_xl = FFN_SEMS["f_xl"]
                f_acw = FFN_SEMS["f_acw"]

                accv = acc[:, :].rearrange("p (t d) -> p t d", d=D)

                def stt(ti, j):
                    return stat[:, ti * 6 + j:ti * 6 + j + 1]

                def _sy(e):
                    for p0 in range(npass):
                        p = pb + p0
                        if p > 0:
                            e.wait_ge(f_da.h, 128 * p)
                            e.wait_ge(f_acw.h, 16 * p)
                        f_x.inc(e.dma_start(
                            out=accv[:, :, :],
                            in_=x_src[p0 * 1024:(p0 + 1) * 1024, :]
                                .rearrange("(t q) d -> q t d", q=128)), 16)
                        for u in range(64):
                            g = u // 8
                            e.wait_ge(f_mmd.h, 128 * p + 16 * g)
                            f_wf.inc(e.dma_start(
                                out=wfg[:, (u % 8) * D:(u % 8 + 1) * D],
                                in_=wf[u, :, :]), 16)
                        e.wait_ge(f_da.h, 128 * (p + 1))
                        f_acw.inc(e.dma_start(
                            out=x_dst[p0 * 1024:(p0 + 1) * 1024, :]
                                .rearrange("(t q) d -> q t d", q=128),
                            in_=accv[:, :, :]), 16)
                    e.wait_ge(f_acw.h, 16 * (pb + npass))
                block.sync(_sy)

                def _g(e):
                    for p0 in range(npass):
                        p = pb + p0
                        for u in range(64):
                            uu = 64 * p + u
                            if uu >= 2:
                                e.wait_ge(f_mmu.h, 2 * uu - 2)
                            sl = u % 2
                            f_wl[sl].inc(e.dma_start(
                                out=sW1[:, sl * 2048:(sl + 1) * 2048]
                                    .rearrange("p (dc j) -> p dc j", j=128),
                                in_=w1tg[u, :, :, :].rearrange("dc p j -> p dc j")), 16)
                            f_wl[sl].inc(e.dma_start(
                                out=sW3[:, sl * 2048:(sl + 1) * 2048]
                                    .rearrange("p (dc j) -> p dc j", j=128),
                                in_=w3tg[u, :, :, :].rearrange("dc p j -> p dc j")), 16)
                block.gpsimd(_g)

                def _t(e):
                    for p0 in range(npass):
                        p = pb + p0
                        e.wait_ge(f_tr.h, 16 * 16 * (p + 1))
                        for g in range(8):
                            for ul in range(8):
                                u = 8 * g + ul
                                uu = 64 * p + u
                                e.wait_ge(f_wl[uu % 2].h, 32 * (uu // 2 + 1))
                                e.wait_ge(f_se.h, 64 * p + u)
                                sl = u % 2
                                for dc in range(16):
                                    for ncx in range(2):
                                        ins = e.matmul(
                                            psg[:, ncx * 512:(ncx + 1) * 512],
                                            sW1[:, sl * 2048 + dc * 128:sl * 2048 + (dc + 1) * 128],
                                            l1t[:, dc * 1024 + ncx * 512:dc * 1024 + (ncx + 1) * 512],
                                            start=(dc == 0), stop=(dc == 15))
                                f_mmu.inc(ins, 1)
                                e.wait_ge(f_vm.h, 64 * p + u)
                                for dc in range(16):
                                    for ncx in range(2):
                                        ins = e.matmul(
                                            psl[:, ncx * 512:(ncx + 1) * 512],
                                            sW3[:, sl * 2048 + dc * 128:sl * 2048 + (dc + 1) * 128],
                                            l1t[:, dc * 1024 + ncx * 512:dc * 1024 + (ncx + 1) * 512],
                                            start=(dc == 0), stop=(dc == 15))
                                f_mmu.inc(ins, 1)
                            e.wait_ge(f_vm.h, 64 * p + 8 * (g + 1))
                            e.wait_ge(f_wf.h, 16 * (64 * p + 8 * (g + 1)))
                            for ti in range(8):
                                for hf in range(2):
                                    wv = 128 * p + 16 * g + 2 * ti - 1 + hf
                                    if wv > 0:
                                        e.wait_ge(f_da.h, wv)
                                    for ul2 in range(8):
                                        for nc2 in range(2):
                                            ins = e.matmul(
                                                psdh[hf][:, nc2 * 512:(nc2 + 1) * 512],
                                                hT[:, ul2 * 1024 + ti * 128:ul2 * 1024 + (ti + 1) * 128],
                                                wfg[:, ul2 * D + (2 * hf + nc2) * 512:ul2 * D + (2 * hf + nc2 + 1) * 512],
                                                start=(ul2 == 0), stop=(ul2 == 7))
                                    f_mmd.inc(ins, 1)
                block.tensor(_t)

                def ln_tile(e, q, ti):
                    def xload(ti):
                        kk = 8 * q + ti
                        f_xl[kk % 2].inc(e.dma_start(
                            out=xl[:, (kk % 2) * D:(kk % 2 + 1) * D],
                            in_=x_src[(q - pb) * 1024 + ti * 128:
                                      (q - pb) * 1024 + (ti + 1) * 128, :]), 16)
                    if ti == 0:
                        xload(0)
                        xload(1)
                    if True:
                        kk = 8 * q + ti
                        sl = (kk % 2) * D
                        e.wait_ge(f_xl[kk % 2].h, 16 * (kk // 2 + 1))
                        e.activation(out=junk[:, :], in_=xl[:, sl:sl + D],
                                     func=AF.Identity, accum_out=stt(ti, 0))
                        e.activation(out=stt(ti, 1), in_=stt(ti, 0),
                                     func=AF.Identity, scale=-1.0 / D)
                        e.activation(out=junk[:, :], in_=xl[:, sl:sl + D],
                                     func=AF.Square, bias=stt(ti, 1),
                                     accum_out=stt(ti, 4))
                        e.activation(out=stt(ti, 5), in_=stt(ti, 4),
                                     func=AF.Ln, scale=1.0 / D, bias=s_eps[:, 0:1])
                        e.activation(out=stt(ti, 3), in_=stt(ti, 5),
                                     func=AF.Exp, scale=-0.5)
                        e.activation(out=stt(ti, 2), in_=stt(ti, 1),
                                     func=AF.Identity, scale=stt(ti, 3))
                        if kk >= 2:
                            e.wait_ge(f_lnw[kk % 2].h, 16 * (kk // 2))
                        e.activation(
                            out=lnsb[:, sl:sl + D],
                            in_=xl[:, sl:sl + D], func=AF.Identity,
                            scale=stt(ti, 3), bias=stt(ti, 2))
                        f_lnw[kk % 2].inc(e.dma_start(
                            out=lnt_b[ti * 128:(ti + 1) * 128, :],
                            in_=lnsb[:, sl:sl + D]), 16)
                        if ti + 2 < 8:
                            xload(ti + 2)

                def ln_trans(e, q):
                    e.wait_ge(f_lnw[0].h, 64 * (q + 1))
                    e.wait_ge(f_lnw[1].h, 64 * (q + 1))
                    if q > 0:
                        e.wait_ge(f_mmu.h, 128 * q)
                    for dc in range(16):
                        f_tr.inc(e.dma_start(
                            out=l1t[:, dc * 1024:(dc + 1) * 1024],
                            in_=lnt_b[:, dc * 128:(dc + 1) * 128],
                            transpose=True), 16)

                def _s(e):
                    for ti in range(8):
                        ln_tile(e, pb, ti)
                    ln_trans(e, pb)
                    for p0 in range(npass):
                        p = pb + p0
                        for u in range(64):
                            uu = 64 * p + u
                            e.wait_ge(f_mmu.h, 2 * uu + 1)
                            if uu >= 2:
                                e.wait_ge(f_vm.h, uu - 1)
                            f_se.inc(e.activation(
                                out=sg[:, (u % 2) * 1024:(u % 2 + 1) * 1024],
                                in_=psg[:, :], func=AF.Silu), 1)
                            if p0 + 1 < npass and u % 8 == 7:
                                ln_tile(e, p + 1, u // 8)
                        if p0 + 1 < npass:
                            ln_trans(e, p + 1)
                bscalar(_s)

                def _v(e):
                    for p0 in range(npass):
                        p = pb + p0
                        for u in range(64):
                            uu = 64 * p + u
                            e.wait_ge(f_se.h, uu + 1)
                            e.wait_ge(f_mmu.h, 2 * uu + 2)
                            f_vm.inc(e.tensor_tensor(
                                out=hT[:, (u % 8) * 1024:(u % 8 + 1) * 1024],
                                in0=sg[:, (u % 2) * 1024:(u % 2 + 1) * 1024],
                                in1=psl[:, :], op=ALU.mult), 1)
                            if u % 8 == 7:
                                g = u // 8
                                for ti in range(8):
                                    for hf in range(2):
                                        e.wait_ge(f_mmd.h, 128 * p + 16 * g + 2 * ti + hf + 1)
                                        if g == 0 and ti == 0 and hf == 0:
                                            e.wait_ge(f_x.h, 16 * (p + 1))
                                        f_da.inc(e.tensor_tensor(
                                            out=accv[:, ti, hf * 1024:(hf + 1) * 1024],
                                            in0=accv[:, ti, hf * 1024:(hf + 1) * 1024],
                                            in1=psdh[hf][:, :], op=ALU.add), 1)
                bvector(_v)
                pop_scope()

        if 1 in stages:
            emit_ffn(x_in, x1s, S)

        if 2 in stages:
            with ExitStack() as c2:
                push_scope(c2)
                sb2 = lambda *a, **kw: c2.enter_context(nc.sbuf_tensor(*a, **kw))
                s_cosf = sb2("s_cosf", [128, 32 * DR], F32)
                s_sinf = sb2("s_sinf", [128, 32 * DR], F32)
                s_coso = sb2("s_coso", [128, 16 * DR], F32)
                s_sino = sb2("s_sino", [128, 16 * DR], F32)
                cnT = sb2("cnT", [128, 4 * S], BF16, is_transpose_dest=True)
                krT = sb2("krT", [128, S], BF16, is_transpose_dest=True)
                scol = sb2("scol", [128, 16], F32)
                tmp1 = sb2("tmp1", [128, 512], F32)
                tmp2 = sb2("tmp2", [128, 512], F32)
                a_c = mksem(octx, "a_c")
                b_fold = mksem(octx, "b_fold")

                def _cl(e):
                    for dst, src in ((s_cosf, cos_f), (s_sinf, sin_f),
                                     (s_coso, cos_o), (s_sino, sin_o)):
                        a_c.inc(e.dma_start(
                            out=dst[:].rearrange("p (t f) -> p t f", f=DR),
                            in_=src.rearrange("(t q) f -> q t f", q=128)), 16)
                block.sync(_cl)

                def rope6(e, srcE, srcO, cE, cO, sE, sO, dE, dO, w):
                    e.tensor_tensor(out=tmp1[:, 0:w], in0=srcE, in1=cE, op=ALU.mult)
                    e.tensor_tensor(out=tmp2[:, 0:w], in0=srcO, in1=sE, op=ALU.mult)
                    e.tensor_tensor(out=dE, in0=tmp1[:, 0:w], in1=tmp2[:, 0:w], op=ALU.subtract)
                    e.tensor_tensor(out=tmp1[:, 0:w], in0=srcO, in1=cO, op=ALU.mult)
                    e.tensor_tensor(out=tmp2[:, 0:w], in0=srcE, in1=sO, op=ALU.mult)
                    return e.tensor_tensor(out=dO, in0=tmp1[:, 0:w], in1=tmp2[:, 0:w], op=ALU.add)

                with ExitStack() as ca:
                    push_scope(ca)
                    sba = lambda *a, **kw: ca.enter_context(nc.sbuf_tensor(*a, **kw))
                    xc = sba("a_xc", [128, 4 * D], F32)
                    l2t = sba("a_l2t", [128, 2 * 16 * 512], BF16, is_transpose_dest=True)
                    lnsb2 = sba("a_lnsb", [128, 2 * D], BF16)
                    junk2 = sba("a_junk", [128, D], BF16)
                    wkvab = sba("a_wkvab", [128, 16 * (KVL + DR)], BF16)
                    cnsb = sba("a_cnsb", [128, 2 * KVL], BF16)
                    krf = sba("a_krf", [128, 2 * DR], F32)
                    krsb = sba("a_krsb", [128, 2 * 128], BF16)
                    ast = sba("a_stat", [128, 32 * 6], F32)
                    psk = ca.enter_context(nc.psum_tensor("a_psk", [128, 2 * 1024], F32))
                    a_x = [mksem(octx, "a_x%d" % i) for i in range(4)]
                    a_vst = mksem(octx, "a_vst")
                    a_sr = mksem(octx, "a_sr")
                    a_sev = mksem(octx, "a_sev")
                    a_lnw = [mksem(octx, "a_lnw0"), mksem(octx, "a_lnw1")]
                    a_tr = mksem(octx, "a_tr")
                    a_kmm = mksem(octx, "a_kmm")
                    a_sr2 = mksem(octx, "a_sr2")
                    a_kev = mksem(octx, "a_kev")
                    a_kro = mksem(octx, "a_kro")
                    a_cnw = [mkdsem("a_cnw0"), mkdsem("a_cnw1")]
                    a_wk = mkdsem("a_wk")

                    def ss(k, j):
                        return ast[:, k * 6 + j:k * 6 + j + 1]

                    def _g(e):
                        a_wk.inc(e.dma_start(
                            out=wkvab[:].rearrange("p (dc f) -> p dc f", f=KVL + DR),
                            in_=wkva.rearrange("(dc p) f -> p dc f", p=128)), 16)
                        for k in range(32):
                            e.wait_ge(a_kev.h, k + 1)
                            e.wait_ge(a_kro.h, k + 1)
                            a_cnw[k % 2].inc(e.dma_start(
                                out=cn_b[k * 128:(k + 1) * 128, :],
                                in_=cnsb[:, (k % 2) * KVL:(k % 2 + 1) * KVL]), 16)
                            a_cnw[k % 2].inc(e.dma_start(
                                out=kr_b[k * 128:(k + 1) * 128, :],
                                in_=krsb[:, (k % 2) * 128:(k % 2 + 1) * 128]), 16)
                    block.gpsimd(_g)

                    def _sy(e):
                        for sc in range(8):
                            for ti in range(4):
                                k = 4 * sc + ti
                                if k >= 4:
                                    e.wait_ge(a_sev.h, k - 3)
                                a_x[k % 4].inc(e.dma_start(
                                    out=xc[:, (k % 4) * D:(k % 4 + 1) * D],
                                    in_=x1s[k * 128:(k + 1) * 128, :]), 16)
                            for ti in range(4):
                                k = 4 * sc + ti
                                e.wait_ge(a_sev.h, k + 1)
                                a_lnw[k % 2].inc(e.dma_start(
                                    out=ln2_b[k * 128:(k + 1) * 128, :],
                                    in_=lnsb2[:, (k % 2) * D:(k % 2 + 1) * D]), 16)
                        e.wait_ge(a_cnw[0].h, 32 * 16)
                        e.wait_ge(a_cnw[1].h, 32 * 16)
                    block.sync(_sy)

                    def _v(e):
                        e.wait_ge(a_c.h, 64)
                        e.memset(krsb[:, DR:128], 0.0)
                        e.memset(krsb[:, 128 + DR:256], 0.0)

                        def stats_chunk(e, sc):
                            for ti in range(4):
                                k = 4 * sc + ti
                                e.wait_ge(a_x[k % 4].h, 16 * (k // 4 + 1))
                                e.tensor_reduce(ss(k, 0), xc[:, (k % 4) * D:(k % 4 + 1) * D],
                                                axis=AX.X, op=ALU.add)
                                a_vst.inc(e.tensor_scalar(
                                    out=ss(k, 1), in0=ss(k, 0),
                                    scalar1=-1.0 / D, scalar2=None, op0=ALU.mult), 1)
                                e.wait_ge(a_sr.h, k + 1)
                                a_vst.inc(e.tensor_tensor(
                                    out=ss(k, 2), in0=ss(k, 1), in1=ss(k, 3), op=ALU.mult), 1)

                        def evac_rope_chunk(e, sc):
                            for ti in range(4):
                                k = 4 * sc + ti
                                e.wait_ge(a_sr2.h, k + 1)
                                if k >= 2:
                                    e.wait_ge(a_cnw[k % 2].h, 32 * (k // 2))
                                kp = (k % 2) * 1024
                                e.tensor_scalar(
                                    out=cnsb[:, (k % 2) * KVL:(k % 2 + 1) * KVL],
                                    in0=psk[:, kp:kp + KVL],
                                    scalar1=ss(k, 2), scalar2=None, op0=ALU.mult)
                                a_kev.inc(e.tensor_copy(
                                    out=krf[:, (k % 2) * DR:(k % 2 + 1) * DR],
                                    in_=psk[:, kp + KVL:kp + KVL + DR]), 1)
                                kbf = (k % 2) * DR
                                kb = (k % 2) * 128
                                cs_ = s_cosf[:, k * DR:(k + 1) * DR]
                                sn_ = s_sinf[:, k * DR:(k + 1) * DR]
                                a_kro.inc(rope6(
                                    e,
                                    krf[:, kbf + 0:kbf + DR:2], krf[:, kbf + 1:kbf + DR:2],
                                    cs_[:, 0:DR:2], cs_[:, 1:DR:2],
                                    sn_[:, 0:DR:2], sn_[:, 1:DR:2],
                                    krsb[:, kb + 0:kb + DR:2], krsb[:, kb + 1:kb + DR:2],
                                    DR // 2), 1)

                        for sc in range(8):
                            stats_chunk(e, sc)
                            if sc >= 2:
                                evac_rope_chunk(e, sc - 2)
                        evac_rope_chunk(e, 6)
                        evac_rope_chunk(e, 7)
                    bvector(_v)

                    def _s(e):
                        def atrans(e, sc):
                            e.wait_ge(a_lnw[0].h, 16 * 2 * (sc + 1))
                            e.wait_ge(a_lnw[1].h, 16 * 2 * (sc + 1))
                            if sc >= 1:
                                e.wait_ge(a_tr.h, 256 * sc)
                            if sc >= 2:
                                e.wait_ge(a_kmm.h, 4 * (sc - 1))
                            for dc in range(16):
                                a_tr.inc(e.dma_start(
                                    out=l2t[:, (sc % 2) * 8192 + dc * 512:(sc % 2) * 8192 + (dc + 1) * 512],
                                    in_=ln2_b[sc * 512:(sc + 1) * 512, dc * 128:(dc + 1) * 128],
                                    transpose=True), 16)

                        def ln_chunk(e, sc):
                            for ti in range(4):
                                k = 4 * sc + ti
                                e.wait_ge(a_x[k % 4].h, 16 * (k // 4 + 1))
                                e.wait_ge(a_vst.h, 2 * k + 1)
                                e.activation(out=junk2[:, :], in_=xc[:, (k % 4) * D:(k % 4 + 1) * D],
                                             func=AF.Square, bias=ss(k, 1), accum_out=ss(k, 4))
                                e.activation(out=ss(k, 5), in_=ss(k, 4), func=AF.Ln,
                                             scale=1.0 / D, bias=s_eps[:, 0:1])
                                a_sr.inc(e.activation(out=ss(k, 3), in_=ss(k, 5),
                                                      func=AF.Exp, scale=-0.5), 1)
                                e.wait_ge(a_vst.h, 2 * k + 2)
                                if k >= 2:
                                    e.wait_ge(a_lnw[k % 2].h, 16 * (k // 2))
                                a_sev.inc(e.activation(
                                    out=lnsb2[:, (k % 2) * D:(k % 2 + 1) * D],
                                    in_=xc[:, (k % 4) * D:(k % 4 + 1) * D],
                                    func=AF.Identity, scale=ss(k, 3), bias=ss(k, 2)), 1)

                        def rms_chunk(e, sc):
                            for ti in range(4):
                                k = 4 * sc + ti
                                e.wait_ge(a_kmm.h, k + 1)
                                kp = (k % 2) * 1024
                                e.activation(out=junk2[:, 0:KVL], in_=psk[:, kp:kp + KVL],
                                             func=AF.Square, accum_out=ss(k, 0))
                                e.activation(out=ss(k, 1), in_=ss(k, 0), func=AF.Ln,
                                             scale=1.0 / KVL, bias=s_eps[:, 1:2])
                                a_sr2.inc(e.activation(out=ss(k, 2), in_=ss(k, 1),
                                                       func=AF.Exp, scale=-0.5), 1)

                        for sc in range(8):
                            ln_chunk(e, sc)
                            if sc >= 2:
                                rms_chunk(e, sc - 2)
                            atrans(e, sc)
                        rms_chunk(e, 6)
                        rms_chunk(e, 7)
                    bscalar(_s)

                    def _t(e):
                        e.wait_ge(a_wk.h, 16)
                        for sc in range(8):
                            e.wait_ge(a_tr.h, 16 * 16 * (sc + 1))
                            for ti in range(4):
                                k = 4 * sc + ti
                                if k >= 2:
                                    e.wait_ge(a_kev.h, k - 1)
                                    e.wait_ge(a_sr2.h, k - 1)
                                kp = (k % 2) * 1024
                                base = (sc % 2) * 8192
                                for dc in range(16):
                                    lh = l2t[:, base + dc * 512 + ti * 128:base + dc * 512 + (ti + 1) * 128]
                                    e.matmul(psk[:, kp:kp + 512], lh,
                                             wkvab[:, dc * (KVL + DR):dc * (KVL + DR) + 512],
                                             start=(dc == 0), stop=(dc == 15))
                                    ins = e.matmul(psk[:, kp + 512:kp + 576], lh,
                                                   wkvab[:, dc * (KVL + DR) + 512:dc * (KVL + DR) + 576],
                                                   start=(dc == 0), stop=(dc == 15))
                                a_kmm.inc(ins, 1)
                    block.tensor(_t)
                    pop_scope()

                cbc = ExitStack()
                qaT = cbc.enter_context(nc.sbuf_tensor("qaT", [128, 6 * TOWN], BF16))

                with ExitStack() as cb:
                    push_scope(cb)
                    sbb = lambda *a, **kw: cb.enter_context(nc.sbuf_tensor(*a, **kw))
                    lgo = sbb("b_lgo", [128, 2 * D], BF16)
                    l2ot = sbb("b_l2ot", [128, 16 * 1024], BF16, is_transpose_dest=True)
                    wqab = sbb("b_wqab", [128, 16 * QL], BF16)
                    sqaT = sbb("b_sqaT", [128, 6 * TOWN], BF16)
                    qnwb = sbb("b_qnwb", [128, 6], F32)
                    sc1 = sbb("b_sc1", [128, 16], F32)
                    psq = cb.enter_context(nc.psum_tensor("b_psq", [128, 2 * 512], F32))
                    pcs = cb.enter_context(nc.psum_tensor("b_pcs", [128, 16], F32))
                    b_tr = mksem(octx, "b_tr")
                    b_gth = [mkdsem("b_gth0"), mkdsem("b_gth1")]
                    b_gw = [mksem(octx, "b_gw0"), mksem(octx, "b_gw1")]
                    b_wq = mkdsem("b_wq")
                    b_mm = mksem(octx, "b_mm")
                    b_ev = mksem(octx, "b_ev")
                    b_cs = mksem(octx, "b_cs")
                    b_sc = mksem(octx, "b_sc")

                    def _sy(e):
                        for j in range(4):
                            b_tr.inc(e.dma_start(out=cnT[:, j * S:(j + 1) * S],
                                                 in_=cn_b[:, j * 128:(j + 1) * 128],
                                                 transpose=True), 16)
                        b_tr.inc(e.dma_start(out=krT[:, :], in_=kr_b[:, :],
                                             transpose=True), 16)
                        for j in range(16):
                            e.wait_ge(b_gth[j % 2].h, 16 * (j // 2 + 1))
                            b_gw[j % 2].inc(e.dma_start(
                                out=lno_b[j * 128:(j + 1) * 128, :],
                                in_=lgo[:, (j % 2) * D:(j % 2 + 1) * D]), 16)
                        for half in range(2):
                            e.wait_ge(b_gw[0].h, 16 * 8)
                            e.wait_ge(b_gw[1].h, 16 * 8)
                            if half == 1:
                                e.wait_ge(b_mm.h, 12)
                            for dc in range(16):
                                b_tr.inc(e.dma_start(
                                    out=l2ot[:, dc * 1024:(dc + 1) * 1024],
                                    in_=lno_b[half * 1024:(half + 1) * 1024,
                                              dc * 128:(dc + 1) * 128],
                                    transpose=True), 16)
                        e.wait_ge(b_tr.h, 16 * 37)
                    block.sync(_sy)

                    def _g(e):
                        b_wq.inc(e.dma_start(
                            out=wqab[:].rearrange("p (dc f) -> p dc f", f=QL),
                            in_=wqa.rearrange("(dc p) f -> p dc f", p=128)), 16)
                        b_wq.inc(e.dma_start(out=qnwb[:, :], in_=qnw[:, :]), 16)
                        for j in range(16):
                            if j >= 2:
                                e.wait_ge(b_gw[j % 2].h, 16 * (j // 2))
                            b_gth[j % 2].inc(e.indirect_dma_start(
                                out=lgo[:, (j % 2) * D:(j % 2 + 1) * D], out_offset=None,
                                in_=ln2_b[:, :],
                                in_offset=IOffs(ap=s_ridx[:, j:j + 1], axis=0)), 16)
                    block.gpsimd(_g)

                    def _t(e):
                        e.wait_ge(b_wq.h, 32)
                        idx = 0
                        for half in range(2):
                            e.wait_ge(b_tr.h, 16 * (5 + 16 * (half + 1)))
                            for m6 in range(6):
                                for ncx in range(2):
                                    if idx >= 2:
                                        e.wait_ge(b_ev.h, 2 * (idx - 1))
                                    for dc in range(16):
                                        ins = e.matmul(
                                            psq[:, (idx % 2) * 512:(idx % 2 + 1) * 512],
                                            wqab[:, dc * QL + m6 * 128:dc * QL + (m6 + 1) * 128],
                                            l2ot[:, dc * 1024 + ncx * 512:dc * 1024 + (ncx + 1) * 512],
                                            start=(dc == 0), stop=(dc == 15))
                                    b_mm.inc(ins, 1)
                                    idx += 1
                        e.wait_ge(b_ev.h, 48)
                        for sub in range(16):
                            for k6 in range(6):
                                ins = e.matmul(pcs[:, sub:sub + 1],
                                               sqaT[:, k6 * TOWN + sub * 128:k6 * TOWN + (sub + 1) * 128],
                                               s_1b[:, 0:1],
                                               start=(k6 == 0), stop=(k6 == 5))
                        b_cs.inc(ins, 1)
                    block.tensor(_t)

                    def _s(e):
                        idx = 0
                        for half in range(2):
                            for m6 in range(6):
                                for ncx in range(2):
                                    cidx = 2 * half + ncx
                                    e.wait_ge(b_mm.h, idx + 1)
                                    e.activation(
                                        out=qaT[:, m6 * TOWN + cidx * 512:m6 * TOWN + (cidx + 1) * 512],
                                        in_=psq[:, (idx % 2) * 512:(idx % 2 + 1) * 512],
                                        func=AF.Copy)
                                    b_ev.inc(e.activation(
                                        out=sqaT[:, m6 * TOWN + cidx * 512:m6 * TOWN + (cidx + 1) * 512],
                                        in_=psq[:, (idx % 2) * 512:(idx % 2 + 1) * 512],
                                        func=AF.Square), 2)
                                    idx += 1
                        e.wait_ge(b_cs.h, 1)
                        e.activation(out=sc1[:, :], in_=pcs[:, :], func=AF.Ln,
                                     scale=1.0 / QL, bias=s_eps[:, 1:2])
                        b_sc.inc(e.activation(out=scol[:, :], in_=sc1[:, :],
                                              func=AF.Exp, scale=-0.5, bias=s_eps[:, 2:3]), 1)
                    bscalar(_s)

                    def _v(e):
                        e.wait_ge(b_cs.h, 1)
                        e.wait_ge(b_wq.h, 32)
                        for k6 in range(6):
                            ins = e.tensor_scalar(out=qaT[:, k6 * TOWN:(k6 + 1) * TOWN],
                                                  in0=qaT[:, k6 * TOWN:(k6 + 1) * TOWN],
                                                  scalar1=qnwb[:, k6:k6 + 1], scalar2=None,
                                                  op0=ALU.mult)
                        b_fold.inc(ins, 1)
                        e.memset(krT[64:65, :], 1.0)
                    bvector(_v)
                    pop_scope()

                with ExitStack() as cg:
                    push_scope(cg)
                    sbg = lambda *a, **kw: cg.enter_context(nc.sbuf_tensor(*a, **kw))
                    wkvbb = sbg("g_wkvbb", [128, 4 * NH * 256], BF16)
                    kvnwb = sbg("g_kvnwb", [128, 4], F32)
                    wqbg = sbg("g_wqbg", [128, 2 * 6 * 384], BF16)
                    qg = sbg("g_qg", [128, 16 * 384], BF16)
                    qrc = sbg("g_qrc", [128, 16 * 256], BF16)
                    qs2 = sbg("g_qs2", [128, 32], F32)
                    sqq = sbg("g_sqq", [128, 32], F32)
                    cn_ = sbg("g_cn", [128, 32], F32)
                    junk3 = sbg("g_junk", [128, 512], BF16)
                    kst = sbg("g_kst", [128, 8], F32)
                    vjunk = sbg("g_vjunk", [128, 192], BF16)
                    qnT = sbg("g_qnT", [128, 2 * TOWN], BF16, is_transpose_dest=True)
                    qrT = sbg("g_qrT", [128, 2 * TOWN], BF16, is_transpose_dest=True)
                    knT = sbg("g_knT", [128, S], BF16)
                    Vh = sbg("g_Vh", [128, 32 * 132], BF16)
                    PT = sbg("g_PT", [128, 2 * 1024], BF16)
                    og = sbg("g_og", [128, 2 * 16 * 128], BF16)
                    kq = sbg("g_kq", [128, 1], F32)
                    kr64 = sbg("g_kr64", [128, 1], F32)
                    krs = sbg("g_krs", [1, 1], F32)
                    tta = sbg("g_tta", [1, 1], F32)
                    rks = sbg("g_rks", [1, 8], F32)
                    rkc = sbg("g_rkc", [128, 1], F32)
                    rcp = sbg("g_rcp", [128, 2], F32)
                    PS = cg.enter_context(nc.psum_tensor("g_ps", [128, 4096], F32))
                    def vA(s2):
                        return PS[:, 2048 + s2 * 512:2048 + s2 * 512 + 129]

                    def vQ(sub):
                        return PS[:, 3072 + (sub % 2) * 512:3072 + (sub % 2) * 512 + 384]

                    psn_s = [PS[:, 0:512], PS[:, 1024:1536]]
                    vVs = [PS[:, 512:640], PS[:, 1536:1664]]
                    ptiny = PS[:, 2304:2320]
                    g_wk = mkdsem("g_wk")
                    g_wkf = mksem(octx, "g_wkf")
                    g_wq = [mkdsem("g_wq0"), mkdsem("g_wq1")]
                    g_qmm = mksem(octx, "g_qmm")
                    g_qev = mksem(octx, "g_qev")
                    g_qs = mksem(octx, "g_qs")
                    g_sq = mksem(octx, "g_sq")
                    g_rope = mksem(octx, "g_rope")
                    g_cw = mksem(octx, "g_cw")
                    g_qw = mksem(octx, "g_qw")
                    g_qtr = mksem(octx, "g_qtr")
                    g_kmm = mksem(octx, "g_kmm")
                    g_kev = mksem(octx, "g_kev")
                    g_vev = mksem(octx, "g_vev")
                    g_bc = mksem(octx, "g_bc")
                    g_rk = mksem(octx, "g_rk")
                    g_rkb = mksem(octx, "g_rkb")
                    g_smm = mksem(octx, "g_smm")
                    g_sex = mksem(octx, "g_sex")
                    g_msk = mksem(octx, "g_msk")
                    g_amm = mksem(octx, "g_amm")
                    g_rc = mksem(octx, "g_rc")
                    g_oev = mksem(octx, "g_oev")
                    g_ow = mksem(octx, "g_ow")
                    g_kr0 = mksem(octx, "g_kr0")
                    g_kr1 = mksem(octx, "g_kr1")

                    P_B, P_MSK, P_HLAST = {}, {}, {}
                    P_kmmK, P_kmmV = {}, {}
                    n_b = n_mskb = n_kmm = 0
                    for grp in range(NG):
                        for h2 in range(2):
                            for ncx in range(8):
                                n_kmm += 1
                                P_kmmK[(grp, h2, ncx)] = n_kmm
                            for vs in range(32):
                                n_kmm += 1
                                P_kmmV[(grp, h2, vs)] = n_kmm
                            for g8 in range(8):
                                for bj in range(g8 + 1):
                                    n_b += 1
                                    P_B[(grp, h2, g8, bj)] = n_b
                                    if bj == g8:
                                        n_mskb += 1
                                        P_MSK[(grp, h2, g8)] = n_mskb
                            P_HLAST[(grp, h2)] = P_B[(grp, h2, 7, 7)]

                    def P_kev(grp, h2, j):
                        return 9 * (2 * grp + h2) + j + 1

                    def P_vev(grp, h2, vs):
                        return 32 * (2 * grp + h2) + vs + 1

                    def P_oev(grp, h2, g8, s2):
                        return 16 * (2 * grp + h2) + 2 * g8 + s2 + 1

                    def P_rc(grp, h2, g8):
                        return 8 * (2 * grp + h2) + g8 + 1

                    def _g(e):
                        g_wk.inc(e.dma_start(out=kvnwb[:, :], in_=kvnw[:, :]), 16)
                        g_wk.inc(e.dma_start(
                            out=wkvbb[:].rearrange("p (kc f) -> p kc f", f=NH * 256),
                            in_=wkvb.rearrange("(kc p) f -> p kc f", p=128)), 16)
                        wqbr = wqb.rearrange("(k p) f -> p k f", p=128)
                        for grp in range(NG):
                            if grp >= 2:
                                e.wait_ge(g_qmm.h, 16 * (grp - 1))
                            g_wq[grp % 2].inc(e.dma_start(
                                out=wqbg[:, (grp % 2) * 2304:(grp % 2 + 1) * 2304]
                                    .rearrange("p (k f) -> p k f", f=384),
                                in_=wqbr[:, :, grp * 384:(grp + 1) * 384]), 16)
                    block.gpsimd(_g)

                    def _vw(e):
                        e.wait_ge(g_wk.h, 32)
                        for kc in range(4):
                            ins = e.tensor_scalar(
                                out=wkvbb[:, kc * NH * 256:(kc + 1) * NH * 256],
                                in0=wkvbb[:, kc * NH * 256:(kc + 1) * NH * 256],
                                scalar1=kvnwb[:, kc:kc + 1], scalar2=None, op0=ALU.mult)
                        g_wkf.inc(ins, 1)
                    bvector(_vw)

                    def _t(e):
                        e.wait_ge(g_wkf.h, 1)
                        e.wait_ge(g_kr0.h, 1)
                        ins = e.matmul(ptiny[0:1, 3:4], kr64[0:64, 0:1], s_1f[0:64, 0:1],
                                       start=True, stop=True)
                        g_kr1.inc(ins, 1)

                        def emit_av(e, key):
                            grp_, h2_, g8_, bj_ = key
                            b = P_B[key]
                            ntt = 4 * (g8_ + 1)
                            if bj_ == g8_:
                                e.wait_ge(g_msk.h, P_MSK[(grp_, h2_, g8_)])
                            else:
                                e.wait_ge(g_sex.h, b)
                            if bj_ == 0 and (grp_, h2_, g8_) != (0, 0, 0):
                                pg8, ph2, pgrp = g8_ - 1, h2_, grp_
                                if pg8 < 0:
                                    pg8 = 7
                                    ph2 -= 1
                                    if ph2 < 0:
                                        ph2, pgrp = 1, pgrp - 1
                                e.wait_ge(g_oev.h, P_oev(pgrp, ph2, pg8, 1))
                            sb = (b % 2) * 1024
                            for p_ in range(2):
                                for k_ in range(2):
                                    tt = 4 * bj_ + 2 * p_ + k_
                                    for s2 in range(2):
                                        ins = e.matmul(
                                            vA(s2),
                                            PT[:, sb + p_ * 512 + k_ * 256 + s2 * 128:
                                               sb + p_ * 512 + k_ * 256 + (s2 + 1) * 128],
                                            Vh[:, tt * 132:tt * 132 + 129],
                                            start=(tt == 0), stop=(tt == ntt - 1))
                            g_amm.inc(ins, 1)

                        for grp in range(NG):
                            e.wait_ge(g_wq[grp % 2].h, 16 * (grp // 2 + 1))
                            e.wait_ge(b_fold.h, 1)
                            for sub in range(16):
                                gi = 16 * grp + sub
                                if gi >= 2:
                                    e.wait_ge(g_qev.h, gi - 1)
                                for k6 in range(6):
                                    ins = e.matmul(
                                        vQ(sub),
                                        qaT[:, k6 * TOWN + sub * 128:k6 * TOWN + (sub + 1) * 128],
                                        wqbg[:, (grp % 2) * 2304 + k6 * 384:(grp % 2) * 2304 + (k6 + 1) * 384],
                                        start=(k6 == 0), stop=(k6 == 5))
                                g_qmm.inc(ins, 1)
                            for h2 in range(2):
                                h = 2 * grp + h2
                                if h > 0:
                                    e.wait_ge(g_sex.h, P_HLAST[(grp, h2 - 1) if h2 else (grp - 1, 1)])
                                for ncx in range(8):
                                    e.wait_ge(g_kev.h, 9 * h + max(0, ncx - 1))
                                    for kc in range(4):
                                        ins = e.matmul(
                                            psn_s[ncx % 2],
                                            wkvbb[:, kc * NH * 256 + h * 256:kc * NH * 256 + h * 256 + 128],
                                            cnT[:, kc * S + ncx * 512:kc * S + (ncx + 1) * 512],
                                            start=(kc == 0), stop=(kc == 3))
                                    g_kmm.inc(ins, 1)
                                for vs in range(32):
                                    if vs >= 2:
                                        e.wait_ge(g_vev.h, P_vev(grp, h2, vs - 2))
                                    for kc in range(4):
                                        ins = e.matmul(
                                            vVs[vs % 2],
                                            cnT[:, kc * S + vs * 128:kc * S + (vs + 1) * 128],
                                            wkvbb[:, kc * NH * 256 + h * 256 + 128:kc * NH * 256 + (h + 1) * 256],
                                            start=(kc == 0), stop=(kc == 3))
                                    g_kmm.inc(ins, 1)
                                if h2 == 0:
                                    e.wait_ge(g_kev.h, P_kev(grp, 0, 8))
                                    if grp >= 1:
                                        e.wait_ge(g_rkb.h, 2 * grp - 1)
                                        e.wait_ge(g_oev.h, P_oev(grp - 1, 1, 7, 1))
                                    ins = e.matmul(ptiny[0:1, 0:1], kq[:, 0:1], s_1f[:, 0:1],
                                                   start=True, stop=True)
                                    g_bc.inc(ins, 1)
                                    e.wait_ge(g_rk.h, grp + 1)
                                    ins = e.matmul(ptiny[:, 2:3], s_1f[0:1, :], rks[0:1, grp:grp + 1],
                                                   start=True, stop=True)
                                    g_bc.inc(ins, 1)
                                    e.wait_ge(g_qtr.h, 16 * 4 * (grp + 1))
                                e.wait_ge(g_kev.h, 9 * (2 * grp + h2) + 8)
                                e.wait_ge(g_vev.h, 32 * (2 * grp + h2) + 32)
                                pend = None
                                for g8 in range(8):
                                    qsl = qnT[:, h2 * TOWN + g8 * 256:h2 * TOWN + (g8 + 1) * 256]
                                    qrl = qrT[0:65, h2 * TOWN + g8 * 256:h2 * TOWN + (g8 + 1) * 256]
                                    for bj in range(g8 + 1):
                                        b = P_B[(grp, h2, g8, bj)]
                                        if b >= 3:
                                            e.wait_ge(g_sex.h, b - 2)
                                        sb = (b % 2) * 1024
                                        for p_ in range(2):
                                            for k_ in range(2):
                                                tt = 4 * bj + 2 * p_ + k_
                                                sl = sb + p_ * 512 + k_ * 256
                                                e.matmul(PS[:, sl:sl + 256],
                                                         knT[:, tt * 128:(tt + 1) * 128],
                                                         qsl, start=True, stop=False)
                                                ins = e.matmul(PS[:, sl:sl + 256],
                                                               krT[0:65, tt * 128:(tt + 1) * 128],
                                                               qrl, start=False, stop=True)
                                        g_smm.inc(ins, 1)
                                        if pend is not None:
                                            emit_av(e, pend)
                                        pend = (grp, h2, g8, bj)
                                emit_av(e, pend)
                    block.tensor(_t)

                    def _s(e):
                        e.wait_ge(b_fold.h, 1)
                        for ncx in range(8):
                            e.activation(out=junk3[0:64, 0:512],
                                         in_=krT[0:64, ncx * 512:(ncx + 1) * 512],
                                         func=AF.Square, accum_out=kst[0:64, ncx:ncx + 1])
                        g_kr0.inc(e.activation(out=junk3[0:64, 0:8], in_=kst[0:64, 0:8],
                                               func=AF.Identity, accum_out=kr64[0:64, 0:1]), 1)

                        def kv_evacs(e, grp, h2):
                            for ncx in range(8):
                                e.wait_ge(g_kmm.h, P_kmmK[(grp, h2, ncx)])
                                g_kev.inc(e.activation(
                                    out=knT[:, ncx * 512:(ncx + 1) * 512],
                                    in_=psn_s[ncx % 2], func=AF.Copy), 1)
                            for ncx in range(8):
                                e.activation(out=junk3[:, 0:512],
                                             in_=knT[:, ncx * 512:(ncx + 1) * 512],
                                             func=AF.Square, accum_out=kst[:, ncx:ncx + 1])
                            g_kev.inc(e.activation(out=junk3[:, 0:8], in_=kst[:, 0:8],
                                                   func=AF.Identity, accum_out=kq[:, 0:1]), 1)

                        def exps_oev(e, grp, h2):
                            if grp >= 1:
                                e.wait_ge(g_ow.h, 32 * grp)
                            for g8 in range(8):
                                for bj in range(g8 + 1):
                                    b = P_B[(grp, h2, g8, bj)]
                                    e.wait_ge(g_smm.h, b)
                                    if b >= 3:
                                        e.wait_ge(g_amm.h, b - 2)
                                    ins = e.raw.activation(
                                        out=PT[:, (b % 2) * 1024:(b % 2 + 1) * 1024],
                                        in_=PS[:, (b % 2) * 1024:(b % 2 + 1) * 1024],
                                        func=AF.Exp)
                                    ins.then_inc(g_sex.h, 1)
                                    g_sex.n += 1
                                e.wait_ge(g_amm.h, P_B[(grp, h2, g8, g8)])
                                e.wait_ge(g_rc.h, P_rc(grp, h2, g8))
                                for s2 in range(2):
                                    g_oev.inc(e.activation(
                                        out=og[:, (h2 * 16 + g8 * 2 + s2) * 128:(h2 * 16 + g8 * 2 + s2 + 1) * 128],
                                        in_=vA(s2)[:, 0:128],
                                        func=AF.Identity, scale=rcp[:, s2:s2 + 1]), 1)

                        for grp in range(NG):
                            for sub in range(16):
                                gi = 16 * grp + sub
                                e.wait_ge(g_qmm.h, gi + 1)
                                if grp >= 1 and sub == 0:
                                    e.wait_ge(g_qw.h, 64 * grp)
                                g_qev.inc(e.activation(
                                    out=qg[:, sub * 384:(sub + 1) * 384],
                                    in_=vQ(sub),
                                    func=AF.Identity, scale=scol[:, sub:sub + 1]), 1)
                            kv_evacs(e, grp, 0)
                            e.wait_ge(g_qs.h, grp + 1)
                            g_sq.inc(e.activation(out=sqq[:, :], in_=qs2[:, :], func=AF.Sqrt), 1)
                            e.wait_ge(g_rkb.h, 2 * grp + 1)
                            g_rk.inc(e.activation(out=rks[0:1, grp:grp + 1], in_=tta[0:1, 0:1],
                                                  func=AF.Sqrt, scale=1.0 / S), 1)
                            exps_oev(e, grp, 0)
                            kv_evacs(e, grp, 1)
                            exps_oev(e, grp, 1)
                    bscalar(_s)

                    def _v(e):
                        e.wait_ge(a_c.h, 64)
                        e.wait_ge(c_ld.h, 16 * 6)
                        e.memset(Vh[:, :].rearrange("p (v c) -> p v c", c=132)[:, :, 128:132], 1.0)
                        e.memset(qrc[:, :].rearrange("p (s f) -> p s f", f=128)[:, :, 65:128], 0.0)
                        e.wait_ge(g_kr1.h, 1)
                        e.tensor_copy(out=krs[0:1, 0:1], in_=ptiny[0:1, 3:4])

                        def v_evacs(e, grp, h2):
                            for vs in range(32):
                                e.wait_ge(g_kmm.h, P_kmmV[(grp, h2, vs)])
                                g_vev.inc(e.tensor_copy(
                                    out=Vh[:, vs * 132:vs * 132 + 128],
                                    in_=vVs[vs % 2]), 1)

                        def masks_recips(e, grp, h2):
                            for g8 in range(8):
                                bd = P_B[(grp, h2, g8, g8)]
                                e.wait_ge(g_sex.h, bd)
                                ins = e.raw.tensor_tensor(
                                    out=PT[:, (bd % 2) * 1024:(bd % 2 + 1) * 1024],
                                    in0=PT[:, (bd % 2) * 1024:(bd % 2 + 1) * 1024],
                                    in1=s_mask[:, 0:1024],
                                    op=ALU.mult)
                                ins.then_inc(g_msk.h, 1)
                                g_msk.n += 1
                                e.wait_ge(g_amm.h, bd)
                                e.reciprocal(out=rcp[:, 0:1], in_=vA(0)[:, 128:129])
                                g_rc.inc(e.reciprocal(out=rcp[:, 1:2], in_=vA(1)[:, 128:129]), 1)

                        qv = qg[:, :].rearrange("p (s f) -> p s f", f=384)
                        qrv = qrc[:, :].rearrange("p (s f) -> p s f", f=256)
                        co = s_coso[:, :].rearrange("p (s f) -> p s f", f=DR)
                        si_ = s_sino[:, :].rearrange("p (s f) -> p s f", f=DR)
                        for grp in range(NG):
                            for sub in range(16):
                                e.wait_ge(g_qev.h, 16 * grp + sub + 1)
                                for h2 in range(2):
                                    e.tensor_tensor(
                                        out=vjunk[:, 0:192],
                                        in0=qg[:, sub * 384 + h2 * 192:sub * 384 + (h2 + 1) * 192],
                                        in1=qg[:, sub * 384 + h2 * 192:sub * 384 + (h2 + 1) * 192],
                                        op=ALU.mult)
                                    ins = e.tensor_reduce(
                                        qs2[:, h2 * 16 + sub:h2 * 16 + sub + 1],
                                        vjunk[:, 0:192], axis=AX.X, op=ALU.add)
                            g_qs.inc(ins, 1)
                            if grp >= 1:
                                e.wait_ge(g_qw.h, 64 * grp)
                            for h2 in range(2):
                                g_rope.inc(rope6(
                                    e,
                                    qv[:, :, h2 * 192 + 128 + 0:h2 * 192 + 192:2],
                                    qv[:, :, h2 * 192 + 128 + 1:h2 * 192 + 192:2],
                                    co[:, :, 0:DR:2], co[:, :, 1:DR:2],
                                    si_[:, :, 0:DR:2], si_[:, :, 1:DR:2],
                                    qrv[:, :, h2 * 128 + 0:h2 * 128 + 64:2],
                                    qrv[:, :, h2 * 128 + 1:h2 * 128 + 64:2],
                                    512), 1)
                            v_evacs(e, grp, 0)
                            e.wait_ge(g_bc.h, 2 * grp + 1)
                            e.tensor_copy(out=tta[0:1, 0:1], in_=ptiny[0:1, 0:1])
                            g_rkb.inc(e.tensor_tensor(out=tta[0:1, 0:1], in0=tta[0:1, 0:1],
                                                      in1=krs[0:1, 0:1], op=ALU.add), 1)
                            e.wait_ge(g_bc.h, 2 * grp + 2)
                            g_rkb.inc(e.tensor_scalar(out=rkc[:, 0:1], in0=ptiny[:, 2:3],
                                                      scalar1=-C_COEF, scalar2=None,
                                                      op0=ALU.mult), 1)
                            e.wait_ge(g_sq.h, grp + 1)
                            e.tensor_scalar(out=cn_[:, :], in0=sqq[:, :],
                                            scalar1=rkc[:, 0:1], scalar2=-80.0,
                                            op0=ALU.mult, op1=ALU.max)
                            for h2 in range(2):
                                g_cw.inc(e.tensor_copy(
                                    out=qrv[:, :, h2 * 128 + 64:h2 * 128 + 65],
                                    in_=cn_[:, :].rearrange("p (h s) -> p s h", h=2)[:, :, h2:h2 + 1]), 1)
                            masks_recips(e, grp, 0)
                            v_evacs(e, grp, 1)
                            masks_recips(e, grp, 1)
                    bvector(_v)

                    def _sy(e):
                        for grp in range(NG):
                            e.wait_ge(g_qev.h, 16 * (grp + 1))
                            e.wait_ge(g_rope.h, 2 * (grp + 1))
                            e.wait_ge(g_cw.h, 2 * (grp + 1))
                            if grp >= 1:
                                e.wait_ge(g_amm.h, P_HLAST[(grp - 1, 1)])
                            qv = qg[:, :].rearrange("p (s f) -> p s f", f=384)
                            qrv = qrc[:, :].rearrange("p (s f) -> p s f", f=256)
                            for h2 in range(2):
                                g_qw.inc(e.dma_start(
                                    out=qn_b[h2, :, :].rearrange("(s p) f -> p s f", p=128),
                                    in_=qv[:, :, h2 * 192:h2 * 192 + 128]), 16)
                                g_qw.inc(e.dma_start(
                                    out=qr_b[h2, :, :].rearrange("(s p) f -> p s f", p=128),
                                    in_=qrv[:, :, h2 * 128:(h2 + 1) * 128]), 16)
                            e.wait_ge(g_qw.h, 64 * (grp + 1))
                            for h2 in range(2):
                                g_qtr.inc(e.dma_start(
                                    out=qnT[:, h2 * TOWN:(h2 + 1) * TOWN],
                                    in_=qn_b[h2, :, :], transpose=True), 16)
                                g_qtr.inc(e.dma_start(
                                    out=qrT[:, h2 * TOWN:(h2 + 1) * TOWN],
                                    in_=qr_b[h2, :, :], transpose=True), 16)
                            for h2 in range(2):
                                e.wait_ge(g_oev.h, P_oev(grp, h2, 7, 1))
                                g_ow.inc(e.dma_start(
                                    out=o_b[2 * grp + h2, :, :].rearrange("(s p) f -> p s f", p=128),
                                    in_=og[:, :].rearrange("p (hs f) -> p hs f", f=128)[:, h2 * 16:(h2 + 1) * 16, :]), 16)
                        e.wait_ge(g_ow.h, 16 * 2 * NG)
                    block.sync(_sy)
                    pop_scope()

                cbc.close()

                with ExitStack() as cd:
                    push_scope(cd)
                    sbd = lambda *a, **kw: cd.enter_context(nc.sbuf_tensor(*a, **kw))
                    oT = sbd("d_oT", [128, 16 * TOWN], BF16, is_transpose_dest=True)
                    wob = sbd("d_wob", [128, 16 * 1024], BF16)
                    xg = sbd("d_xg", [128, 2 * D], F32)
                    x2t = sbd("d_x2t", [128, 2 * 1024], F32)
                    pso = cd.enter_context(nc.psum_tensor("d_pso", [128, 2 * 1024], F32))
                    d_tr = mksem(octx, "d_tr")
                    d_wo = mkdsem("d_wo")
                    d_gx = [mkdsem("d_gx0"), mkdsem("d_gx1")]
                    d_mm = mksem(octx, "d_mm")
                    d_ad = mksem(octx, "d_ad")
                    d_xw = [mksem(octx, "d_xw0"), mksem(octx, "d_xw1")]

                    def _sy(e):
                        for hd in range(16):
                            d_tr.inc(e.dma_start(
                                out=oT[:, hd * TOWN:(hd + 1) * TOWN],
                                in_=o_b[hd, :, :], transpose=True), 16)
                        for dh in range(2):
                            for sub in range(16):
                                k = 16 * dh + sub
                                e.wait_ge(d_ad.h, k + 1)
                                d_xw[k % 2].inc(e.dma_start(
                                    out=x2s[sub * 128:(sub + 1) * 128, dh * 1024:(dh + 1) * 1024],
                                    in_=x2t[:, (k % 2) * 1024:(k % 2 + 1) * 1024]), 16)
                        e.wait_ge(d_xw[0].h, 16 * 16)
                        e.wait_ge(d_xw[1].h, 16 * 16)
                    block.sync(_sy)

                    def _g(e):
                        wor = wo.rearrange("(hd p) d -> p hd d", p=128)
                        for dh in range(2):
                            if dh == 1:
                                e.wait_ge(d_mm.h, 16)
                            d_wo.inc(e.dma_start(
                                out=wob[:].rearrange("p (hd f) -> p hd f", f=1024),
                                in_=wor[:, :, dh * 1024:(dh + 1) * 1024]), 16)
                            for sub in range(16):
                                k = 16 * dh + sub
                                if k >= 2:
                                    e.wait_ge(d_ad.h, k - 1)
                                d_gx[k % 2].inc(e.indirect_dma_start(
                                    out=xg[:, (k % 2) * D:(k % 2 + 1) * D], out_offset=None,
                                    in_=x1s[:, :],
                                    in_offset=IOffs(ap=s_ridx[:, sub:sub + 1], axis=0)), 16)
                    block.gpsimd(_g)

                    def _t(e):
                        e.wait_ge(d_tr.h, 16 * 16)
                        for dh in range(2):
                            e.wait_ge(d_wo.h, 16 * (dh + 1))
                            for sub in range(16):
                                k = 16 * dh + sub
                                if k >= 2:
                                    e.wait_ge(d_ad.h, k - 1)
                                for hd in range(16):
                                    for ncx in range(2):
                                        ins = e.matmul(
                                            pso[:, (k % 2) * 1024 + ncx * 512:(k % 2) * 1024 + (ncx + 1) * 512],
                                            oT[:, hd * TOWN + sub * 128:hd * TOWN + (sub + 1) * 128],
                                            wob[:, hd * 1024 + ncx * 512:hd * 1024 + (ncx + 1) * 512],
                                            start=(hd == 0), stop=(hd == 15))
                                d_mm.inc(ins, 1)
                    block.tensor(_t)

                    def _v(e):
                        for dh in range(2):
                            for sub in range(16):
                                k = 16 * dh + sub
                                e.wait_ge(d_mm.h, k + 1)
                                e.wait_ge(d_gx[k % 2].h, 16 * (k // 2 + 1))
                                if k >= 2:
                                    e.wait_ge(d_xw[k % 2].h, 16 * (k // 2))
                                d_ad.inc(e.tensor_tensor(
                                    out=x2t[:, (k % 2) * 1024:(k % 2 + 1) * 1024],
                                    in0=pso[:, (k % 2) * 1024:(k % 2 + 1) * 1024],
                                    in1=xg[:, (k % 2) * D + dh * 1024:(k % 2) * D + (dh + 1) * 1024],
                                    op=ALU.add), 1)
                    bvector(_v)
                    pop_scope()

        if 3 in stages:
            emit_ffn(x2s, out_t, TOWN)

    return nc


def _host_consts():
    inv = 1.0 / (10000.0 ** (np.arange(0, DR, 2, dtype=np.float32) / DR))
    pos = np.arange(S, dtype=np.float32)
    freqs = np.outer(pos, inv).astype(np.float32)
    emb = np.concatenate([freqs, freqs], axis=-1)
    cosf = np.cos(emb).astype(np.float32)
    sinf = np.sin(emb).astype(np.float32)
    H64 = _hadamard(64)
    H128 = _hadamard(128)
    HA = np.kron(np.eye(2), H64)
    hc = np.zeros((128, 256), np.float32)
    hc[:, 0:128] = HA
    hc[:, 128:256] = H128
    hconst = hc.astype(ml_dtypes.bfloat16)
    onesf = np.ones((128, 128), np.float32)
    onesb = np.ones((128, 1), np.float32).astype(ml_dtypes.bfloat16)
    return cosf, sinf, hconst, onesf, onesb


def make_in_maps(inputs):
    x = np.ascontiguousarray(np.asarray(inputs["L0_attn_x"], dtype=np.float32))
    W1 = np.asarray(inputs["L0_ffn_W1"], dtype=np.float32)
    W2 = np.asarray(inputs["L0_ffn_W2"], dtype=np.float32)
    W3 = np.asarray(inputs["L0_ffn_W3"], dtype=np.float32)

    w1tg = np.ascontiguousarray(W1.reshape(128, 64, 16, 128).transpose(1, 2, 3, 0))
    w3tg = np.ascontiguousarray(W3.reshape(128, 64, 16, 128).transpose(1, 2, 3, 0))
    w2t = np.ascontiguousarray(W2.T)

    cosf, sinf, hconst, onesf, onesb = _host_consts()

    base = {
        "w1tg": w1tg, "w3tg": w3tg, "w2t": w2t,
        "wqa": np.ascontiguousarray(inputs["L0_attn_w_qa"], dtype=np.float32),
        "wqb": np.ascontiguousarray(inputs["L0_attn_w_qb"], dtype=np.float32),
        "wkva": np.ascontiguousarray(inputs["L0_attn_w_kva"], dtype=np.float32),
        "wkvb": np.ascontiguousarray(inputs["L0_attn_w_kvb"], dtype=np.float32),
        "wo": np.ascontiguousarray(inputs["L0_attn_w_o"], dtype=np.float32),
        "qnw": np.ascontiguousarray(np.asarray(inputs["L0_attn_q_a_norm_w"], np.float32).reshape(6, 128).T),
        "kvnw": np.ascontiguousarray(np.asarray(inputs["L0_attn_kv_a_norm_w"], np.float32).reshape(4, 128).T),
        "cos_f": cosf, "sin_f": sinf,
        "hconst": hconst, "ones_f": onesf, "ones_b": onesb,
        "epsc": np.broadcast_to(
            np.array([LNEPS, RMSEPS, QSC_BIAS, 0.0], np.float32), (128, 4)).copy(),
    }

    in_maps = []
    own_rows_all = []
    for c in range(8):
        b, half = c // 2, c % 2
        own = np.concatenate([np.arange(g * 512 + half * 256, g * 512 + half * 256 + 256)
                              for g in range(8)]).astype(np.int64)
        own_rows_all.append(own)
        rix = own.reshape(16, 128).T.astype(np.uint32)
        mask = np.zeros((128, 4, 256), np.float32)
        ti = np.arange(128)[:, None]
        qi = np.arange(256)[None, :]
        for o in range(4):
            mask[:, o, :] = (o * 128 + ti - half * 256 <= qi).astype(np.float32)
        im = dict(base)
        im["x"] = np.ascontiguousarray(x[b])
        im["cos_o"] = np.ascontiguousarray(cosf[own])
        im["sin_o"] = np.ascontiguousarray(sinf[own])
        im["maskc"] = np.ascontiguousarray(mask.reshape(128, 4 * 256).astype(ml_dtypes.bfloat16))
        im["ridx"] = np.ascontiguousarray(rix)
        in_maps.append(im)
    return in_maps, own_rows_all


def kernel(**inputs):
    in_maps, own_rows_all = make_in_maps(inputs)
    nc = build_nc()
    res = run_bass_kernel_spmd(nc, in_maps, core_ids=list(range(8)))
    out = np.empty((BATCH, S, D), np.float32)
    for c in range(8):
        b = c // 2
        out[b, own_rows_all[c]] = np.asarray(res.results[c]["out"], np.float32)
    return out
